# version 03
# baseline (speedup 1.0000x reference)
import math
import numpy as np
from contextlib import ExitStack

import concourse.bass as bass
import concourse.mybir as mybir
from concourse.bass_utils import run_bass_kernel_spmd

F32 = mybir.dt.float32
BF16 = mybir.dt.bfloat16
AF = mybir.ActivationFunctionType
ALU = mybir.AluOpType
AX = mybir.AxisListType

B, S, D = 8, 2048, 1024
NCHUNK = D // 128
TB = 512
NTB = S // TB
NEG = -60.0

ENGINES = ("pe", "act", "dve", "pool", "sync")


class _Op:
    __slots__ = ("eng", "emit", "deps", "opid", "inc", "ord", "dma", "stream", "snum", "tag")

    def __init__(self, eng, emit, deps, opid, dma=False, stream=None, snum=0, tag=""):
        self.eng, self.emit, self.deps, self.opid = eng, emit, deps, opid
        self.dma, self.stream, self.snum, self.tag = dma, stream, snum, tag
        self.inc = False
        self.ord = 0


class Prog:

    def __init__(self, nc):
        self.nc = nc
        self.ops = []
        self.writer = {}
        self.readers = {}
        self.fence_last = {}
        self.last_on = {}
        self.streams = {}

    def _dep_filter(self, opid, deps):
        me = self.ops[opid]
        out = set()
        for d in deps:
            if d is None or d == opid:
                continue
            o = self.ops[d]
            if (not o.dma) and o.eng == me.eng and not me.dma:
                continue
            if (not o.dma) and me.dma and o.eng == me.eng:
                continue
            out.add(d)
        return out

    def _bounce(self, eng, dep):
        cands = [e for e in ("sync", "pool", "act") if e != eng] or ["sync"]
        beng = cands[len(self.ops) % len(cands)]
        bid = len(self.ops)
        b = _Op(beng, lambda e: e.nop(), {dep}, bid, tag="bounce")
        self.ops.append(b)
        self.last_on[beng] = bid
        return bid

    def op(self, eng, emit, reads=(), writes=(), sreads=(), tag=""):
        reads = tuple(reads) + tuple(sreads)
        deps = set()
        for r in reads:
            deps.add(self.writer.get(r))
        for r in writes:
            deps.add(self.writer.get(r))
            deps.update(self.readers.get(r, ()))
        deps.update(self.fence_last.values())
        deps.discard(None)
        same = [d for d in deps if (not self.ops[d].dma) and self.ops[d].eng == eng]
        if same:
            deps.difference_update(same)
            deps.add(self._bounce(eng, max(same)))
        opid = len(self.ops)
        o = _Op(eng, emit, set(), opid, tag=tag)
        self.ops.append(o)
        o.deps = deps
        for r in reads:
            self.readers.setdefault(r, []).append(opid)
        for r in writes:
            self.writer[r] = opid
            self.readers[r] = []
        self.last_on[eng] = opid
        return opid

    def dma(self, eng, out, in_, reads=(), writes=(), stream="d", tag="", **kw):
        opid = len(self.ops)
        n = self.streams.get(stream, 0)
        self.streams[stream] = n + 1

        def emit(e, _out=out, _in=in_, _kw=kw):
            return e.dma_start(out=_out, in_=_in, **_kw)

        o = _Op(eng, emit, set(), opid, dma=True, stream=stream, snum=n + 1, tag=tag)
        self.ops.append(o)
        deps = set()
        for r in reads:
            deps.add(self.writer.get(r))
        for r in writes:
            deps.add(self.writer.get(r))
            deps.update(self.readers.get(r, ()))
        deps.update(self.fence_last.values())
        o.deps = self._dep_filter(opid, deps)
        for r in reads:
            self.readers.setdefault(r, []).append(opid)
        for r in writes:
            self.writer[r] = opid
            self.readers[r] = []
        self.last_on[eng] = opid
        self.last_on[("dma", stream)] = opid
        return opid

    def fence(self):
        for k, v in self.last_on.items():
            self.fence_last[k] = v

    def finalize_and_emit(self, es):
        nc = self.nc
        sems = {e: es.enter_context(nc.semaphore(f"sem_{e}")) for e in ENGINES}
        dsems = {s: es.enter_context(nc.semaphore(f"dsem_{s}")) for s in self.streams}
        for o in self.ops:
            for d in o.deps:
                self.ops[d].inc = True
        for o in self.ops:
            if o.dma:
                o.inc = True
        cnt = {e: 0 for e in ENGINES}
        for o in self.ops:
            if o.dma:
                o.ord = o.snum * 16
            else:
                if o.inc:
                    cnt[o.eng] += 1
                o.ord = cnt[o.eng]
        emitted_waits = 0
        block = es.enter_context(nc.Block())
        prog = self

        def section(eng_name):
            def body(e):
                floor = {}
                for o in prog.ops:
                    if o.eng != eng_name:
                        continue
                    waits = {}
                    for d in o.deps:
                        od = prog.ops[d]
                        if od.dma:
                            key = ("dma", od.stream)
                            val = od.snum * 16
                        else:
                            key = od.eng
                            val = od.ord
                        if val > waits.get(key, 0):
                            waits[key] = val
                    for key, val in sorted(waits.items(), key=str):
                        if floor.get(key, 0) >= val:
                            continue
                        floor[key] = val
                        sem = dsems[key[1]] if isinstance(key, tuple) else sems[key]
                        e.wait_ge(sem, val)
                    inst = o.emit(e)
                    if o.inc:
                        if o.dma:
                            inst.then_inc(dsems[o.stream], 16)
                        else:
                            inst.then_inc(sems[o.eng], 1)
            return body

        for eng_name in ENGINES:
            dec = {"pe": block.tensor, "act": block.scalar, "dve": block.vector,
                   "pool": block.gpsimd, "sync": block.sync}[eng_name]
            if any(o.eng == eng_name for o in self.ops):
                dec(section(eng_name))
        return sems, dsems

    def pe_mm(self, out, lhsT, rhs, start=True, stop=True, reads=(), writes=(), tp=None, tag=""):
        def emit(e):
            return e.matmul(out=out, lhsT=lhsT, rhs=rhs, start=start, stop=stop,
                            tile_position=tp)
        return self.op("pe", emit, reads=reads, writes=writes, tag=tag)

    def pe_transpose(self, out, in_, identity, reads=(), writes=(), tag=""):
        def emit(e):
            return e.transpose(out=out, in_=in_, identity=identity)
        return self.op("pe", emit, reads=reads, writes=writes, tag=tag)

    def act(self, out, in_, func, bias=0.0, scale=1.0, accum_out=None,
            reads=(), writes=(), tag=""):
        def emit(e):
            return e.activation(out=out, in_=in_, func=func, bias=bias, scale=scale,
                                accum_out=accum_out)
        return self.op("act", emit, reads=reads, writes=writes, tag=tag)

    def dve(self, name, *args, reads=(), writes=(), sreads=(), tag="", **kw):
        def emit(e):
            return getattr(e, name)(*args, **kw)
        return self.op("dve", emit, reads=reads, writes=writes, sreads=sreads, tag=tag)

    def pool(self, name, *args, reads=(), writes=(), sreads=(), tag="", **kw):
        def emit(e):
            return getattr(e, name)(*args, **kw)
        return self.op("pool", emit, reads=reads, writes=writes, sreads=sreads, tag=tag)



class Ctx:

    def __init__(self, nc, es, p):
        self.nc, self.es, self.p = nc, es, p
        self.sb = {}
        self.dram = {}
        self.uid = 0
        self.cvals = {}

    def sbuf(self, name, shape, dt):
        t = self.es.enter_context(self.nc.sbuf_tensor(name, shape, dt))
        self.sb[name] = t
        return t

    def psum(self, name, shape, dt=F32):
        t = self.es.enter_context(self.nc.psum_tensor(name, shape, dt))
        return t

    def dram_in(self, name, shape, dt=F32):
        ap = self.nc.dram_tensor(name, list(shape), dt, kind="ExternalInput").ap()
        self.dram[name] = ap
        return ap

    def dram_out(self, name, shape, dt=F32):
        ap = self.nc.dram_tensor(name, list(shape), dt, kind="ExternalOutput").ap()
        self.dram[name] = ap
        return ap

    def dram_tmp(self, name, shape, dt=F32):
        ap = self.nc.dram_tensor(name, list(shape), dt).ap()
        self.dram[name] = ap
        return ap

    def uniq(self, base):
        self.uid += 1
        return f"{base}{self.uid}"

    def constf(self, *vals):
        for v in vals:
            v = float(v)
            if v in self.cvals:
                continue
            n = len(self.cvals)
            ap = self.sb["CVAL"][0:1, n:n + 1]
            self.p.dve("memset", ap, v, writes=[("CVAL",)])
            self.nc.const_aps.register(F32, v, ap)
            self.cvals[v] = ap


def alloc_pools(c: Ctx):
    c.sbuf("XRES", [128, NCHUNK * S], F32)
    c.sbuf("XLN", [128, NCHUNK * S], BF16)
    c.sbuf("WBUF", [128, 8192], BF16)
    c.sbuf("HBUF", [128, 4096], BF16)
    c.sbuf("FBUF", [128, 1536], F32)
    c.sbuf("SROW", [1, 3072], F32)
    c.sbuf("CONST", [128, 4096], BF16)
    c.sbuf("CVAL", [1, 64], F32)
    c.sbuf("BCOL", [128, 16], F32)
    c.sbuf("BROW", [1, 1024], BF16)
    c.sbuf("TRI", [128, 128], BF16)
    c.sbuf("VSTORE", [128, 16 * 1040], BF16)
    c.sbuf("KTILE", [128, S], BF16)
    c.sbuf("NODIAG", [128, 128], BF16)
    c.sbuf("QTILE", [64, TB], BF16)
    c.sbuf("ONESBR", [1, 128], BF16)
    c.sbuf("ONES", [128, 16], F32)
    c.sbuf("ONESR", [1, 128], F32)
    c.sbuf("ONESB", [128, 16], BF16)
    c.sbuf("IDN", [128, 128], F32)
    c.sbuf("GRB", [8, S], BF16)
    c.sbuf("WROW", [1, S], BF16)
    c.sbuf("WTOK", [128, 16], F32)
    c.sbuf("CONVW", [128, 32], F32)
    c.sbuf("B3COL", [128, 192], F32)
    c.pb = [c.psum(f"pb{i}", [128, 512], F32) for i in range(8)]
    c.pbT = c.pb[7][:].bitcast(BF16)


def rview(t, c0, n):
    return t[:, c0:c0 + n]


def xchunk(c, name, ch, t0=0, nt=S):
    return c.sb[name][:, ch * S + t0: ch * S + t0 + nt]


def declare_common(c: Ctx):
    c.dram_in("c_ident", (128, 128), F32)
    c.dram_in("c_tri", (128, 128), F32)


def host_common():
    kk, qq = np.meshgrid(np.arange(128), np.arange(128), indexing="ij")
    return {"c_ident": np.eye(128, dtype=np.float32),
            "c_tri": (kk <= qq).astype(np.float32)}


def emit_init(c: Ctx):
    p = c.p
    p.dma("pool", c.sb["TRI"][:], c.dram["c_tri"], writes=[("TRI",)], stream="cl")
    p.dma("sync", c.sb["IDN"][:], c.dram["c_ident"], writes=[("IDN",)], stream="cl")
    p.dve("memset", c.sb["ONES"][:], 1.0, writes=[("ONES",)])
    p.dve("memset", c.sb["ONESR"][:], 1.0, writes=[("ONES",)])
    p.dve("memset", c.sb["ONESB"][:], 1.0, writes=[("ONESB",)])
    p.dve("memset", c.sb["ONESBR"][:], 1.0, writes=[("ONES",)])


def emit_load_x(c: Ctx, x_in):
    p = c.p
    for ch in range(NCHUNK):
        p.dma("sync", xchunk(c, "XRES", ch), x_in[ch * 128:(ch + 1) * 128, :],
              writes=[("XRES", ch)], stream="xin")


def emit_store_x(c: Ctx, x_out):
    p = c.p
    for ch in range(NCHUNK):
        p.dma("sync", x_out[ch * 128:(ch + 1) * 128, :], xchunk(c, "XRES", ch),
              reads=[("XRES", ch)], writes=[("XOUT", ch)], stream="xout")
    p.op("pool", lambda e: e.nop(), reads=[("XOUT", ch) for ch in range(NCHUNK)])


def res_keys(c, name):

    return [(name, ch) for ch in range(NCHUNK)]


def emit_ln(c: Ctx, dst="XLN", eps=1e-5, tag="ln", src_name="XRES",
            banks=(0, 1, 2, 3)):
    p = c.p
    hb = c.sb["HBUF"]
    srow = c.sb["SROW"]
    onesb = c.sb["ONESB"]
    ones1 = c.sb["ONESR"]
    b_sx, b_sq, b_a, b_b = (c.pb[i] for i in banks)
    kb = [("PB", i) for i in banks]
    K = (tag,)
    for tb in range(NTB):
        t0 = tb * TB
        for ch in range(NCHUNK):
            xv = xchunk(c, src_name, ch, t0, TB)
            xbv = hb[:, 512:512 + TB]
            sqv = hb[:, 0:TB]
            p.dve("tensor_copy", out=xbv, in_=xv,
                  reads=[(src_name, ch)], writes=[(K, "xb")], tag=f"{tag}_cast")
            p.act(sqv, xv, AF.Square,
                  reads=[(src_name, ch)], writes=[(K, "sq")], tag=f"{tag}_sq")
            p.pe_mm(b_sx[0:1, 0:TB], onesb[:, 0:1], xbv,
                    start=(ch == 0), stop=(ch == NCHUNK - 1),
                    reads=[(K, "xb")], writes=[kb[0]])
            p.pe_mm(b_sq[0:1, 0:TB], onesb[:, 0:1], sqv,
                    start=(ch == 0), stop=(ch == NCHUNK - 1),
                    reads=[(K, "sq")], writes=[kb[1]])
        mean = srow[0:1, 0:TB]
        nm = srow[0:1, TB:2 * TB]
        t_ = srow[0:1, 2 * TB:3 * TB]
        var = srow[0:1, 3 * TB:4 * TB]
        rstd = srow[0:1, 4 * TB:5 * TB]
        p.dve("tensor_scalar_mul", out=mean, in0=b_sx[0:1, 0:TB], scalar1=1.0 / D,
              reads=[kb[0]], writes=[(K, "mean")])
        p.dve("tensor_scalar_mul", out=nm, in0=mean, scalar1=-1.0,
              reads=[(K, "mean")], writes=[(K, "nm")])
        p.dve("tensor_mul", out=t_, in0=nm, in1=mean,
              reads=[(K, "mean"), (K, "nm")], writes=[(K, "t")])
        p.dve("scalar_tensor_tensor", out=var, in0=b_sq[0:1, 0:TB], scalar=1.0 / D,
              in1=t_, op0=ALU.mult, op1=ALU.add,
              reads=[kb[1], (K, "t")], writes=[(K, "var")])
        c.constf(eps)
        p.act(var, var, AF.Ln, bias=eps,
              reads=[(K, "var"), ("CVAL",)], writes=[(K, "lnv")])
        p.act(rstd, var, AF.Exp, scale=-0.5,
              reads=[(K, "lnv")], writes=[(K, "rstd")])
        p.pe_mm(b_a[:, 0:TB], ones1[:], rstd, start=True, stop=True,
                reads=[(K, "rstd"), ("ONES",)], writes=[kb[2]])
        p.pe_mm(b_b[:, 0:TB], ones1[:], nm, start=True, stop=True,
                reads=[(K, "nm"), ("ONES",)], writes=[kb[3]])
        fb = c.sb["FBUF"]
        for ch in range(NCHUNK):
            xv = xchunk(c, src_name, ch, t0, TB)
            tv = fb[:, 0:TB]
            p.dve("tensor_add", out=tv, in0=xv, in1=b_b[:, 0:TB],
                  reads=[(src_name, ch), kb[3]], writes=[(K, "tv")])
            p.dve("tensor_mul", out=xchunk(c, dst, ch, t0, TB), in0=tv, in1=b_a[:, 0:TB],
                  reads=[(K, "tv"), kb[2]], writes=[(dst, ch)])



def host_prep_l0_attn(inp):
    d = {}
    for nm in ("Q", "K", "V"):
        w = inp[f"L0_attn_W_{nm}"]
        d[f"l0a_w{nm.lower()}"] = np.ascontiguousarray(
            np.transpose(w, (1, 0, 2)).reshape(D, D))
        b = inp[f"L0_attn_b_{nm}"]
        if nm == "V":
            d["l0a_bv_row"] = np.ascontiguousarray(b.reshape(1, D))
        else:
            d[f"l0a_b{nm.lower()}_col"] = np.ascontiguousarray(
                b.reshape(D)[None, :].reshape(8, 128).T)
    d["l0a_wo"] = np.ascontiguousarray(inp["L0_attn_W_O"].reshape(D, D))
    d["l0a_bo_col"] = np.ascontiguousarray(inp["L0_attn_b_O"].reshape(8, 128).T)
    pos = np.arange(S, dtype=np.float64)
    dim = np.arange(32, dtype=np.float64)
    freq = 10000.0 ** (dim / 32.0)
    freq = np.concatenate([freq, freq])
    ang = pos[:, None] / freq[None, :]
    cos = np.cos(ang).astype(np.float32).T
    sin = np.sin(ang).astype(np.float32).T
    sgn = np.where(np.arange(64) < 32, -1.0, 1.0).astype(np.float32)[:, None]
    d["l0a_cos"] = np.ascontiguousarray(np.tile(cos, (2, 1)))
    d["l0a_sin"] = np.ascontiguousarray(np.tile(sin * sgn, (2, 1)))
    return d


def declare_l0_attn(c: Ctx):
    pass_ = None
    for nm, shp in [("l0a_wq", (D, D)), ("l0a_wk", (D, D)), ("l0a_wv", (D, D)),
                    ("l0a_wo", (D, D)), ("l0a_bq_col", (128, 8)), ("l0a_bk_col", (128, 8)),
                    ("l0a_bv_row", (1, D)), ("l0a_bo_col", (128, 8)),
                    ("l0a_cos", (128, S)), ("l0a_sin", (128, S))]:
        c.dram_in(nm, shp, F32)
    c.dram_tmp("l0a_qT", (D, S), BF16)
    c.dram_tmp("l0a_kT", (D, S), BF16)


def _proj_rope(c, wname, bcol_name, dst_dram, tag):
    p = c.p
    wb = c.sb["WBUF"]
    fb = c.sb["FBUF"]
    hb = c.sb["HBUF"]
    cosT = c.sb["CONST"]
    K = (tag,)
    for dch in range(NCHUNK):
        p.dma("pool", wb[:, dch * 1024:(dch + 1) * 1024],
              c.dram[wname][dch * 128:(dch + 1) * 128, :],
              writes=[("WBUF", dch)], stream="wl", tag=f"{tag}_w")
    bcol = c.sb["BCOL"]
    p.dma("sync", bcol[:, 0:8], c.dram[bcol_name], writes=[("BCOL",)], stream="bl")
    for och in range(NCHUNK):
        for tb in range(NTB):
            t0 = tb * TB
            ps = c.pb[tb % 2]
            kps = ("PB", tb % 2)
            for dch in range(NCHUNK):
                p.pe_mm(ps[:], wb[:, dch * 1024 + och * 128: dch * 1024 + och * 128 + 128],
                        xchunk(c, "XLN", dch, t0, TB),
                        start=(dch == 0), stop=(dch == NCHUNK - 1),
                        reads=[("WBUF", dch), ("XLN", dch)], writes=[kps])
            qf = fb[:, 0:TB]
            qp = fb[:, TB:2 * TB]
            t1 = fb[:, 2 * TB:3 * TB]
            p.dve("tensor_scalar_add", out=qf, in0=ps[:], scalar1=bcol[:, och:och + 1],
                  reads=[kps, ("BCOL",)], writes=[(K, "qf")])
            for blk in range(4):
                srcr = (blk ^ 1) * 32
                p.dma("sync", qp[blk * 32:blk * 32 + 32, :], qf[srcr:srcr + 32, :],
                      reads=[(K, "qf")], writes=[(K, "qp", blk)], stream="shuf")
            p.dve("tensor_mul", out=t1, in0=qf, in1=cosT[:, t0:t0 + TB],
                  reads=[(K, "qf"), ("CONSTC",)], writes=[(K, "t1")])
            qb_ = hb[:, 0:TB]
            p.dve("tensor_mul", out=qp, in0=qp, in1=cosT[:, S + t0:S + t0 + TB],
                  reads=[(K, "qp", 0), (K, "qp", 1), (K, "qp", 2), (K, "qp", 3), ("CONSTC",)],
                  writes=[(K, "qp2")])
            p.dve("tensor_add", out=qb_, in0=t1, in1=qp,
                  reads=[(K, "t1"), (K, "qp2")], writes=[(K, "qb")])
            p.dma("sync", dst_dram[och * 128:(och + 1) * 128, t0:t0 + TB], qb_,
                  reads=[(K, "qb")], writes=[(K, "od", och, tb)], stream="qout")
    return [(K, "od", och, tb) for och in range(NCHUNK) for tb in range(NTB)]


def emit_l0_attn(c: Ctx):
    p = c.p
    wb = c.sb["WBUF"]
    hb = c.sb["HBUF"]
    fb = c.sb["FBUF"]
    srow = c.sb["SROW"]
    NH, HD = 16, 64
    VW = 65 * NH
    emit_ln(c, tag="l0a_ln")
    p.dma("pool", c.sb["CONST"][:, 0:S], c.dram["l0a_cos"], writes=[("CONSTC",)], stream="cl")
    p.dma("pool", c.sb["CONST"][:, S:2 * S], c.dram["l0a_sin"], writes=[("CONSTC",)], stream="cl")
    qdone = _proj_rope(c, "l0a_wq", "l0a_bq_col", c.dram["l0a_qT"], "l0q")
    kdone = _proj_rope(c, "l0a_wk", "l0a_bk_col", c.dram["l0a_kT"], "l0k")
    for dch in range(NCHUNK):
        p.dma("pool", wb[:, dch * 1024:(dch + 1) * 1024],
              c.dram["l0a_wv"][dch * 128:(dch + 1) * 128, :],
              writes=[("WBUF", dch)], stream="wl", tag="l0v_w")
    bvrow = c.sb["BROW"]
    p.dma("pool", bvrow[0:1, 0:D], c.dram["l0a_bv_row"], writes=[("BROW",)], stream="bl")
    vstore = c.sb["VSTORE"]
    p.dve("memset", vstore[:], 0.0, writes=[("VST", kt2) for kt2 in range(16)])
    p.dve("memset", vstore[:, 64::65], 1.0,
          reads=[("VST", kt2) for kt2 in range(16)],
          writes=[("VONE",)])
    for kt in range(S // 128):
        tt0 = kt * 128
        for half in range(2):
            ps = c.pb[2 + half]
            kps = ("PB", 2 + half)
            for dch in range(NCHUNK):
                p.pe_mm(ps[:], xchunk(c, "XLN", dch, tt0, 128),
                        wb[:, dch * 1024 + half * 512: dch * 1024 + half * 512 + 512],
                        start=(dch == 0), stop=False,
                        reads=[("WBUF", dch), ("XLN", dch)], writes=[kps])
            p.pe_mm(ps[:], c.sb["ONESBR"][0:1, 0:128],
                    bvrow[0:1, half * 512:half * 512 + 512],
                    start=False, stop=True,
                    reads=[("BROW",), ("ONES",)], writes=[kps])
            dstv = vstore[:, kt * VW + half * 8 * 65: kt * VW + half * 8 * 65 + 8 * 65]
            dst3 = dstv.rearrange("p (h w) -> p h w", h=8)[:, :, 0:64]
            src3 = ps[:].rearrange("p (h w) -> p h w", h=8)
            p.dve("tensor_copy", out=dst3, in_=src3,
                  reads=[kps, ("VONE",)], writes=[("VST", kt)], tag="vevac")
    p.fence()
    emit_flash16(c, "l0fl", c.dram["l0a_qT"], c.dram["l0a_kT"], qdone, kdone)
    emit_outproj_residual(c, "l0o", c.dram["l0a_wo"], c.dram["l0a_bo_col"])
    p.fence()



def host_prep_l0_ffn(inp):
    d = {}
    d["l0f_scol"] = np.ascontiguousarray(inp["L0_ffn_ln_scale"].reshape(8, 128).T)
    d["l0f_bcol"] = np.ascontiguousarray(inp["L0_ffn_ln_bias"].reshape(8, 128).T)
    d["l0f_gwT"] = np.ascontiguousarray(inp["L0_ffn_gate_w"].T)
    d["l0f_Wg"] = inp["L0_ffn_Wg"]
    d["l0f_Wu"] = inp["L0_ffn_Wu"]
    d["l0f_Wd"] = inp["L0_ffn_Wd"]
    d["l0f_sWg"] = inp["L0_ffn_sWg"]
    d["l0f_sWu"] = inp["L0_ffn_sWu"]
    d["l0f_sWd"] = inp["L0_ffn_sWd"]
    ident = np.eye(128, dtype=np.float32)
    d["c_ident"] = ident
    return d


def declare_l0_ffn(c: Ctx):
    c.dram_in("l0f_scol", (128, 8), F32)
    c.dram_in("l0f_bcol", (128, 8), F32)
    c.dram_in("l0f_gwT", (D, 8), F32)
    c.dram_in("l0f_Wg", (8, D, 2048), F32)
    c.dram_in("l0f_Wu", (8, D, 2048), F32)
    c.dram_in("l0f_Wd", (8, 2048, D), F32)
    c.dram_in("l0f_sWg", (D, 2048), F32)
    c.dram_in("l0f_sWu", (D, 2048), F32)
    c.dram_in("l0f_sWd", (2048, D), F32)
    c.dram_tmp("l0f_h", (2048, S), BF16)


def emit_moe_gate_top2(c: Ctx, nexp, gwT_name, scol_name, bcol_name, tag,
                       renorm_eps=None, use_sigmoid=False, inner_ln=True,
                       probs_first2=False, bias_row=None):
    p = c.p
    p.fence()
    hb = c.sb["HBUF"]
    fb = c.sb["FBUF"]
    wb = c.sb["WBUF"]
    srow = c.sb["SROW"]
    K = (tag,)
    gwt = c.sb["CONST"]
    p.dma("pool", gwt[:, 0:NCHUNK * nexp].rearrange("p (c e) -> p c e", c=NCHUNK),
          c.dram[gwT_name].rearrange("(c p) e -> p c e", p=128),
          writes=[(K, "gwt")], stream="wl")
    idn = c.sb["IDN"]
    if bias_row is not None:
        p.dma("pool", c.sb["BROW"][0:1, 0:nexp], c.dram[bias_row],
              writes=[(K, "brow")], stream="bl")
    if inner_ln:
        scol = c.sb["BCOL"]
        p.dma("sync", scol[:, 0:8], c.dram[scol_name], writes=[(K, "scol")], stream="bl")
        p.dma("sync", scol[:, 8:16], c.dram[bcol_name], writes=[(K, "bcol")], stream="bl")
    onesb = c.sb["ONESB"]
    pT = c.pbT
    grb = c.sb["GRB"]
    for tb in range(NTB):
        t0 = tb * TB
        if inner_ln:
            for ch in range(NCHUNK):
                xv = xchunk(c, "XLN", ch, t0, TB)
                sqv = hb[:, 0:TB]
                p.act(sqv, xv, AF.Square, reads=[("XLN", ch)], writes=[(K, "sq")])
                p.pe_mm(c.pb[0][0:1, 0:TB], onesb[:, 0:1], xv,
                        start=(ch == 0), stop=(ch == NCHUNK - 1),
                        reads=[("XLN", ch)], writes=[("PB", 0)])
                p.pe_mm(c.pb[1][0:1, 0:TB], onesb[:, 0:1], sqv,
                        start=(ch == 0), stop=(ch == NCHUNK - 1),
                        reads=[(K, "sq")], writes=[("PB", 1)])
            mean = srow[0:1, 0:TB]
            nm = srow[0:1, TB:2 * TB]
            t_ = srow[0:1, 2 * TB:3 * TB]
            var = srow[0:1, 3 * TB:4 * TB]
            rstd = srow[0:1, 4 * TB:5 * TB]
            p.dve("tensor_scalar_mul", out=mean, in0=c.pb[0][0:1, 0:TB], scalar1=1.0 / D,
                  reads=[("PB", 0)], writes=[(K, "mean")])
            p.dve("tensor_scalar_mul", out=nm, in0=mean, scalar1=-1.0,
                  reads=[(K, "mean")], writes=[(K, "nm")])
            p.dve("tensor_mul", out=t_, in0=nm, in1=mean,
                  reads=[(K, "mean"), (K, "nm")], writes=[(K, "t")])
            p.dve("scalar_tensor_tensor", out=var, in0=c.pb[1][0:1, 0:TB], scalar=1.0 / D,
                  in1=t_, op0=ALU.mult, op1=ALU.add,
                  reads=[("PB", 1), (K, "t")], writes=[(K, "var")])
            c.constf(1e-5)
            p.act(var, var, AF.Ln, bias=1e-5, reads=[(K, "var"), ("CVAL",)], writes=[(K, "lnv")])
            p.act(rstd, var, AF.Exp, scale=-0.5, reads=[(K, "lnv")], writes=[(K, "rstd")])
            p.pe_mm(c.pb[2][:, 0:TB], c.sb["ONESR"][:], rstd, start=True, stop=True,
                    reads=[(K, "rstd"), ("ONES",)], writes=[("PB", 2)])
            p.pe_mm(c.pb[3][:, 0:TB], c.sb["ONESR"][:], nm, start=True, stop=True,
                    reads=[(K, "nm"), ("ONES",)], writes=[("PB", 3)])
            for ch in range(NCHUNK):
                xv = xchunk(c, "XLN", ch, t0, TB)
                tv = fb[:, 0:TB]
                p.dve("tensor_add", out=tv, in0=xv, in1=c.pb[3][:, 0:TB],
                      reads=[("XLN", ch), ("PB", 3)], writes=[(K, "tv")])
                p.dve("tensor_mul", out=tv, in0=tv, in1=c.pb[2][:, 0:TB],
                      reads=[(K, "tv"), ("PB", 2)], writes=[(K, "tv2")])
                p.dve("tensor_single_scalar", out=tv, in_=tv, scalar=scol[:, ch:ch + 1],
                      op=ALU.mult, reads=[(K, "tv2"), (K, "scol")], writes=[(K, "tv3")])
                p.dve("tensor_scalar", out=hb[:, ch * TB:ch * TB + TB], in0=tv,
                      scalar1=scol[:, 8 + ch:9 + ch], scalar2=None, op0=ALU.add,
                      reads=[(K, "tv3"), (K, "bcol")], writes=[(K, "xn", ch)])
            xn_src = lambda ch: hb[:, ch * TB:ch * TB + TB]
            xn_key = lambda ch: (K, "xn", ch)
        else:
            xn_src = lambda ch: xchunk(c, "XLN", ch, t0, TB)
            xn_key = lambda ch: ("XLN", ch)
        for tt in range(TB // 128):
            ps_l = c.pb[4][:, 0:nexp]
            for ch in range(NCHUNK):
                p.pe_mm(ps_l, xn_src(ch)[:, tt * 128:(tt + 1) * 128],
                        gwt[:, ch * nexp:(ch + 1) * nexp],
                        start=(ch == 0),
                        stop=(ch == NCHUNK - 1 and bias_row is None),
                        reads=[xn_key(ch), (K, "gwt")], writes=[("PB", 4)])
            if bias_row is not None:
                p.pe_mm(ps_l, c.sb["ONESBR"][0:1, 0:128], c.sb["BROW"][0:1, 0:nexp],
                        start=False, stop=True,
                        reads=[(K, "brow"), ("ONES",)], writes=[("PB", 4)])
            B0 = 512
            L = fb[:, B0 + 0:B0 + nexp]
            E = fb[:, B0 + 16:B0 + 16 + nexp]
            em = fb[:, B0 + 32:B0 + 32 + nexp]
            e2 = fb[:, B0 + 48:B0 + 48 + nexp]
            m1 = fb[:, B0 + 96:B0 + 97]
            m2 = fb[:, B0 + 97:B0 + 98]
            ssum = fb[:, B0 + 98:B0 + 99]
            sinv = fb[:, B0 + 99:B0 + 100]
            p1 = fb[:, B0 + 100:B0 + 101]
            p2 = fb[:, B0 + 101:B0 + 102]
            q = fb[:, B0 + 102:B0 + 103]
            w1 = fb[:, B0 + 103:B0 + 104]
            w2 = fb[:, B0 + 104:B0 + 105]
            mk1 = fb[:, B0 + 112:B0 + 112 + nexp]
            mk2 = fb[:, B0 + 128:B0 + 128 + nexp]
            p.dve("tensor_copy", out=L, in_=ps_l, reads=[("PB", 4)], writes=[(K, "L")])
            if use_sigmoid:
                p.act(E, L, AF.Sigmoid, reads=[(K, "L")], writes=[(K, "E")])
            else:
                p.act(E, L, AF.Exp, reads=[(K, "L")], writes=[(K, "E")])
            p.dve("reduce_max", out=m1, in_=E, axis=AX.X, reads=[(K, "E")], writes=[(K, "m1")])
            p.dve("tensor_tensor", out=mk1, in0=E, in1=m1.broadcast_to([128, nexp]),
                  op=ALU.is_ge, reads=[(K, "E")], sreads=[(K, "m1")], writes=[(K, "mk1")])
            p.dve("tensor_mul", out=em, in0=E, in1=mk1,
                  reads=[(K, "E"), (K, "mk1")], writes=[(K, "em")])
            p.dve("tensor_sub", out=e2, in0=E, in1=em,
                  reads=[(K, "E"), (K, "em")], writes=[(K, "e2")])
            p.dve("reduce_max", out=m2, in_=e2, axis=AX.X, reads=[(K, "e2")], writes=[(K, "m2")])
            p.dve("tensor_tensor", out=mk2, in0=e2, in1=m2.broadcast_to([128, nexp]),
                  op=ALU.is_ge, reads=[(K, "e2")], sreads=[(K, "m2")], writes=[(K, "mk2")])
            if probs_first2:
                p.dve("reduce_sum", out=ssum, in_=E, axis=AX.X, reads=[(K, "E")], writes=[(K, "ss")])
                p.dve("reciprocal", out=sinv, in_=ssum, sreads=[(K, "ss")], writes=[(K, "si")])
                p.dve("tensor_mul", out=w1, in0=E[:, 0:1], in1=sinv,
                      reads=[(K, "E")], sreads=[(K, "si")], writes=[(K, "w1")])
                p.dve("tensor_mul", out=w2, in0=E[:, 1:2], in1=sinv,
                      reads=[(K, "E")], sreads=[(K, "si")], writes=[(K, "w2")])
            elif renorm_eps is not None:
                if use_sigmoid:
                    p.dve("tensor_add", out=q, in0=m1, in1=m2,
                          sreads=[(K, "m1"), (K, "m2")], writes=[(K, "q0")])
                    if renorm_eps:
                        p.dve("tensor_scalar_add", out=q, in0=q, scalar1=renorm_eps,
                              sreads=[(K, "q0")], writes=[(K, "q1")])
                    p.dve("reciprocal", out=q, in_=q, sreads=[(K, "q1") if renorm_eps else (K, "q0")], writes=[(K, "q")])
                    p.dve("tensor_mul", out=w1, in0=m1, in1=q,
                          sreads=[(K, "m1"), (K, "q")], writes=[(K, "w1")])
                    p.dve("tensor_mul", out=w2, in0=m2, in1=q,
                          sreads=[(K, "m2"), (K, "q")], writes=[(K, "w2")])
                else:
                    p.dve("reduce_sum", out=ssum, in_=E, axis=AX.X, reads=[(K, "E")], writes=[(K, "ss")])
                    p.dve("reciprocal", out=sinv, in_=ssum, sreads=[(K, "ss")], writes=[(K, "si")])
                    p.dve("tensor_mul", out=p1, in0=m1, in1=sinv,
                          sreads=[(K, "m1"), (K, "si")], writes=[(K, "p1")])
                    p.dve("tensor_mul", out=p2, in0=m2, in1=sinv,
                          sreads=[(K, "m2"), (K, "si")], writes=[(K, "p2")])
                    p.dve("tensor_add", out=q, in0=p1, in1=p2,
                          sreads=[(K, "p1"), (K, "p2")], writes=[(K, "q0")])
                    p.dve("tensor_scalar_add", out=q, in0=q, scalar1=renorm_eps,
                          sreads=[(K, "q0")], writes=[(K, "q1")])
                    p.dve("reciprocal", out=q, in_=q, sreads=[(K, "q1")], writes=[(K, "q")])
                    p.dve("tensor_mul", out=w1, in0=p1, in1=q,
                          sreads=[(K, "p1"), (K, "q")], writes=[(K, "w1")])
                    p.dve("tensor_mul", out=w2, in0=p2, in1=q,
                          sreads=[(K, "p2"), (K, "q")], writes=[(K, "w2")])
            else:
                p.dve("reduce_sum", out=ssum, in_=E, axis=AX.X, reads=[(K, "E")], writes=[(K, "ss")])
                p.dve("reciprocal", out=sinv, in_=ssum, sreads=[(K, "ss")], writes=[(K, "si")])
                p.dve("tensor_mul", out=w1, in0=m1, in1=sinv,
                      sreads=[(K, "m1"), (K, "si")], writes=[(K, "w1")])
                p.dve("tensor_mul", out=w2, in0=m2, in1=sinv,
                      sreads=[(K, "m2"), (K, "si")], writes=[(K, "w2")])
            p.dve("tensor_tensor", out=mk1, in0=mk1, in1=w1.broadcast_to([128, nexp]),
                  op=ALU.mult, reads=[(K, "mk1")], sreads=[(K, "w1")], writes=[(K, "wt1")])
            p.dve("tensor_tensor", out=mk2, in0=mk2, in1=w2.broadcast_to([128, nexp]),
                  op=ALU.mult, reads=[(K, "mk2")], sreads=[(K, "w2")], writes=[(K, "wt2")])
            wtb = c.sb["WTOK"]
            p.dve("tensor_add", out=wtb[:, 0:nexp], in0=mk1, in1=mk2,
                  reads=[(K, "wt1"), (K, "wt2")], writes=[(K, "wtok")])
            p.pe_transpose(c.pb[7][0:nexp, 0:128], wtb[:, 0:nexp], idn[:],
                           reads=[(K, "wtok"), ("IDN",)], writes=[("PB", 7)])
            p.dve("tensor_copy", out=grb[0:nexp, t0 + tt * 128:t0 + (tt + 1) * 128],
                  in_=c.pb[7][0:nexp, 0:128],
                  reads=[("PB", 7)], writes=[(K, "grb", tb, tt)], tag="grbevac")
    p.op("pool", lambda e: e.nop(),
         reads=[(K, "grb", tb, tt) for tb in range(NTB) for tt in range(TB // 128)],
         writes=[("GRBDONE",)])
    p.fence()


def emit_l0_ffn(c: Ctx):
    p = c.p
    wb = c.sb["WBUF"]
    hb = c.sb["HBUF"]
    emit_ln(c, tag="l0f_ln")
    emit_moe_gate_top2(c, 8, "l0f_gwT", "l0f_scol", "l0f_bcol", "l0f_gate",
                       renorm_eps=1e-5, inner_ln=True)
    emit_swiglu_experts(c, "l0f_moe", n_routed=8, NI=16, INTER=2048,
                        wg_r="l0f_Wg", wu_r="l0f_Wu", wd_r="l0f_Wd",
                        wg_s="l0f_sWg", wu_s="l0f_sWu", wd_s="l0f_sWd",
                        h_dram=c.dram["l0f_h"])
    p.fence()



def declare_l1_attn(c: Ctx):
    c.dram_in("l1a_xg", (8, S, D), BF16)


def emit_l1_attn(c: Ctx, st_list=None):
    p = c.p
    p.fence()
    wb = c.sb["WBUF"]
    hb = c.sb["HBUF"]
    fb = c.sb["FBUF"]
    xg = c.dram["l1a_xg"]
    idn = c.sb["IDN"]
    K = ("l1a",)
    for st in (st_list if st_list is not None else range(S // 128)):
        s0 = st * 128
        for b in range(8):
            p.dma("sync", wb[:, b * 1024:(b + 1) * 1024], xg[b, s0:s0 + 128, :],
                  writes=[(K, "xg", b)], stream="xg")
        G = fb[:, 1024:1032]
        E = fb[:, 1040:1048]
        ssum = fb[:, 1048:1049]
        winv = fb[:, 1049:1050]
        w = fb[:, 1056:1064]
        for b in range(8):
            prod = hb[:, (b % 2) * 1024:(b % 2) * 1024 + 1024]
            p.dve("tensor_mul", out=prod, in0=wb[:, 0:1024],
                  in1=wb[:, b * 1024:(b + 1) * 1024],
                  reads=[(K, "xg", 0), (K, "xg", b)], writes=[(K, "prod", b % 2)])
            p.act(hb[:, 2048 + (b % 2) * 1024: 2048 + (b % 2) * 1024 + 1024], prod,
                  AF.Identity, scale=1.0 / 32.0, accum_out=G[:, b:b + 1],
                  reads=[(K, "prod", b % 2)], writes=[(K, "G", b), (K, "pd", b % 2)])
        p.act(E, G, AF.Exp, reads=[(K, "G", b) for b in range(8)], writes=[(K, "E")])
        p.dve("reduce_sum", out=ssum, in_=E, axis=AX.X, reads=[(K, "E")], writes=[(K, "ss")])
        p.dve("reciprocal", out=winv, in_=ssum, sreads=[(K, "ss")], writes=[(K, "wi")])
        p.dve("tensor_scalar", out=w, in0=E, scalar1=winv, scalar2=None, op0=ALU.mult,
              reads=[(K, "E")], sreads=[(K, "wi")], writes=[(K, "w")])
        acc = fb[:, 0:1024]
        p.dve("tensor_scalar", out=acc, in0=wb[:, 0:1024], scalar1=w[:, 0:1],
              scalar2=None, op0=ALU.mult,
              reads=[(K, "xg", 0)], sreads=[(K, "w")], writes=[(K, "acc")])
        for b in range(1, 8):
            p.dve("scalar_tensor_tensor", out=acc, in0=wb[:, b * 1024:(b + 1) * 1024],
                  scalar=w[:, b:b + 1], in1=acc, op0=ALU.mult, op1=ALU.add,
                  reads=[(K, "xg", b), (K, "acc")], sreads=[(K, "w")], writes=[(K, "acc")])
        for ch in range(NCHUNK):
            p.pe_transpose(c.pb[ch % 2][:, 0:128],
                           acc[:, ch * 128:(ch + 1) * 128],
                           idn[:],
                           reads=[(K, "acc"), ("IDN",)], writes=[("PB", ch % 2)])
            p.dve("tensor_add", out=xchunk(c, "XRES", ch, s0, 128),
                  in0=xchunk(c, "XRES", ch, s0, 128), in1=c.pb[ch % 2][:, 0:128],
                  reads=[("XRES", ch), ("PB", ch % 2)], writes=[("XRES", ch)],
                  tag="l1a_res")
    p.fence()


def emit_flash16(c: Ctx, tag, qT_dram, kT_dram, qdone, kdone, scale=0.125):
    p = c.p
    hb = c.sb["HBUF"]
    fb = c.sb["FBUF"]
    srow = c.sb["SROW"]
    tri = c.sb["TRI"]
    vstore = c.sb["VSTORE"]
    ktile = c.sb["KTILE"]
    qtile = c.sb["QTILE"]
    NH = 16
    VW = 65 * NH
    K = (tag,)
    for h in range(NH):
        hch, hrow = h // 2, (h % 2) * 64
        p.dma("sync", ktile[0:64, :], kT_dram[h * 64:(h + 1) * 64, :],
              reads=kdone, writes=[(K, "KT")], stream="kld", tag=f"kld{h}")
        for qb in range(NTB):
            t0 = qb * TB
            p.dma("sync", qtile[0:64, 0:TB], qT_dram[h * 64:(h + 1) * 64, t0:t0 + TB],
                  reads=qdone, writes=[(K, "QT")], stream="kld", tag=f"qld{h}")
            pso = c.pb[6]
            kpso = ("PB", 6)
            nkt = (qb + 1) * 4
            for kt in range(nkt):
                ps = c.pb[3 + (kt % 2)]
                kps = ("PB", 3 + (kt % 2))
                p.pe_mm(ps[:], ktile[0:64, kt * 128:(kt + 1) * 128], qtile[0:64, 0:TB],
                        start=True, stop=True,
                        reads=[(K, "KT"), (K, "QT")], writes=[kps], tag=f"qk{h}")
                pt = hb[:, (kt % 2) * TB:(kt % 2) * TB + TB]
                kpt = ((K, "P", kt % 2),)
                p.act(pt, ps[:], AF.Exp, scale=scale,
                      reads=[kps], writes=[kpt], tag=f"exp{h}")
                qoff = kt * 128 - t0
                if qoff >= 0:
                    if qoff > 0:
                        p.pool("memset", hb[:, (kt % 2) * TB:(kt % 2) * TB + qoff], 0.0,
                               reads=[kpt], writes=[kpt], tag="mz")
                    p.pool("tensor_mul", out=hb[:, (kt % 2) * TB + qoff:(kt % 2) * TB + qoff + 128],
                           in0=hb[:, (kt % 2) * TB + qoff:(kt % 2) * TB + qoff + 128],
                           in1=tri[:], reads=[kpt, ("TRI",)], writes=[kpt], tag="mt")
                p.pe_mm(pso[0:65, :], vstore[:, kt * VW + h * 65: kt * VW + h * 65 + 65], pt,
                        start=(kt == 0), stop=(kt == nkt - 1),
                        reads=[("VST", kt), kpt], writes=[kpso], tag=f"pv{h}")
            lrow = srow[0:1, 5 * TB:5 * TB + TB]
            p.act(lrow, pso[64:65, 0:TB], AF.Ln, reads=[kpso], writes=[(K, "lr")])
            p.act(lrow, lrow, AF.Exp, scale=-1.0, reads=[(K, "lr")], writes=[(K, "lr2")])
            psn = c.pb[5]
            p.pe_mm(psn[0:64, 0:TB], c.sb["ONESR"][0:1, 0:64], lrow,
                    start=True, stop=True, reads=[(K, "lr2"), ("ONES",)], writes=[("PB", 5)])
            nb = fb[:, 0:TB]
            p.dve("tensor_copy", out=nb[0:64, :], in_=psn[0:64, 0:TB],
                  reads=[("PB", 5)], writes=[(K, "nb")])
            p.dve("tensor_mul", out=c.sb["XLN"][hrow:hrow + 64, hch * S + t0:hch * S + t0 + TB],
                  in0=pso[0:64, 0:TB], in1=nb[0:64, :],
                  reads=[kpso, (K, "nb")], writes=[("XLN", hch)], tag="aoevac")


def emit_outproj_residual(c: Ctx, tag, w_dram, bcol_dram=None):
    p = c.p
    wb = c.sb["WBUF"]
    for dch in range(NCHUNK):
        p.dma("pool", wb[:, dch * 1024:(dch + 1) * 1024],
              w_dram[dch * 128:(dch + 1) * 128, :],
              writes=[("WBUF", dch)], stream="wl", tag=f"{tag}_w")
    if bcol_dram is not None:
        bocol = c.sb["BCOL"]
        p.dma("sync", bocol[:, 8:16], bcol_dram, writes=[("BCOL2",)], stream="bl")
    for tb in range(NTB):
        t0 = tb * TB
        for dch in range(NCHUNK):
            ps = c.pb[dch % 2]
            kps = ("PB", dch % 2)
            for ach in range(NCHUNK):
                p.pe_mm(ps[:], wb[:, ach * 1024 + dch * 128: ach * 1024 + dch * 128 + 128],
                        xchunk(c, "XLN", ach, t0, TB),
                        start=(ach == 0), stop=(ach == NCHUNK - 1),
                        reads=[("WBUF", ach), ("XLN", ach)], writes=[kps])
            if bcol_dram is not None:
                p.dve("scalar_tensor_tensor", out=xchunk(c, "XRES", dch, t0, TB),
                      in0=ps[:], scalar=c.sb["BCOL"][:, 8 + dch:9 + dch],
                      in1=xchunk(c, "XRES", dch, t0, TB), op0=ALU.add, op1=ALU.add,
                      reads=[kps, ("BCOL2",), ("XRES", dch)], writes=[("XRES", dch)],
                      tag=f"{tag}_res")
            else:
                p.dve("tensor_add", out=xchunk(c, "XRES", dch, t0, TB),
                      in0=xchunk(c, "XRES", dch, t0, TB), in1=ps[:],
                      reads=[kps, ("XRES", dch)], writes=[("XRES", dch)],
                      tag=f"{tag}_res")



def host_prep_l1_ffn(inp):
    d = {}
    d["l1f_wln1_col"] = np.ascontiguousarray(inp["L1_ffn_w_ln1"].reshape(8, 128).T)
    d["l1f_wln2_col"] = np.ascontiguousarray(inp["L1_ffn_w_ln2"].reshape(8, 128).T)
    for nm in ("W_kvd", "W_qd", "W_ku", "W_qu", "W_vu", "W_ropeq", "W_ropek", "W_o",
               "Wg_s", "Wu_s", "Wd_s", "Wg_r", "Wu_r", "Wd_r"):
        d[f"l1f_{nm}"] = inp[f"L1_ffn_{nm}"]
    d["l1f_routerT"] = np.ascontiguousarray(inp["L1_ffn_W_router"].T)
    d["l1f_routerT"] = np.ascontiguousarray(inp["L1_ffn_W_router"])
    d["l1f_rbias"] = np.ascontiguousarray(inp["L1_ffn_routing_bias"].reshape(1, 7))
    hh = 32
    inv = 1.0 / (10000.0 ** (np.arange(0, hh, 2, dtype=np.float64) / hh))
    t = np.arange(S, dtype=np.float64)
    fr = np.outer(t, inv)
    emb = np.concatenate([fr, fr], 1)
    cos = np.cos(emb).astype(np.float32).T
    sin = np.sin(emb).astype(np.float32).T
    sgn = np.where(np.arange(32) < 16, -1.0, 1.0).astype(np.float32)[:, None]
    d["l1f_cos"] = np.ascontiguousarray(np.tile(cos, (4, 1)))
    d["l1f_sin"] = np.ascontiguousarray(np.tile(sin * sgn, (4, 1)))
    return d


def declare_l1_ffn(c: Ctx):
    H, L, NHH = D, 256, 512
    c.dram_in("l1f_wln1_col", (128, 8), F32)
    c.dram_in("l1f_wln2_col", (128, 8), F32)
    c.dram_in("l1f_W_kvd", (H, L), F32)
    c.dram_in("l1f_W_qd", (H, L), F32)
    c.dram_in("l1f_W_ku", (L, NHH), F32)
    c.dram_in("l1f_W_qu", (L, NHH), F32)
    c.dram_in("l1f_W_vu", (L, H), F32)
    c.dram_in("l1f_W_ropeq", (L, NHH), F32)
    c.dram_in("l1f_W_ropek", (H, NHH), F32)
    c.dram_in("l1f_W_o", (H, H), F32)
    c.dram_in("l1f_Wg_s", (H, H), F32)
    c.dram_in("l1f_Wu_s", (H, H), F32)
    c.dram_in("l1f_Wd_s", (H, H), F32)
    c.dram_in("l1f_Wg_r", (7, H, H), F32)
    c.dram_in("l1f_Wu_r", (7, H, H), F32)
    c.dram_in("l1f_Wd_r", (7, H, H), F32)
    c.dram_in("l1f_routerT", (H, 7), F32)
    c.dram_in("l1f_rbias", (1, 7), F32)
    c.dram_in("l1f_cos", (128, S), F32)
    c.dram_in("l1f_sin", (128, S), F32)
    c.dram_tmp("l1f_xln", (D, S), BF16)
    c.dram_tmp("l1f_x1", (D, S), BF16)
    c.dram_tmp("l1f_qT", (D, S), BF16)
    c.dram_tmp("l1f_kT", (D, S), BF16)
    c.dram_tmp("l1f_h", (1024, S), BF16)
    c.dram_tmp("l1f_vT", (S, 1024), BF16)


def emit_rms_inplace(c: Ctx, tag, wcol_name, eps=1e-5, src_dram=None, src_name="XLN"):
    p = c.p
    hb = c.sb["HBUF"]
    srow = c.sb["SROW"]
    onesb = c.sb["ONESB"]
    wcol = c.sb["BCOL"]
    K = (tag,)
    p.dma("sync", wcol[:, 0:8], c.dram[wcol_name], writes=[(K, "wcol")], stream="bl")
    for tb in range(NTB):
        t0 = tb * TB
        for ch in range(NCHUNK):
            if src_dram is not None:
                xv = hb[:, 2048 + ch % 2 * TB: 2048 + ch % 2 * TB + TB]
                p.dma("sync", xv, src_dram[ch * 128:(ch + 1) * 128, t0:t0 + TB],
                      writes=[(K, "xv", ch % 2)], stream="x1s")
                rk = [(K, "xv", ch % 2)]
            else:
                xv = xchunk(c, src_name, ch, t0, TB)
                rk = [(src_name, ch)]
            sqv = hb[:, 0:TB]
            p.act(sqv, xv, AF.Square, reads=rk, writes=[(K, "sq")])
            p.pe_mm(c.pb[0][0:1, 0:TB], onesb[:, 0:1], sqv,
                    start=(ch == 0), stop=(ch == NCHUNK - 1),
                    reads=[(K, "sq")], writes=[("PB", 0)])
        nrm = srow[0:1, 0:TB]
        p.dve("tensor_scalar_mul", out=nrm, in0=c.pb[0][0:1, 0:TB], scalar1=1.0 / D,
              reads=[("PB", 0)], writes=[(K, "ms")])
        p.act(nrm, nrm, AF.Sqrt, reads=[(K, "ms")], writes=[(K, "rt")])
        p.dve("tensor_scalar_add", out=nrm, in0=nrm, scalar1=eps,
              reads=[(K, "rt")], writes=[(K, "rte")])
        p.act(nrm, nrm, AF.Ln, reads=[(K, "rte")], writes=[(K, "ln")])
        p.act(nrm, nrm, AF.Exp, scale=-1.0, reads=[(K, "ln")], writes=[(K, "ri")])
        p.pe_mm(c.pb[1][:, 0:TB], c.sb["ONESR"][:], nrm, start=True, stop=True,
                reads=[(K, "ri"), ("ONES",)], writes=[("PB", 1)])
        for ch in range(NCHUNK):
            if src_dram is not None:
                xv = hb[:, 2048 + ch % 2 * TB: 2048 + ch % 2 * TB + TB]
                p.dma("sync", xv, src_dram[ch * 128:(ch + 1) * 128, t0:t0 + TB],
                      writes=[(K, "xv2", ch % 2)], stream="x1s")
                rk = [(K, "xv2", ch % 2)]
            else:
                xv = xchunk(c, src_name, ch, t0, TB)
                rk = [(src_name, ch)]
            tv = c.sb["FBUF"][:, 0:TB]
            p.dve("tensor_mul", out=tv, in0=xv, in1=c.pb[1][:, 0:TB],
                  reads=rk + [("PB", 1)], writes=[(K, "tv")])
            p.dve("tensor_scalar", out=xchunk(c, "XLN", ch, t0, TB), in0=tv,
                  scalar1=wcol[:, ch:ch + 1], scalar2=None, op0=ALU.mult,
                  reads=[(K, "tv"), (K, "wcol")], writes=[("XLN", ch)])


def emit_l1_ffn(c: Ctx, upto=None):
    p = c.p
    wb = c.sb["WBUF"]
    hb = c.sb["HBUF"]
    fb = c.sb["FBUF"]
    K = ("l1f",)
    emit_ln(c, tag="l1f_ln")
    for tb in range(NTB):
        t0 = tb * TB
        for ch in range(NCHUNK):
            p.dve("tensor_add", out=xchunk(c, "XRES", ch, t0, TB),
                  in0=xchunk(c, "XRES", ch, t0, TB), in1=xchunk(c, "XLN", ch, t0, TB),
                  reads=[("XRES", ch), ("XLN", ch)], writes=[("XRES", ch)], tag="l1f_addln")
    for ch in range(NCHUNK):
        p.dma("sync", c.dram["l1f_xln"][ch * 128:(ch + 1) * 128, :], xchunk(c, "XLN", ch),
              reads=[("XLN", ch)], writes=[(K, "xlnd", ch)], stream="qout")
    emit_rms_inplace(c, "l1f_rms1", "l1f_wln1_col")
    p.fence()
    if upto == "rms1":
        return
    vst = c.sb["VSTORE"]
    for nm, dst0 in (("l1f_W_kvd", 0), ("l1f_W_qd", 4096)):
        for dch in range(NCHUNK):
            p.dma("pool", wb[:, dch * 256:dch * 256 + 256],
                  c.dram[nm][dch * 128:(dch + 1) * 128, :],
                  writes=[(K, "wlat", dch)], stream="wl")
        for och in range(2):
            for tb in range(NTB):
                t0 = tb * TB
                ps = c.pb[tb % 2]
                kps = ("PB", tb % 2)
                for dch in range(NCHUNK):
                    p.pe_mm(ps[:], wb[:, dch * 256 + och * 128: dch * 256 + och * 128 + 128],
                            xchunk(c, "XLN", dch, t0, TB),
                            start=(dch == 0), stop=(dch == NCHUNK - 1),
                            reads=[(K, "wlat", dch), ("XLN", dch)], writes=[kps])
                p.dve("tensor_copy", out=vst[:, dst0 + och * S + t0: dst0 + och * S + t0 + TB],
                      in_=ps[:], reads=[kps], writes=[(K, "lat", dst0, och)], tag="latevac")
    latk = [(K, "lat", d0, o) for d0 in (0, 4096) for o in range(2)]
    cosT = c.sb["CONST"]
    p.dma("pool", cosT[:, 0:S], c.dram["l1f_cos"], writes=[("CONSTC",)], stream="cl")
    p.dma("pool", cosT[:, S:2 * S], c.dram["l1f_sin"], writes=[("CONSTC",)], stream="cl")
    for nm, lat0, dst in (("l1f_W_ku", 0, "l1f_kT"), ("l1f_W_qu", 4096, "l1f_qT")):
        for lch in range(2):
            p.dma("pool", wb[:, 2048 + lch * 512:2048 + lch * 512 + 512],
                  c.dram[nm][lch * 128:(lch + 1) * 128, :],
                  writes=[(K, "wup", lch)], stream="wl")
        for och in range(4):
            for tb in range(NTB):
                t0 = tb * TB
                ps = c.pb[2]
                kps = ("PB", 2)
                for lch in range(2):
                    p.pe_mm(ps[:], wb[:, 2048 + lch * 512 + och * 128: 2048 + lch * 512 + och * 128 + 128],
                            vst[:, lat0 + lch * S + t0: lat0 + lch * S + t0 + TB],
                            start=(lch == 0), stop=(lch == 1),
                            reads=[(K, "wup", lch)] + latk, writes=[kps])
                bfv = hb[:, 0:TB]
                p.dve("tensor_copy", out=bfv, in_=ps[:], reads=[kps], writes=[(K, "upe")])
                for hh4 in range(4):
                    p.dma("sync",
                          c.dram[dst][och * 256 + hh4 * 64: och * 256 + hh4 * 64 + 32, t0:t0 + TB],
                          bfv[hh4 * 32:(hh4 + 1) * 32, :],
                          reads=[(K, "upe")], writes=[(K, "kqd", dst, och, tb)], stream="qout")
    for nm, src_mode, dst in (("l1f_W_ropek", "xln", "l1f_kT"), ("l1f_W_ropeq", "qlat", "l1f_qT")):
        nsrc = NCHUNK if src_mode == "xln" else 2
        for sch in range(nsrc):
            p.dma("pool", wb[:, 4096 + sch * 512:4096 + sch * 512 + 512],
                  c.dram[nm][sch * 128:(sch + 1) * 128, :],
                  writes=[(K, "wrp", sch)], stream="wl")
        for och in range(4):
            for tb in range(NTB):
                t0 = tb * TB
                ps = c.pb[3]
                kps = ("PB", 3)
                for sch in range(nsrc):
                    rhsv = (xchunk(c, "XLN", sch, t0, TB) if src_mode == "xln"
                            else vst[:, 4096 + sch * S + t0: 4096 + sch * S + t0 + TB])
                    rrd = [("XLN", sch)] if src_mode == "xln" else latk
                    p.pe_mm(ps[:], wb[:, 4096 + sch * 512 + och * 128: 4096 + sch * 512 + och * 128 + 128],
                            rhsv,
                            start=(sch == 0), stop=(sch == nsrc - 1),
                            reads=[(K, "wrp", sch)] + rrd, writes=[kps])
                qf = fb[:, 0:TB]
                qp = fb[:, TB:2 * TB]
                p.dve("tensor_copy", out=qf, in_=ps[:], reads=[kps], writes=[(K, "qf")])
                for blk in range(8):
                    srcr = (blk ^ 1) * 16
                    p.dma("sync", qp[blk * 16:blk * 16 + 16, :], qf[srcr:srcr + 16, :],
                          reads=[(K, "qf")], writes=[(K, "qp", blk)], stream="shuf")
                t1 = fb[:, 2 * TB:3 * TB]
                p.dve("tensor_mul", out=t1, in0=qf, in1=cosT[:, t0:t0 + TB],
                      reads=[(K, "qf"), ("CONSTC",)], writes=[(K, "t1")])
                p.dve("tensor_mul", out=qp, in0=qp, in1=cosT[:, S + t0:S + t0 + TB],
                      reads=[(K, "qp", b) for b in range(8)] + [("CONSTC",)],
                      writes=[(K, "qp2")])
                bfv = hb[:, 0:TB]
                p.dve("tensor_add", out=bfv, in0=t1, in1=qp,
                      reads=[(K, "t1"), (K, "qp2")], writes=[(K, "rp")])
                for hh4 in range(4):
                    p.dma("sync",
                          c.dram[dst][och * 256 + hh4 * 64 + 32: och * 256 + hh4 * 64 + 64, t0:t0 + TB],
                          bfv[hh4 * 32:(hh4 + 1) * 32, :],
                          reads=[(K, "rp")], writes=[(K, "rpd", dst, och, tb)], stream="qout")
    p.fence()
    for lch in range(2):
        p.dma("pool", wb[:, lch * 1024:lch * 1024 + 1024],
              c.dram["l1f_W_vu"][lch * 128:(lch + 1) * 128, :],
              writes=[(K, "wv", lch)], stream="wl")
    vstore = c.sb["VSTORE"]
    VW = 65 * 16
    for kt in range(S // 128):
        tt0 = kt * 128
        for half in range(2):
            ps = c.pb[2 + half]
            kps = ("PB", 2 + half)
            for lch in range(2):
                p.pe_mm(ps[:], vst[:, lch * S + tt0: lch * S + tt0 + 128],
                        wb[:, lch * 1024 + half * 512: lch * 1024 + half * 512 + 512],
                        start=(lch == 0), stop=(lch == 1),
                        reads=[(K, "wv", lch)] + latk, writes=[kps])
            bfv = hb[:, half * TB:half * TB + TB]
            p.dve("tensor_copy", out=bfv, in_=ps[:], reads=[kps], writes=[(K, "ve", half)])
            p.dma("sync", c.dram["l1f_vT"][tt0:tt0 + 128, half * 512:half * 512 + 512], bfv,
                  reads=[(K, "ve", half)], writes=[(K, "vd", kt, half)], stream="qout")
    p.fence()
    if upto == "proj":
        return
    p.dve("memset", vstore[:], 0.0, writes=[("VST", kt2) for kt2 in range(16)], tag="vclr")
    p.dve("memset", vstore[:, 64::65], 1.0,
          reads=[("VST", kt2) for kt2 in range(16)], writes=[("VONE",)])
    for kt in range(S // 128):
        p.dma("sync",
              vstore[:, kt * VW:(kt + 1) * VW].rearrange("p (h w) -> p h w", h=16)[:, :, 0:64],
              c.dram["l1f_vT"][kt * 128:(kt + 1) * 128, :].rearrange("p (h w) -> p h w", h=16),
              reads=[(K, "vd", kt, 0), (K, "vd", kt, 1), ("VONE",)],
              writes=[("VST", kt)], stream="kld")
    p.fence()
    if upto == "refill":
        return
    qdone = [(K, "kqd", "l1f_qT", och, tb) for och in range(4) for tb in range(NTB)] + \
            [(K, "rpd", "l1f_qT", och, tb) for och in range(4) for tb in range(NTB)]
    kdone = [(K, "kqd", "l1f_kT", och, tb) for och in range(4) for tb in range(NTB)] + \
            [(K, "rpd", "l1f_kT", och, tb) for och in range(4) for tb in range(NTB)]
    emit_flash16(c, "l1fl", c.dram["l1f_qT"], c.dram["l1f_kT"], qdone, kdone)
    p.fence()
    if upto == "flash":
        return
    for dch in range(NCHUNK):
        p.dma("pool", wb[:, dch * 1024:(dch + 1) * 1024],
              c.dram["l1f_W_o"][dch * 128:(dch + 1) * 128, :],
              writes=[("WBUF", dch)], stream="wl", tag="l1o_w")
    for tb in range(NTB):
        t0 = tb * TB
        for dch in range(NCHUNK):
            ps = c.pb[dch % 2]
            kps = ("PB", dch % 2)
            for ach in range(NCHUNK):
                p.pe_mm(ps[:], wb[:, ach * 1024 + dch * 128: ach * 1024 + dch * 128 + 128],
                        xchunk(c, "XLN", ach, t0, TB),
                        start=(ach == 0), stop=(ach == NCHUNK - 1),
                        reads=[("WBUF", ach), ("XLN", ach)], writes=[kps])
            xlt = hb[:, 0:TB]
            p.dma("sync", xlt, c.dram["l1f_xln"][dch * 128:(dch + 1) * 128, t0:t0 + TB],
                  reads=[(K, "xlnd", dch)], writes=[(K, "xlt")], stream="x1s")
            p.dve("tensor_add", out=xchunk(c, "XRES", dch, t0, TB),
                  in0=xchunk(c, "XRES", dch, t0, TB), in1=ps[:],
                  reads=[kps, ("XRES", dch)], writes=[("XRES", dch)], tag="l1mla_res")
            x1t = hb[:, TB:2 * TB]
            p.dve("tensor_add", out=x1t, in0=xlt, in1=ps[:],
                  reads=[kps, (K, "xlt")], writes=[(K, "x1t")])
            p.dma("sync", c.dram["l1f_x1"][dch * 128:(dch + 1) * 128, t0:t0 + TB], x1t,
                  reads=[(K, "x1t")], writes=[(K, "x1d", dch, tb)], stream="qout")
    p.fence()
    if upto == "mla":
        return
    emit_rms_inplace(c, "l1f_rms2", "l1f_wln2_col", src_dram=c.dram["l1f_x1"])
    p.fence()
    emit_moe_gate_top2(c, 7, "l1f_routerT", None, None, "l1f_gate",
                       renorm_eps=0.0, use_sigmoid=True, inner_ln=False,
                       bias_row="l1f_rbias")
    emit_swiglu_experts(c, "l1f_moe", n_routed=7, NI=8, INTER=1024,
                        wg_r="l1f_Wg_r", wu_r="l1f_Wu_r", wd_r="l1f_Wd_r",
                        wg_s="l1f_Wg_s", wu_s="l1f_Wu_s", wd_s="l1f_Wd_s",
                        h_dram=c.dram["l1f_h"])
    p.fence()


def emit_swiglu_experts(c: Ctx, tag, n_routed, NI, INTER,
                        wg_r, wu_r, wd_r, wg_s, wu_s, wd_s, h_dram):
    p = c.p
    wb = c.sb["WBUF"]
    hb = c.sb["HBUF"]
    grb = c.sb["GRB"]
    wrow = c.sb["WROW"]
    for e in range(n_routed + 1):
        shared = (e == n_routed)
        wg_d = c.dram[wg_s] if shared else c.dram[wg_r][e]
        wu_d = c.dram[wu_s] if shared else c.dram[wu_r][e]
        wd_d = c.dram[wd_s] if shared else c.dram[wd_r][e]
        if not shared:
            p.dma("sync", wrow[0:1, :], grb[e:e + 1, :],
                  reads=[("GRBDONE",)], writes=[("WROW",)], stream="wrow")
            for tb in range(NTB):
                p.pe_mm(c.pb[2][:, 0:TB], c.sb["ONESBR"][0:1, 0:128],
                        wrow[0:1, tb * TB:(tb + 1) * TB],
                        start=True, stop=True,
                        reads=[("WROW",), ("ONES",)], writes=[("PB", 2)])
                p.dve("tensor_copy", out=hb[:, 1024 + tb * TB:1024 + (tb + 1) * TB],
                      in_=c.pb[2][:, 0:TB],
                      reads=[("PB", 2)], writes=[("WBC", tb)], tag="wbc")
        wgv = wb[:, 0:2048].rearrange("p (s c n) -> p s c n", s=2, c=NCHUNK)
        wuv = wb[:, 2048:4096].rearrange("p (s c n) -> p s c n", s=2, c=NCHUNK)
        wdv = wb[:, 4096:6144].rearrange("p (s n) -> p s n", s=2)
        for i in range(NI):
            sl = i % 2
            p.dma("pool", wgv[:, sl],
                  wg_d.rearrange("(c p) n -> p c n", p=128)[:, :, i * 128:(i + 1) * 128],
                  writes=[("WG", sl)], stream="wl", tag=f"{tag}wg{e}_{i}")
            p.dma("pool", wuv[:, sl],
                  wu_d.rearrange("(c p) n -> p c n", p=128)[:, :, i * 128:(i + 1) * 128],
                  writes=[("WU", sl)], stream="wl", tag=f"{tag}wu{e}_{i}")
            for tb in range(NTB):
                t0 = tb * TB
                for ch in range(NCHUNK):
                    p.pe_mm(c.pb[0][:], wgv[:, sl, ch], xchunk(c, "XLN", ch, t0, TB),
                            start=(ch == 0), stop=(ch == NCHUNK - 1),
                            reads=[("WG", sl), ("XLN", ch)], writes=[("PB", 0)])
                    p.pe_mm(c.pb[1][:], wuv[:, sl, ch], xchunk(c, "XLN", ch, t0, TB),
                            start=(ch == 0), stop=(ch == NCHUNK - 1),
                            reads=[("WU", sl), ("XLN", ch)], writes=[("PB", 1)])
                gs = hb[:, 3072:3584]
                uw = hb[:, 3584:4096]
                p.act(gs, c.pb[0][:], AF.Silu, reads=[("PB", 0)], writes=[("GS",)])
                if shared:
                    p.dve("tensor_copy", out=uw, in_=c.pb[1][:],
                          reads=[("PB", 1)], writes=[("UW",)])
                else:
                    p.dve("tensor_mul", out=uw, in0=c.pb[1][:],
                          in1=hb[:, 1024 + tb * TB:1024 + (tb + 1) * TB],
                          reads=[("PB", 1), ("WBC", tb)], writes=[("UW",)])
                htile = hb[:, sl * TB:(sl + 1) * TB]
                p.dve("tensor_mul", out=htile, in0=gs, in1=uw,
                      reads=[("GS",), ("UW",)], writes=[("HT", sl)])
                p.dma("sync", h_dram[i * 128:(i + 1) * 128, t0:t0 + TB], htile,
                      reads=[("HT", sl)], writes=[("HD", i, tb)], stream="hout")
        for tb in range(NTB):
            t0 = tb * TB
            for i in range(NI):
                sl = i % 2
                p.dma("pool", wdv[:, sl], wd_d[i * 128:(i + 1) * 128, :],
                      writes=[("WD", sl)], stream="wl", tag=f"{tag}wd{e}_{i}")
                p.dma("sync", hb[:, sl * TB:(sl + 1) * TB], h_dram[i * 128:(i + 1) * 128, t0:t0 + TB],
                      reads=[("HD", i, tb)], writes=[("HT", sl)], stream="hin")
                for dch in range(NCHUNK):
                    p.pe_mm(c.pb[dch][:], wdv[:, sl, dch * 128:(dch + 1) * 128],
                            hb[:, sl * TB:(sl + 1) * TB],
                            start=(i == 0), stop=(i == NI - 1),
                            reads=[("WD", sl), ("HT", sl)], writes=[("PB", dch)])
            for dch in range(NCHUNK):
                p.dve("tensor_add", out=xchunk(c, "XRES", dch, t0, TB),
                      in0=xchunk(c, "XRES", dch, t0, TB), in1=c.pb[dch][:],
                      reads=[("XRES", dch), ("PB", dch)], writes=[("XRES", dch)],
                      tag=f"{tag}_res")



def host_prep_l2_attn(inp):
    d = {}
    d["l2a_prw_col"] = np.ascontiguousarray(inp["L2_attn_pre_norm_w"].reshape(8, 128).T)
    d["l2a_ffnw_col"] = np.ascontiguousarray(inp["L2_attn_ffn_norm_w"].reshape(8, 128).T)
    cw = inp["L2_attn_conv_w"].reshape(1024, 4)
    d["l2a_convw_col"] = np.ascontiguousarray(cw.reshape(8, 128, 4).transpose(1, 0, 2).reshape(128, 32))
    d["l2a_convb_col"] = np.ascontiguousarray(inp["L2_attn_conv_b"].reshape(8, 128).T)
    d["l2a_db_col"] = np.ascontiguousarray(inp["L2_attn_decay_bias"].reshape(8, 128).T)
    d["l2a_waT"] = np.ascontiguousarray(inp["L2_attn_w_a"].T)
    d["l2a_wiT"] = np.ascontiguousarray(inp["L2_attn_w_i"].T)
    d["l2a_wvT"] = np.ascontiguousarray(inp["L2_attn_w_v"].T)
    d["l2a_wopT"] = np.ascontiguousarray(inp["L2_attn_out_proj_w"].T)
    wgu = inp["L2_attn_w_gate_up"]
    d["l2a_wg"] = np.ascontiguousarray(wgu[0:4096].T)
    d["l2a_wu"] = np.ascontiguousarray(wgu[4096:8192].T)
    d["l2a_wd"] = np.ascontiguousarray(inp["L2_attn_w_down"].T)
    return d


def declare_l2_attn(c: Ctx):
    for nm, shp in [("l2a_prw_col", (128, 8)), ("l2a_ffnw_col", (128, 8)),
                    ("l2a_convw_col", (128, 32)), ("l2a_convb_col", (128, 8)),
                    ("l2a_db_col", (128, 8)),
                    ("l2a_waT", (D, D)), ("l2a_wiT", (D, D)), ("l2a_wvT", (D, D)),
                    ("l2a_wopT", (D, D)), ("l2a_wg", (D, 4096)), ("l2a_wu", (D, 4096)),
                    ("l2a_wd", (4096, D))]:
        c.dram_in(nm, shp, F32)
    c.dram_tmp("l2a_a", (D, S), F32)
    c.dram_tmp("l2a_iv", (D, S), F32)
    c.dram_tmp("l2a_h", (4096, S), BF16)
    c.dram_tmp("l2a_xln", (D, S), BF16)
    c.dram_tmp("l2a_x1", (D, S), BF16)


def emit_l2_attn(c: Ctx, upto=None):
    p = c.p
    wb = c.sb["WBUF"]
    hb = c.sb["HBUF"]
    fb = c.sb["FBUF"]
    K = ("l2a",)
    emit_ln(c, tag="l2a_ln")
    for tb in range(NTB):
        t0 = tb * TB
        for ch in range(NCHUNK):
            p.dve("tensor_add", out=xchunk(c, "XRES", ch, t0, TB),
                  in0=xchunk(c, "XRES", ch, t0, TB), in1=xchunk(c, "XLN", ch, t0, TB),
                  reads=[("XRES", ch), ("XLN", ch)], writes=[("XRES", ch)], tag="l2a_addln")
    for ch in range(NCHUNK):
        p.dma("sync", c.dram["l2a_xln"][ch * 128:(ch + 1) * 128, :], xchunk(c, "XLN", ch),
              reads=[("XLN", ch)], writes=[(K, "xlnd", ch)], stream="qout")
    emit_rms_inplace(c, "l2a_rms1", "l2a_prw_col", eps=1e-6)
    p.fence()
    if upto == "rms1":
        return
    ccol = c.sb["BCOL"]
    p.dma("sync", ccol[:, 0:8], c.dram["l2a_convb_col"], writes=[(K, "cb")], stream="bl")
    cwcol = c.sb["CONVW"]
    p.dma("sync", cwcol[:], c.dram["l2a_convw_col"], writes=[(K, "cw")], stream="bl")
    for ch in range(NCHUNK):
        pad = hb[:, 0:2051]
        p.dve("memset", pad[:, 0:3], 0.0, writes=[(K, "pad0")])
        p.dve("tensor_copy", out=pad[:, 3:2051], in_=xchunk(c, "XLN", ch),
              reads=[("XLN", ch), (K, "pad0")], writes=[(K, "pad")])
        acc2 = c.sb["VSTORE"][:].bitcast(F32)[:, 0:2048]
        p.dve("tensor_scalar", out=acc2, in0=pad[:, 0:2048], scalar1=cwcol[:, ch * 4:ch * 4 + 1],
              scalar2=None, op0=ALU.mult,
              reads=[(K, "pad"), (K, "cw")], writes=[(K, "acc")])
        for j in range(1, 4):
            p.dve("scalar_tensor_tensor", out=acc2, in0=pad[:, j:j + 2048],
                  scalar=cwcol[:, ch * 4 + j:ch * 4 + j + 1], in1=acc2,
                  op0=ALU.mult, op1=ALU.add,
                  reads=[(K, "pad"), (K, "cw"), (K, "acc")], writes=[(K, "acc")])
        p.dve("tensor_scalar", out=xchunk(c, "XLN", ch), in0=acc2,
              scalar1=ccol[:, ch:ch + 1], scalar2=None, op0=ALU.add,
              reads=[(K, "acc"), (K, "cb")], writes=[("XLN", ch)])
    p.fence()
    if upto == "conv":
        return
    dbcol = c.sb["BCOL"]
    p.dma("sync", dbcol[:, 8:16], c.dram["l2a_db_col"], writes=[(K, "db")], stream="bl")
    for nm, role in (("l2a_waT", "a"), ("l2a_wiT", "i"), ("l2a_wvT", "v")):
        for dch in range(NCHUNK):
            p.dma("pool", wb[:, dch * 1024:(dch + 1) * 1024],
                  c.dram[nm][dch * 128:(dch + 1) * 128, :],
                  writes=[("WBUF", dch)], stream="wl", tag=f"l2a_{role}w")
        for och in range(NCHUNK):
            for tb in range(NTB):
                t0 = tb * TB
                bank = 2 + ((och * NTB + tb) % 2) if role != "v" else 4 + ((och * NTB + tb) % 2)
                ps = c.pb[bank]
                kps = ("PB", bank)
                for dch in range(NCHUNK):
                    p.pe_mm(ps[:], wb[:, dch * 1024 + och * 128: dch * 1024 + och * 128 + 128],
                            xchunk(c, "XLN", dch, t0, TB),
                            start=(dch == 0), stop=(dch == NCHUNK - 1),
                            reads=[("WBUF", dch), ("XLN", dch)], writes=[kps])
                if role == "a":
                    av = fb[:, 0:TB]
                    p.act(av, ps[:], AF.Sigmoid, bias=dbcol[:, 8 + och:9 + och],
                          reads=[kps, (K, "db")], writes=[(K, "av")])
                    p.dma("sync", c.dram["l2a_a"][och * 128:(och + 1) * 128, t0:t0 + TB], av,
                          reads=[(K, "av")], writes=[(K, "ad", och, tb)], stream="hout")
                elif role == "i":
                    iv = fb[:, TB:2 * TB]
                    p.act(iv, ps[:], AF.Sigmoid, reads=[kps], writes=[(K, "ig", och, tb)])
                    p.dma("sync", c.dram["l2a_iv"][och * 128:(och + 1) * 128, t0:t0 + TB], iv,
                          reads=[(K, "ig", och, tb)], writes=[(K, "igd", och, tb)], stream="hout")
                else:
                    iv = fb[:, TB:2 * TB]
                    p.dma("sync", iv, c.dram["l2a_iv"][och * 128:(och + 1) * 128, t0:t0 + TB],
                          reads=[(K, "igd", och, tb)], writes=[(K, "ig2", och, tb)], stream="hin")
                    vv = fb[:, 2 * TB:3 * TB]
                    p.dve("tensor_mul", out=vv, in0=iv, in1=ps[:],
                          reads=[kps, (K, "ig2", och, tb)], writes=[(K, "vv")])
                    p.dma("sync", c.dram["l2a_iv"][och * 128:(och + 1) * 128, t0:t0 + TB], vv,
                          reads=[(K, "vv")], writes=[(K, "ivd", och, tb)], stream="hout")
    p.fence()
    if upto == "gates":
        return
    LNCLIP = float(np.log(1e-10))
    vsf = c.sb["VSTORE"][:].bitcast(F32)
    A = vsf[:, 0:2048]
    Bv = vsf[:, 2048:4096]
    C = vsf[:, 4096:6144]
    Dv = vsf[:, 6144:8192]
    for ch in range(NCHUNK):
        Ka = (K, "sc", "A")
        Kb = (K, "sc", "B")
        Kc = (K, "sc", "C")
        Kd = (K, "sc", "D")
        p.dma("sync", A, c.dram["l2a_a"][ch * 128:(ch + 1) * 128, :],
              reads=[(K, "ad", ch, tb) for tb in range(NTB)], writes=[Ka], stream="hin")
        p.dma("sync", Bv, c.dram["l2a_iv"][ch * 128:(ch + 1) * 128, :],
              reads=[(K, "ivd", ch, tb) for tb in range(NTB)], writes=[Kb], stream="hin")
        p.act(C, A, AF.Square, reads=[Ka], writes=[Kc])
        p.dve("tensor_scalar_mul", out=C, in0=C, scalar1=-1.0, reads=[Kc], writes=[Kc])
        p.dve("tensor_scalar_add", out=C, in0=C, scalar1=1.0, reads=[Kc], writes=[Kc])
        p.dve("tensor_scalar_max", out=C, in0=C, scalar1=1e-8, reads=[Kc], writes=[Kc])
        p.act(C, C, AF.Sqrt, reads=[Kc], writes=[Kc])
        p.dve("tensor_mul", out=Bv, in0=Bv, in1=C, reads=[Kb, Kc], writes=[Kb])
        p.act(C, A, AF.Ln, reads=[Ka], writes=[Kc])
        for blk in range(32):
            p.dve("tensor_tensor_scan", out=A[:, blk * 64:(blk + 1) * 64],
                  data0=C[:, blk * 64:(blk + 1) * 64], data1=C[:, blk * 64:(blk + 1) * 64],
                  initial=0.0, op0=ALU.add, op1=ALU.bypass,
                  reads=[Kc], writes=[Ka], tag="cum1")
        p.act(C, A, AF.Exp, reads=[Ka], writes=[Kc])
        p.dve("tensor_scalar_max", out=Dv, in0=A, scalar1=LNCLIP, reads=[Ka], writes=[Kd])
        p.act(Dv, Dv, AF.Exp, scale=-1.0, reads=[Kd], writes=[Kd])
        p.dve("tensor_mul", out=Bv, in0=Bv, in1=Dv, reads=[Kb, Kd], writes=[Kb])
        for blk in range(32):
            p.dve("tensor_tensor_scan", out=Dv[:, blk * 64:(blk + 1) * 64],
                  data0=Bv[:, blk * 64:(blk + 1) * 64], data1=Bv[:, blk * 64:(blk + 1) * 64],
                  initial=0.0, op0=ALU.add, op1=ALU.bypass,
                  reads=[Kb], writes=[Kd], tag="cum2")
        p.dve("tensor_mul", out=Bv, in0=C, in1=Dv, reads=[Kc, Kd], writes=[Kb])
        for blk in range(1, 32):
            p.dve("scalar_tensor_tensor", out=Bv[:, blk * 64:(blk + 1) * 64],
                  in0=C[:, blk * 64:(blk + 1) * 64],
                  scalar=Bv[:, blk * 64 - 1:blk * 64],
                  in1=Bv[:, blk * 64:(blk + 1) * 64], op0=ALU.mult, op1=ALU.add,
                  reads=[Kc], sreads=[Kb], writes=[Kb], tag="interchunk")
        p.dve("tensor_copy", out=xchunk(c, "XLN", ch), in_=Bv,
              reads=[Kb], writes=[("XLN", ch)], tag="recout")
    p.fence()
    if upto == "scan":
        return
    for dch in range(NCHUNK):
        p.dma("pool", wb[:, dch * 1024:(dch + 1) * 1024],
              c.dram["l2a_wopT"][dch * 128:(dch + 1) * 128, :],
              writes=[("WBUF", dch)], stream="wl", tag="l2o_w")
    for tb in range(NTB):
        t0 = tb * TB
        for dch in range(NCHUNK):
            ps = c.pb[dch % 2]
            kps = ("PB", dch % 2)
            for ach in range(NCHUNK):
                p.pe_mm(ps[:], wb[:, ach * 1024 + dch * 128: ach * 1024 + dch * 128 + 128],
                        xchunk(c, "XLN", ach, t0, TB),
                        start=(ach == 0), stop=(ach == NCHUNK - 1),
                        reads=[("WBUF", ach), ("XLN", ach)], writes=[kps])
            xlt = hb[:, 0:TB]
            p.dma("sync", xlt, c.dram["l2a_xln"][dch * 128:(dch + 1) * 128, t0:t0 + TB],
                  reads=[(K, "xlnd", dch)], writes=[(K, "xlt")], stream="hin")
            p.dve("tensor_add", out=xchunk(c, "XRES", dch, t0, TB),
                  in0=xchunk(c, "XRES", dch, t0, TB), in1=ps[:],
                  reads=[kps, ("XRES", dch)], writes=[("XRES", dch)], tag="l2o_res")
            x1t = hb[:, TB:2 * TB]
            p.dve("tensor_add", out=x1t, in0=xlt, in1=ps[:],
                  reads=[kps, (K, "xlt")], writes=[(K, "x1t")])
            p.dma("sync", c.dram["l2a_x1"][dch * 128:(dch + 1) * 128, t0:t0 + TB], x1t,
                  reads=[(K, "x1t")], writes=[(K, "x1d", dch, tb)], stream="qout")
    p.fence()
    if upto == "oproj":
        return
    emit_rms_inplace(c, "l2a_rms2", "l2a_ffnw_col", eps=1e-6, src_dram=c.dram["l2a_x1"])
    p.fence()
    if upto == "rms2":
        return
    emit_swiglu_experts(c, "l2a_ffn", n_routed=0, NI=32, INTER=4096,
                        wg_r=None, wu_r=None, wd_r=None,
                        wg_s="l2a_wg", wu_s="l2a_wu", wd_s="l2a_wd",
                        h_dram=c.dram["l2a_h"])
    p.fence()






def host_prep_l2_ffn(inp):
    d = {}
    d["l2f_gwT"] = np.ascontiguousarray(inp["L2_ffn_router_w"])
    d["l2f_rb"] = np.ascontiguousarray(inp["L2_ffn_router_b"].reshape(1, 8))
    d["l2f_w1"] = inp["L2_ffn_w1"]
    d["l2f_w2"] = inp["L2_ffn_w2"]
    d["l2f_lin2"] = inp["L2_ffn_lin2_w"]
    d["l2f_l2b_col"] = np.ascontiguousarray(inp["L2_ffn_lin2_b"].reshape(8, 128).T)
    return d


def declare_l2_ffn(c: Ctx):
    c.dram_in("l2f_gwT", (D, 8), F32)
    c.dram_in("l2f_rb", (1, 8), F32)
    c.dram_in("l2f_w1", (8, D, 256), F32)
    c.dram_in("l2f_w2", (8, 256, 4096), F32)
    c.dram_in("l2f_lin2", (4096, D), F32)
    c.dram_in("l2f_l2b_col", (128, 8), F32)
    c.dram_tmp("l2f_h", (4096, S), BF16)


def emit_l2_ffn(c: Ctx):
    p = c.p
    wb = c.sb["WBUF"]
    hb = c.sb["HBUF"]
    K = ("l2f",)
    emit_ln(c, tag="l2f_ln")
    emit_moe_gate_top2(c, 8, "l2f_gwT", None, None, "l2f_gate",
                       renorm_eps=None, use_sigmoid=False, inner_ln=False,
                       bias_row="l2f_rb")
    grb = c.sb["GRB"]
    wrow = c.sb["WROW"]
    tstore = c.sb["VSTORE"]
    for tb in range(NTB):
        t0 = tb * TB
        for e in range(8):
            p.dma("sync", wrow[0:1, 0:TB], grb[e:e + 1, t0:t0 + TB],
                  reads=[("GRBDONE",)], writes=[("WROW",)], stream="wrow")
            p.pe_mm(c.pb[6][:, 0:TB], c.sb["ONESBR"][0:1, 0:128], wrow[0:1, 0:TB],
                    start=True, stop=True, reads=[("WROW",), ("ONES",)], writes=[("PB", 6)])
            p.dve("tensor_copy", out=hb[:, e * TB:(e + 1) * TB], in_=c.pb[6][:, 0:TB],
                  reads=[("PB", 6)], writes=[(K, "wbc", e)], tag="wbc")
        for e in range(8):
            sl = e % 2
            p.dma("pool", wb[:, sl * 2048:(sl + 1) * 2048].rearrange("p (c n) -> p c n", c=NCHUNK),
                  c.dram["l2f_w1"][e].rearrange("(c p) n -> p c n", p=128),
                  writes=[(K, "w1", sl)], stream="wl", tag=f"w1_{e}")
            w1v = wb[:, sl * 2048:(sl + 1) * 2048].rearrange("p (c n) -> p c n", c=NCHUNK)
            for rch in range(2):
                ps = c.pb[rch]
                kps = ("PB", rch)
                for dch in range(NCHUNK):
                    p.pe_mm(ps[:], w1v[:, dch, rch * 128:(rch + 1) * 128],
                            xchunk(c, "XLN", dch, t0, TB),
                            start=(dch == 0), stop=(dch == NCHUNK - 1),
                            reads=[(K, "w1", sl), ("XLN", dch)], writes=[kps])
                p.dve("tensor_mul", out=tstore[:, (e * 2 + rch) * TB:(e * 2 + rch + 1) * TB],
                      in0=ps[:], in1=hb[:, e * TB:(e + 1) * TB],
                      reads=[kps, (K, "wbc", e)], writes=[(K, "te", e, rch)], tag="teevac")
        for fch in range(32):
            sl = fch % 2
            p.dma("pool", wb[:, 4096 + sl * 2048: 4096 + sl * 2048 + 2048].rearrange(
                      "p (e rc n) -> p e rc n", e=8, rc=2),
                  c.dram["l2f_w2"][:, :, fch * 128:(fch + 1) * 128].rearrange(
                      "e (rc p) n -> p e rc n", p=128),
                  writes=[(K, "w2", sl)], stream="wl", tag=f"w2_{fch}")
            ps = c.pb[2 + sl]
            kps = ("PB", 2 + sl)
            nmm = 0
            for e in range(8):
                for rch in range(2):
                    p.pe_mm(ps[:], wb[:, 4096 + sl * 2048 + (e * 2 + rch) * 128: 4096 + sl * 2048 + (e * 2 + rch) * 128 + 128],
                            tstore[:, (e * 2 + rch) * TB:(e * 2 + rch + 1) * TB],
                            start=(nmm == 0), stop=(nmm == 15),
                            reads=[(K, "w2", sl), (K, "te", e, rch)], writes=[kps])
                    nmm += 1
            gl = hb[:, 4096 - TB:4096]
            p.act(gl, ps[:], AF.Gelu, reads=[kps], writes=[(K, "gl", sl), (K, "wbc", 7)])
            p.dma("sync", c.dram["l2f_h"][fch * 128:(fch + 1) * 128, t0:t0 + TB], gl,
                  reads=[(K, "gl", sl)], writes=[(K, "hd", fch, tb)], stream="hout")
    p.fence()
    bcol = c.sb["BCOL"]
    p.dma("sync", bcol[:, 8:16], c.dram["l2f_l2b_col"], writes=[(K, "l2b")], stream="bl")
    for tb in range(NTB):
        t0 = tb * TB
        for i in range(32):
            sl = i % 2
            p.dma("pool", wb[:, 4096 + sl * 1024:4096 + sl * 1024 + 1024],
                  c.dram["l2f_lin2"][i * 128:(i + 1) * 128, :],
                  writes=[(K, "wd", sl)], stream="wl")
            p.dma("sync", hb[:, sl * TB:(sl + 1) * TB],
                  c.dram["l2f_h"][i * 128:(i + 1) * 128, t0:t0 + TB],
                  reads=[(K, "hd", i, tb)], writes=[(K, "ht", sl)], stream="hin")
            for dch in range(NCHUNK):
                p.pe_mm(c.pb[dch][:], wb[:, 4096 + sl * 1024 + dch * 128:4096 + sl * 1024 + dch * 128 + 128],
                        hb[:, sl * TB:(sl + 1) * TB],
                        start=(i == 0), stop=(i == 31),
                        reads=[(K, "wd", sl), (K, "ht", sl)], writes=[("PB", dch)])
        for dch in range(NCHUNK):
            p.dve("scalar_tensor_tensor", out=xchunk(c, "XRES", dch, t0, TB),
                  in0=c.pb[dch][:], scalar=bcol[:, 8 + dch:9 + dch],
                  in1=xchunk(c, "XRES", dch, t0, TB), op0=ALU.add, op1=ALU.add,
                  reads=[("PB", dch), (K, "l2b"), ("XRES", dch)], writes=[("XRES", dch)],
                  tag="l2f_res")
    p.fence()



def host_prep_l3_attn(inp):
    d = {}
    d["l3a_wqT"] = np.ascontiguousarray(inp["L3_attn_Wq"])
    d["l3a_wqT"] = np.ascontiguousarray(inp["L3_attn_Wq"].T)
    d["l3a_wkT"] = np.ascontiguousarray(inp["L3_attn_Wk"].T)
    d["l3a_wvT"] = np.ascontiguousarray(inp["L3_attn_Wv"].T)
    d["l3a_bq_col"] = np.ascontiguousarray(inp["L3_attn_bq"].reshape(8, 128).T)
    d["l3a_bk_col"] = np.ascontiguousarray(inp["L3_attn_bk"].reshape(8, 128).T)
    d["l3a_bv_row"] = np.ascontiguousarray(inp["L3_attn_bv"].reshape(1, D))
    d["c_nodiag"] = (1.0 - np.eye(128)).astype(np.float32)
    return d


def declare_l3_attn(c: Ctx):
    for nm, shp in [("l3a_wqT", (D, D)), ("l3a_wkT", (D, D)), ("l3a_wvT", (D, D)),
                    ("l3a_bq_col", (128, 8)), ("l3a_bk_col", (128, 8)),
                    ("l3a_bv_row", (1, D)), ("c_nodiag", (128, 128))]:
        c.dram_in(nm, shp, F32)
    c.dram_tmp("l3a_qT", (D, S), BF16)
    c.dram_tmp("l3a_kT", (D, S), BF16)


def _proj_plain(c, wname, bcol_name, dst_dram, tag):
    p = c.p
    wb = c.sb["WBUF"]
    hb = c.sb["HBUF"]
    K = (tag,)
    for dch in range(NCHUNK):
        p.dma("pool", wb[:, dch * 1024:(dch + 1) * 1024],
              c.dram[wname][dch * 128:(dch + 1) * 128, :],
              writes=[("WBUF", dch)], stream="wl", tag=f"{tag}_w")
    bcol = c.sb["BCOL"]
    p.dma("sync", bcol[:, 0:8], c.dram[bcol_name], writes=[("BCOL",)], stream="bl")
    for och in range(NCHUNK):
        for tb in range(NTB):
            t0 = tb * TB
            ps = c.pb[tb % 2]
            kps = ("PB", tb % 2)
            for dch in range(NCHUNK):
                p.pe_mm(ps[:], wb[:, dch * 1024 + och * 128: dch * 1024 + och * 128 + 128],
                        xchunk(c, "XLN", dch, t0, TB),
                        start=(dch == 0), stop=(dch == NCHUNK - 1),
                        reads=[("WBUF", dch), ("XLN", dch)], writes=[kps])
            qb_ = hb[:, 0:TB]
            p.dve("tensor_scalar", out=qb_, in0=ps[:], scalar1=bcol[:, och:och + 1],
                  scalar2=None, op0=ALU.add,
                  reads=[kps, ("BCOL",)], writes=[(K, "qb")])
            p.dma("sync", dst_dram[och * 128:(och + 1) * 128, t0:t0 + TB], qb_,
                  reads=[(K, "qb")], writes=[(K, "od", och, tb)], stream="qout")
    return [(K, "od", och, tb) for och in range(NCHUNK) for tb in range(NTB)]


def emit_l3_attn(c: Ctx):
    p = c.p
    wb = c.sb["WBUF"]
    hb = c.sb["HBUF"]
    fb = c.sb["FBUF"]
    srow = c.sb["SROW"]
    K = ("l3a",)
    emit_ln(c, tag="l3a_ln")
    nodiag = c.sb["NODIAG"]
    p.dma("pool", nodiag[:], c.dram["c_nodiag"], writes=[(K, "nd")], stream="cl")
    qdone = _proj_plain(c, "l3a_wqT", "l3a_bq_col", c.dram["l3a_qT"], "l3q")
    kdone = _proj_plain(c, "l3a_wkT", "l3a_bk_col", c.dram["l3a_kT"], "l3k")
    for dch in range(NCHUNK):
        p.dma("pool", wb[:, dch * 1024:(dch + 1) * 1024],
              c.dram["l3a_wvT"][dch * 128:(dch + 1) * 128, :],
              writes=[("WBUF", dch)], stream="wl", tag="l3v_w")
    bvrow = c.sb["BROW"]
    p.dma("pool", bvrow[0:1, 0:D], c.dram["l3a_bv_row"], writes=[("BROW",)], stream="bl")
    vstore = c.sb["VSTORE"]
    VW = 1025
    p.dve("memset", vstore[:], 0.0, writes=[("VST", kt2) for kt2 in range(16)], tag="vclr")
    p.dve("memset", vstore[:, 1024::1025], 1.0,
          reads=[("VST", kt2) for kt2 in range(16)], writes=[("VONE",)])
    for kt in range(S // 128):
        tt0 = kt * 128
        for half in range(2):
            ps = c.pb[2 + half]
            kps = ("PB", 2 + half)
            for dch in range(NCHUNK):
                p.pe_mm(ps[:], xchunk(c, "XLN", dch, tt0, 128),
                        wb[:, dch * 1024 + half * 512: dch * 1024 + half * 512 + 512],
                        start=(dch == 0), stop=False,
                        reads=[("WBUF", dch), ("XLN", dch)], writes=[kps])
            p.pe_mm(ps[:], c.sb["ONESBR"][0:1, 0:128],
                    bvrow[0:1, half * 512:half * 512 + 512],
                    start=False, stop=True,
                    reads=[("BROW",), ("ONES",)], writes=[kps])
            p.dve("tensor_copy", out=vstore[:, kt * VW + half * 512: kt * VW + half * 512 + 512],
                  in_=ps[:], reads=[kps, ("VONE",)], writes=[("VST", kt)], tag="vevac")
    p.fence()
    for qb in range(NTB):
        t0 = qb * TB
        qtl = c.sb["KTILE"]
        for chq in range(NCHUNK):
            p.dma("sync", hb[:, chq * TB:(chq + 1) * TB],
                  c.dram["l3a_qT"][chq * 128:(chq + 1) * 128, t0:t0 + TB],
                  reads=qdone, writes=[(K, "q", chq)], stream="kld")
        lps = c.pb[6]
        for kt in range(S // 128):
            ps = c.pb[4 + (kt % 2)]
            kps = ("PB", 4 + (kt % 2))
            ktile = c.sb["KTILE"]
            kof = (kt % 2) * 1024
            for chk in range(NCHUNK):
                p.dma("sync", ktile[:, kof + chk * 128:kof + (chk + 1) * 128],
                      c.dram["l3a_kT"][chk * 128:(chk + 1) * 128, kt * 128:(kt + 1) * 128],
                      reads=kdone, writes=[(K, "k", kt % 2, chk)], stream="kld")
            for chk in range(NCHUNK):
                p.pe_mm(ps[:], ktile[:, kof + chk * 128:kof + (chk + 1) * 128],
                        hb[:, chk * TB:(chk + 1) * TB],
                        start=(chk == 0), stop=(chk == NCHUNK - 1),
                        reads=[(K, "k", kt % 2, chk), (K, "q", chk)],
                        writes=[kps], tag="l3qk")
            pt = wb[:, kt * TB:(kt + 1) * TB]
            kpt = (K, "P", kt)
            p.act(pt, ps[:], AF.Exp, scale=1.0 / 1024.0,
                  reads=[kps], writes=[kpt], tag="l3exp")
            qoff = kt * 128 - t0
            if 0 <= qoff < TB:
                p.pool("tensor_mul", out=wb[:, kt * TB + qoff: kt * TB + qoff + 128],
                       in0=wb[:, kt * TB + qoff: kt * TB + qoff + 128], in1=nodiag[:],
                       reads=[kpt, (K, "nd")], writes=[kpt], tag="l3mask")
            p.pe_mm(lps[0:1, 0:TB], c.sb["ONESB"][:, 0:1], pt,
                    start=(kt == 0), stop=(kt == 15),
                    reads=[kpt], writes=[("PB", 6)], tag="l3l")
        lrow = srow[0:1, 5 * TB:6 * TB]
        p.act(lrow, lps[0:1, 0:TB], AF.Ln, reads=[("PB", 6)], writes=[(K, "lr")])
        p.act(lrow, lrow, AF.Exp, scale=-1.0, reads=[(K, "lr")], writes=[(K, "lr2")])
        psn = c.pb[5]
        p.pe_mm(psn[:, 0:TB], c.sb["ONESR"][:], lrow,
                start=True, stop=True, reads=[(K, "lr2"), ("ONES",)], writes=[("PB", 5)])
        nbv = fb[:, 0:TB]
        p.dve("tensor_copy", out=nbv, in_=psn[:, 0:TB],
              reads=[("PB", 5)], writes=[(K, "nb")])
        for half in range(2):
            for dh in range(4):
                dch = half * 4 + dh
                pso = c.pb[dh]
                kpso = ("PB", dh)
                for kt in range(S // 128):
                    p.pe_mm(pso[:], vstore[:, kt * VW + dch * 128: kt * VW + dch * 128 + 128],
                            wb[:, kt * TB:(kt + 1) * TB],
                            start=(kt == 0), stop=(kt == 15),
                            reads=[("VST", kt), (K, "P", kt)], writes=[kpso], tag="l3pv")
            for dh in range(4):
                dch = half * 4 + dh
                tmp = fb[:, TB:2 * TB]
                p.dve("tensor_mul", out=tmp, in0=c.pb[dh][:], in1=nbv,
                      reads=[("PB", dh), (K, "nb")], writes=[(K, "tmp")])
                p.dve("tensor_add", out=xchunk(c, "XRES", dch, t0, TB),
                      in0=xchunk(c, "XRES", dch, t0, TB), in1=tmp,
                      reads=[(K, "tmp"), ("XRES", dch)], writes=[("XRES", dch)],
                      tag="l3a_res")
    p.fence()



def host_prep_l3_ffn(inp):
    d = {}
    d["l3f_gwT"] = np.ascontiguousarray(inp["L3_ffn_gate_w"].T)
    d["l3f_gb"] = np.ascontiguousarray(inp["L3_ffn_gate_b"].reshape(1, 8))
    for nm in ("w1", "w2", "w3"):
        d[f"l3f_{nm}"] = inp[f"L3_ffn_{nm}"]
    for nm in ("b1", "b2", "b3"):
        d[f"l3f_{nm}"] = np.ascontiguousarray(
            inp[f"L3_ffn_{nm}"].reshape(8, 8, 128).transpose(2, 0, 1).reshape(128, 64))
    return d


def declare_l3_ffn(c: Ctx):
    c.dram_in("l3f_gwT", (D, 8), F32)
    c.dram_in("l3f_gb", (1, 8), F32)
    for nm in ("w1", "w2", "w3"):
        c.dram_in(f"l3f_{nm}", (8, D, D), F32)
    for nm in ("b1", "b2", "b3"):
        c.dram_in(f"l3f_{nm}", (128, 64), F32)
    c.dram_tmp("l3f_h1", (D, S), BF16)
    c.dram_tmp("l3f_h2", (D, S), BF16)


def emit_l3_ffn(c: Ctx):
    p = c.p
    wb = c.sb["WBUF"]
    hb = c.sb["HBUF"]
    K = ("l3f",)
    emit_ln(c, tag="l3f_ln")
    emit_moe_gate_top2(c, 8, "l3f_gwT", None, None, "l3f_gate",
                       renorm_eps=None, use_sigmoid=False, inner_ln=False,
                       probs_first2=True, bias_row="l3f_gb")
    grb = c.sb["GRB"]
    wrow = c.sb["WROW"]
    bcol = c.sb["BCOL"]
    for nm in ("b1", "b2", "b3"):
        pass
    bc1 = c.sb["B3COL"]
    for i, nm in enumerate(("l3f_b1", "l3f_b2", "l3f_b3")):
        p.dma("sync", bc1[:, i * 64:(i + 1) * 64], c.dram[nm],
              writes=[(K, "bc", i)], stream="bl")
    for e in range(8):
        p.dma("sync", wrow[0:1, :], grb[e:e + 1, :],
              reads=[("GRBDONE",)], writes=[("WROW",)], stream="wrow")
        for tb in range(NTB):
            p.pe_mm(c.pb[6][:, 0:TB], c.sb["ONESBR"][0:1, 0:128],
                    wrow[0:1, tb * TB:(tb + 1) * TB],
                    start=True, stop=True,
                    reads=[("WROW",), ("ONES",)], writes=[("PB", 6)])
            p.dve("tensor_copy", out=c.sb["VSTORE"][:, tb * TB:(tb + 1) * TB],
                  in_=c.pb[6][:, 0:TB],
                  reads=[("PB", 6)], writes=[("WBC", tb)], tag="wbc")
        for stage in range(3):
            wnm = c.dram[f"l3f_w{stage + 1}"][e]
            src_name = "XLN" if stage == 0 else None
            src_dram = None if stage == 0 else c.dram[f"l3f_h{stage}"]
            dst_dram = c.dram[f"l3f_h{stage + 1}"] if stage < 2 else None
            for dch in range(NCHUNK):
                p.dma("pool", wb[:, dch * 1024:(dch + 1) * 1024],
                      wnm[dch * 128:(dch + 1) * 128, :],
                      writes=[("WBUF", dch)], stream="wl", tag=f"l3f_w{stage}")
            for tb in range(NTB):
                t0 = tb * TB
                if stage > 0:
                    for dch in range(NCHUNK):
                        p.dma("sync", hb[:, dch * TB:(dch + 1) * TB],
                              src_dram[dch * 128:(dch + 1) * 128, t0:t0 + TB],
                              reads=[(K, f"h{stage}", dch, tb)],
                              writes=[(K, "ht", dch)], stream="hin")
                for och in range(NCHUNK):
                    ps = c.pb[och % 2]
                    kps = ("PB", och % 2)
                    for dch in range(NCHUNK):
                        if stage == 0:
                            rhv = xchunk(c, "XLN", dch, t0, TB)
                            rrd = [("XLN", dch)]
                        else:
                            rhv = hb[:, dch * TB:(dch + 1) * TB]
                            rrd = [(K, "ht", dch)]
                        p.pe_mm(ps[:], wb[:, dch * 1024 + och * 128: dch * 1024 + och * 128 + 128],
                                rhv,
                                start=(dch == 0), stop=(dch == NCHUNK - 1),
                                reads=[("WBUF", dch)] + rrd, writes=[kps])
                    if stage < 2:
                        ov = c.sb["FBUF"][:, TB:2 * TB]
                        p.act(ov, ps[:], AF.Relu, bias=bc1[:, stage * 64 + e * 8 + och:stage * 64 + e * 8 + och + 1],
                              reads=[kps, (K, "bc", stage)], writes=[(K, "ov")])
                        p.dma("pool", dst_dram[och * 128:(och + 1) * 128, t0:t0 + TB], ov,
                              reads=[(K, "ov")],
                              writes=[(K, f"h{stage + 1}", och, tb)], stream="hout")
                    else:
                        tmp = c.sb["FBUF"][:, 0:TB]
                        p.dve("scalar_tensor_tensor", out=tmp, in0=ps[:],
                              scalar=bc1[:, 128 + e * 8 + och:129 + e * 8 + och],
                              in1=c.sb["VSTORE"][:, tb * TB:(tb + 1) * TB],
                              op0=ALU.add, op1=ALU.mult,
                              reads=[kps, (K, "bc", 2), ("WBC", tb)], writes=[(K, "tmp")])
                        p.dve("tensor_add", out=xchunk(c, "XRES", och, t0, TB),
                              in0=xchunk(c, "XRES", och, t0, TB), in1=tmp,
                              reads=[(K, "tmp"), ("XRES", och)], writes=[("XRES", och)],
                              tag="l3f_res")
    p.fence()



_LAYERS = [
    ("l0a", declare_l0_attn, emit_l0_attn, host_prep_l0_attn),
    ("l0f", declare_l0_ffn, emit_l0_ffn, host_prep_l0_ffn),
    ("l1a", declare_l1_attn, emit_l1_attn, None),
    ("l1f", declare_l1_ffn, emit_l1_ffn, host_prep_l1_ffn),
    ("l2a", declare_l2_attn, emit_l2_attn, host_prep_l2_attn),
    ("l2f", declare_l2_ffn, emit_l2_ffn, host_prep_l2_ffn),
    ("l3a", declare_l3_attn, emit_l3_attn, host_prep_l3_attn),
    ("l3f", declare_l3_ffn, emit_l3_ffn, host_prep_l3_ffn),
]


def _build_layer_launch(name, decl, emit, with_xln_out=False):
    nc = bass.Bass("TRN2", target_bir_lowering=False, debug=False)
    es = ExitStack()
    p = Prog(nc)
    c = Ctx(nc, es, p)
    x_in = c.dram_in("x_in", [D, S], F32)
    x_out = c.dram_out("x_out", [D, S], F32)
    xln_out = c.dram_out("xln_out", [D, S], BF16) if with_xln_out else None
    declare_common(c)
    decl(c)
    alloc_pools(c)
    emit_init(c)
    emit_load_x(c, x_in)
    emit(c)
    if with_xln_out:
        emit_ln(c, tag="post_ln")
        for ch in range(NCHUNK):
            p.dma("sync", xln_out[ch * 128:(ch + 1) * 128, :], xchunk(c, "XLN", ch),
                  reads=[("XLN", ch)], writes=[("XLNO", ch)], stream="xout")
    emit_store_x(c, x_out)
    if with_xln_out:
        p.op("pool", lambda e: e.nop(), reads=[("XLNO", ch) for ch in range(NCHUNK)])
    p.finalize_and_emit(es)
    es.close()
    return nc


def _build_groupA():
    nc = bass.Bass("TRN2", target_bir_lowering=False, debug=False)
    es = ExitStack()
    p = Prog(nc)
    c = Ctx(nc, es, p)
    x_in = c.dram_in("x_in", [D, S], F32)
    x_out = c.dram_out("x_out", [D, S], F32)
    xln_out = c.dram_out("xln_out", [D, S], BF16)
    declare_common(c)
    declare_l0_attn(c)
    declare_l0_ffn(c)
    alloc_pools(c)
    emit_init(c)
    emit_load_x(c, x_in)
    emit_l0_attn(c)
    emit_l0_ffn(c)
    emit_ln(c, tag="post_ln")
    for ch in range(NCHUNK):
        p.dma("sync", xln_out[ch * 128:(ch + 1) * 128, :], xchunk(c, "XLN", ch),
              reads=[("XLN", ch)], writes=[("XLNO", ch)], stream="xout")
    emit_store_x(c, x_out)
    p.op("pool", lambda e: e.nop(), reads=[("XLNO", ch) for ch in range(NCHUNK)])
    p.finalize_and_emit(es)
    es.close()
    return nc


def _build_groupB():
    nc = bass.Bass("TRN2", target_bir_lowering=False, debug=False)
    es = ExitStack()
    p = Prog(nc)
    c = Ctx(nc, es, p)
    x_in = c.dram_in("x_in", [D, S], F32)
    x_out = c.dram_out("x_out", [D, S], F32)
    declare_common(c)
    declare_l1_attn(c)
    declare_l1_ffn(c)
    declare_l2_attn(c)
    declare_l2_ffn(c)
    declare_l3_attn(c)
    declare_l3_ffn(c)
    alloc_pools(c)
    emit_init(c)
    emit_load_x(c, x_in)
    emit_l1_attn(c)
    emit_l1_ffn(c)
    emit_l2_attn(c)
    emit_l2_ffn(c)
    emit_l3_attn(c)
    emit_l3_ffn(c)
    emit_store_x(c, x_out)
    p.finalize_and_emit(es)
    es.close()
    return nc


def kernel(**inputs):
    import ml_dtypes
    x = np.asarray(inputs["L0_attn_x"])
    xs = [np.ascontiguousarray(x[b].T) for b in range(B)]
    common = host_common()
    prepsA = {}
    prepsA.update(host_prep_l0_attn(inputs))
    prepsA.update(host_prep_l0_ffn(inputs))
    ncA = _build_groupA()
    in_maps = []
    for b in range(B):
        m = {"x_in": xs[b]}
        m.update(common)
        m.update(prepsA)
        in_maps.append(m)
    resA = run_bass_kernel_spmd(ncA, in_maps, core_ids=list(range(B)))
    xs = [np.asarray(resA.results[b]["x_out"]) for b in range(B)]
    xg = np.stack([np.asarray(resA.results[b]["xln_out"]) for b in range(B)])
    xln_tok = np.ascontiguousarray(xg.transpose(0, 2, 1))
    if xln_tok.dtype == np.uint16:
        xln_tok = xln_tok.view(ml_dtypes.bfloat16)

    prepsB = {}
    prepsB.update(host_prep_l1_ffn(inputs))
    prepsB.update(host_prep_l2_attn(inputs))
    prepsB.update(host_prep_l2_ffn(inputs))
    prepsB.update(host_prep_l3_attn(inputs))
    prepsB.update(host_prep_l3_ffn(inputs))
    ncB = _build_groupB()
    in_maps = []
    for b in range(B):
        m = {"x_in": xs[b]}
        m.update(common)
        m.update(prepsB)
        order = [(b + i) % B for i in range(B)]
        m["l1a_xg"] = np.ascontiguousarray(xln_tok[order])
        in_maps.append(m)
    resB = run_bass_kernel_spmd(ncB, in_maps, core_ids=list(range(B)))
    xs = [np.asarray(resB.results[b]["x_out"]) for b in range(B)]
    out = np.stack([xs[b].T for b in range(B)]).astype(np.float32)
    return out


# version 05 (final)
# speedup vs baseline: 1.1910x; 1.1910x over previous
import math
import numpy as np
from contextlib import ExitStack

import concourse.bass as bass
import concourse.mybir as mybir
from concourse.bass_utils import run_bass_kernel_spmd

F32 = mybir.dt.float32
BF16 = mybir.dt.bfloat16
AF = mybir.ActivationFunctionType
ALU = mybir.AluOpType
AX = mybir.AxisListType

B, S, D = 8, 2048, 1024
NCHUNK = D // 128
TB = 512
NTB = S // TB
NEG = -60.0

ENGINES = ("pe", "act", "dve", "pool", "sync")


class _Op:
    __slots__ = ("eng", "emit", "deps", "opid", "inc", "ord", "dma", "stream", "snum", "tag")

    def __init__(self, eng, emit, deps, opid, dma=False, stream=None, snum=0, tag=""):
        self.eng, self.emit, self.deps, self.opid = eng, emit, deps, opid
        self.dma, self.stream, self.snum, self.tag = dma, stream, snum, tag
        self.inc = False
        self.ord = 0


class Prog:

    def __init__(self, nc):
        self.nc = nc
        self.ops = []
        self.writer = {}
        self.readers = {}
        self.fence_last = {}
        self.last_on = {}
        self.streams = {}

    def _dep_filter(self, opid, deps):
        me = self.ops[opid]
        out = set()
        for d in deps:
            if d is None or d == opid:
                continue
            o = self.ops[d]
            if (not o.dma) and o.eng == me.eng and not me.dma:
                continue
            if (not o.dma) and me.dma and o.eng == me.eng:
                continue
            out.add(d)
        return out

    def _bounce(self, eng, dep):
        beng = "pool" if eng != "pool" else "dve"
        bid = len(self.ops)
        b = _Op(beng, lambda e: e.nop(), {dep}, bid, tag="bounce")
        self.ops.append(b)
        self.last_on[beng] = bid
        return bid

    def op(self, eng, emit, reads=(), writes=(), sreads=(), tag=""):
        reads = tuple(reads) + tuple(sreads)
        deps = set()
        for r in reads:
            deps.add(self.writer.get(r))
        for r in writes:
            deps.add(self.writer.get(r))
            deps.update(self.readers.get(r, ()))
        deps.update(self.fence_last.values())
        deps.discard(None)
        same = [d for d in deps if (not self.ops[d].dma) and self.ops[d].eng == eng]
        if same:
            deps.difference_update(same)
            deps.add(self._bounce(eng, max(same)))
        opid = len(self.ops)
        o = _Op(eng, emit, set(), opid, tag=tag)
        self.ops.append(o)
        o.deps = deps
        for r in reads:
            self.readers.setdefault(r, []).append(opid)
        for r in writes:
            self.writer[r] = opid
            self.readers[r] = []
        self.last_on[eng] = opid
        return opid

    def dma(self, eng, out, in_, reads=(), writes=(), stream="d", tag="", **kw):
        stream = f"{eng}.{stream}"
        opid = len(self.ops)
        n = self.streams.get(stream, 0)
        self.streams[stream] = n + 1

        def emit(e, _out=out, _in=in_, _kw=kw):
            return e.dma_start(out=_out, in_=_in, **_kw)

        o = _Op(eng, emit, set(), opid, dma=True, stream=stream, snum=n + 1, tag=tag)
        self.ops.append(o)
        deps = set()
        for r in reads:
            deps.add(self.writer.get(r))
        for r in writes:
            deps.add(self.writer.get(r))
            deps.update(self.readers.get(r, ()))
        deps.update(self.fence_last.values())
        o.deps = self._dep_filter(opid, deps)
        for r in reads:
            self.readers.setdefault(r, []).append(opid)
        for r in writes:
            self.writer[r] = opid
            self.readers[r] = []
        self.last_on[eng] = opid
        self.last_on[("dma", stream)] = opid
        return opid

    def fence(self):
        for k, v in self.last_on.items():
            self.fence_last[k] = v

    def finalize_and_emit(self, es):
        nc = self.nc
        sems = {e: es.enter_context(nc.semaphore(f"sem_{e}")) for e in ENGINES}
        dsems = {s: es.enter_context(nc.semaphore(f"dsem_{s}")) for s in self.streams}
        for o in self.ops:
            for d in o.deps:
                self.ops[d].inc = True
        for o in self.ops:
            if o.dma:
                o.inc = True
        cnt = {e: 0 for e in ENGINES}
        for o in self.ops:
            if o.dma:
                o.ord = o.snum * 16
            else:
                if o.inc:
                    cnt[o.eng] += 1
                o.ord = cnt[o.eng]
        emitted_waits = 0
        block = es.enter_context(nc.Block())
        prog = self

        def section(eng_name):
            def body(e):
                floor = {}
                for o in prog.ops:
                    if o.eng != eng_name:
                        continue
                    waits = {}
                    for d in o.deps:
                        od = prog.ops[d]
                        if od.dma:
                            key = ("dma", od.stream)
                            val = od.snum * 16
                        else:
                            key = od.eng
                            val = od.ord
                        if val > waits.get(key, 0):
                            waits[key] = val
                    for key, val in sorted(waits.items(), key=str):
                        if floor.get(key, 0) >= val:
                            continue
                        floor[key] = val
                        sem = dsems[key[1]] if isinstance(key, tuple) else sems[key]
                        e.wait_ge(sem, val)
                    inst = o.emit(e)
                    if o.inc:
                        if o.dma:
                            inst.then_inc(dsems[o.stream], 16)
                        else:
                            inst.then_inc(sems[o.eng], 1)
            return body

        for eng_name in ENGINES:
            dec = {"pe": block.tensor, "act": block.scalar, "dve": block.vector,
                   "pool": block.gpsimd, "sync": block.sync}[eng_name]
            if any(o.eng == eng_name for o in self.ops):
                dec(section(eng_name))
        return sems, dsems

    def pe_mm(self, out, lhsT, rhs, start=True, stop=True, reads=(), writes=(), tp=None, tag=""):
        def emit(e):
            return e.matmul(out=out, lhsT=lhsT, rhs=rhs, start=start, stop=stop,
                            tile_position=tp)
        return self.op("pe", emit, reads=reads, writes=writes, tag=tag)

    def pe_transpose(self, out, in_, identity, reads=(), writes=(), tag=""):
        def emit(e):
            return e.transpose(out=out, in_=in_, identity=identity)
        return self.op("pe", emit, reads=reads, writes=writes, tag=tag)

    def act(self, out, in_, func, bias=0.0, scale=1.0, accum_out=None,
            reads=(), writes=(), tag=""):
        def emit(e):
            return e.activation(out=out, in_=in_, func=func, bias=bias, scale=scale,
                                accum_out=accum_out)
        return self.op("act", emit, reads=reads, writes=writes, tag=tag)

    def dve(self, name, *args, reads=(), writes=(), sreads=(), tag="", **kw):
        def emit(e):
            return getattr(e, name)(*args, **kw)
        return self.op("dve", emit, reads=reads, writes=writes, sreads=sreads, tag=tag)

    def pool(self, name, *args, reads=(), writes=(), sreads=(), tag="", **kw):
        def emit(e):
            return getattr(e, name)(*args, **kw)
        return self.op("pool", emit, reads=reads, writes=writes, sreads=sreads, tag=tag)



class Ctx:

    def __init__(self, nc, es, p):
        self.nc, self.es, self.p = nc, es, p
        self.sb = {}
        self.dram = {}
        self.uid = 0
        self.cvals = {}

    def sbuf(self, name, shape, dt):
        t = self.es.enter_context(self.nc.sbuf_tensor(name, shape, dt))
        self.sb[name] = t
        return t

    def psum(self, name, shape, dt=F32):
        t = self.es.enter_context(self.nc.psum_tensor(name, shape, dt))
        return t

    def dram_in(self, name, shape, dt=F32):
        ap = self.nc.dram_tensor(name, list(shape), dt, kind="ExternalInput").ap()
        self.dram[name] = ap
        return ap

    def dram_out(self, name, shape, dt=F32):
        ap = self.nc.dram_tensor(name, list(shape), dt, kind="ExternalOutput").ap()
        self.dram[name] = ap
        return ap

    def dram_tmp(self, name, shape, dt=F32):
        ap = self.nc.dram_tensor(name, list(shape), dt).ap()
        self.dram[name] = ap
        return ap

    def uniq(self, base):
        self.uid += 1
        return f"{base}{self.uid}"

    def constf(self, *vals):
        for v in vals:
            v = float(v)
            if v in self.cvals:
                continue
            n = len(self.cvals)
            ap = self.sb["CVAL"][0:1, n:n + 1]
            self.p.dve("memset", ap, v, writes=[("CVAL",)])
            self.nc.const_aps.register(F32, v, ap)
            self.cvals[v] = ap


def alloc_pools(c: Ctx):
    c.sbuf("XRES", [128, NCHUNK * S], F32)
    c.sbuf("XLN", [128, NCHUNK * S], BF16)
    c.sbuf("WBUF", [128, 8192], BF16)
    c.sbuf("HBUF", [128, 4096], BF16)
    c.sbuf("FBUF", [128, 1536], F32)
    c.sbuf("SROW", [1, 3072], F32)
    c.sbuf("CONST", [128, 4096], BF16)
    c.sbuf("CVAL", [1, 64], F32)
    c.sbuf("BCOL", [128, 16], F32)
    c.sbuf("BROW", [1, 1024], BF16)
    c.sbuf("TRI", [128, 128], BF16)
    c.sbuf("VSTORE", [128, 16 * 1040], BF16)
    c.sbuf("KTILE", [128, S], BF16)
    c.sbuf("NODIAG", [128, 128], BF16)
    c.sbuf("QTILE", [64, TB], BF16)
    c.sbuf("ONESBR", [1, 128], BF16)
    c.sbuf("ONES", [128, 16], F32)
    c.sbuf("ONESR", [1, 128], F32)
    c.sbuf("ONESB", [128, 16], BF16)
    c.sbuf("IDN", [128, 128], F32)
    c.sbuf("GRB", [8, S], BF16)
    c.sbuf("WROW", [1, S], BF16)
    c.sbuf("WTOK", [128, 16], F32)
    c.sbuf("CONVW", [128, 32], F32)
    c.sbuf("B3COL", [128, 192], F32)
    c.pb = [c.psum(f"pb{i}", [128, 512], F32) for i in range(8)]
    c.pbT = c.pb[7][:].bitcast(BF16)


def rview(t, c0, n):
    return t[:, c0:c0 + n]


def xchunk(c, name, ch, t0=0, nt=S):
    return c.sb[name][:, ch * S + t0: ch * S + t0 + nt]


def declare_common(c: Ctx):
    c.dram_in("c_ident", (128, 128), F32)
    c.dram_in("c_tri", (128, 128), F32)


def host_common():
    kk, qq = np.meshgrid(np.arange(128), np.arange(128), indexing="ij")
    return {"c_ident": np.eye(128, dtype=np.float32),
            "c_tri": (kk <= qq).astype(np.float32)}


def emit_init(c: Ctx):
    p = c.p
    p.dma("pool", c.sb["TRI"][:], c.dram["c_tri"], writes=[("TRI",)], stream="cl")
    p.dma("sync", c.sb["IDN"][:], c.dram["c_ident"], writes=[("IDN",)], stream="cl")
    p.dve("memset", c.sb["ONES"][:], 1.0, writes=[("ONES",)])
    p.dve("memset", c.sb["ONESR"][:], 1.0, writes=[("ONES",)])
    p.dve("memset", c.sb["ONESB"][:], 1.0, writes=[("ONESB",)])
    p.dve("memset", c.sb["ONESBR"][:], 1.0, writes=[("ONES",)])


def emit_load_x(c: Ctx, x_in):
    p = c.p
    for ch in range(NCHUNK):
        p.dma("sync", xchunk(c, "XRES", ch), x_in[ch * 128:(ch + 1) * 128, :],
              writes=[("XRES", ch)], stream="xin")


def emit_store_x(c: Ctx, x_out):
    p = c.p
    for ch in range(NCHUNK):
        p.dma("sync", x_out[ch * 128:(ch + 1) * 128, :], xchunk(c, "XRES", ch),
              reads=[("XRES", ch)], writes=[("XOUT", ch)], stream="xout")
    p.op("pool", lambda e: e.nop(), reads=[("XOUT", ch) for ch in range(NCHUNK)])


def res_keys(c, name):

    return [(name, ch) for ch in range(NCHUNK)]


def emit_ln(c: Ctx, dst="XLN", eps=1e-5, tag="ln", src_name="XRES",
            banks=(0, 1, 2, 3)):
    p = c.p
    hb = c.sb["HBUF"]
    srow = c.sb["SROW"]
    onesb = c.sb["ONESB"]
    ones1 = c.sb["ONESR"]
    b_sx, b_sq, b_a, b_b = (c.pb[i] for i in banks)
    kb = [("PB", i) for i in banks]
    K = (tag,)
    for tb in range(NTB):
        t0 = tb * TB
        for ch in range(NCHUNK):
            xv = xchunk(c, src_name, ch, t0, TB)
            xbv = hb[:, 512:512 + TB]
            sqv = hb[:, 0:TB]
            p.dve("tensor_copy", out=xbv, in_=xv,
                  reads=[(src_name, ch)], writes=[(K, "xb")], tag=f"{tag}_cast")
            p.act(sqv, xv, AF.Square,
                  reads=[(src_name, ch)], writes=[(K, "sq")], tag=f"{tag}_sq")
            p.pe_mm(b_sx[0:1, 0:TB], onesb[:, 0:1], xbv,
                    start=(ch == 0), stop=(ch == NCHUNK - 1),
                    reads=[(K, "xb")], writes=[kb[0]])
            p.pe_mm(b_sq[0:1, 0:TB], onesb[:, 0:1], sqv,
                    start=(ch == 0), stop=(ch == NCHUNK - 1),
                    reads=[(K, "sq")], writes=[kb[1]])
        mean = srow[0:1, 0:TB]
        nm = srow[0:1, TB:2 * TB]
        t_ = srow[0:1, 2 * TB:3 * TB]
        var = srow[0:1, 3 * TB:4 * TB]
        rstd = srow[0:1, 4 * TB:5 * TB]
        p.dve("tensor_scalar_mul", out=mean, in0=b_sx[0:1, 0:TB], scalar1=1.0 / D,
              reads=[kb[0]], writes=[(K, "mean")])
        p.dve("tensor_scalar_mul", out=nm, in0=mean, scalar1=-1.0,
              reads=[(K, "mean")], writes=[(K, "nm")])
        p.dve("tensor_mul", out=t_, in0=nm, in1=mean,
              reads=[(K, "mean"), (K, "nm")], writes=[(K, "t")])
        p.dve("scalar_tensor_tensor", out=var, in0=b_sq[0:1, 0:TB], scalar=1.0 / D,
              in1=t_, op0=ALU.mult, op1=ALU.add,
              reads=[kb[1], (K, "t")], writes=[(K, "var")])
        c.constf(eps)
        p.act(var, var, AF.Ln, bias=eps,
              reads=[(K, "var"), ("CVAL",)], writes=[(K, "lnv")])
        p.act(rstd, var, AF.Exp, scale=-0.5,
              reads=[(K, "lnv")], writes=[(K, "rstd")])
        p.pe_mm(b_a[:, 0:TB], ones1[:], rstd, start=True, stop=True,
                reads=[(K, "rstd"), ("ONES",)], writes=[kb[2]])
        p.pe_mm(b_b[:, 0:TB], ones1[:], nm, start=True, stop=True,
                reads=[(K, "nm"), ("ONES",)], writes=[kb[3]])
        fb = c.sb["FBUF"]
        for ch in range(NCHUNK):
            xv = xchunk(c, src_name, ch, t0, TB)
            tv = fb[:, 0:TB]
            p.dve("tensor_add", out=tv, in0=xv, in1=b_b[:, 0:TB],
                  reads=[(src_name, ch), kb[3]], writes=[(K, "tv")])
            p.dve("tensor_mul", out=xchunk(c, dst, ch, t0, TB), in0=tv, in1=b_a[:, 0:TB],
                  reads=[(K, "tv"), kb[2]], writes=[(dst, ch)])



def host_prep_l0_attn(inp):
    d = {}
    for nm in ("Q", "K", "V"):
        w = inp[f"L0_attn_W_{nm}"]
        d[f"l0a_w{nm.lower()}"] = np.ascontiguousarray(
            np.transpose(w, (1, 0, 2)).reshape(D, D))
        b = inp[f"L0_attn_b_{nm}"]
        if nm == "V":
            d["l0a_bv_row"] = np.ascontiguousarray(b.reshape(1, D))
        else:
            d[f"l0a_b{nm.lower()}_col"] = np.ascontiguousarray(
                b.reshape(D)[None, :].reshape(8, 128).T)
    d["l0a_wo"] = np.ascontiguousarray(inp["L0_attn_W_O"].reshape(D, D))
    d["l0a_bo_col"] = np.ascontiguousarray(inp["L0_attn_b_O"].reshape(8, 128).T)
    pos = np.arange(S, dtype=np.float64)
    dim = np.arange(32, dtype=np.float64)
    freq = 10000.0 ** (dim / 32.0)
    freq = np.concatenate([freq, freq])
    ang = pos[:, None] / freq[None, :]
    cos = np.cos(ang).astype(np.float32).T
    sin = np.sin(ang).astype(np.float32).T
    sgn = np.where(np.arange(64) < 32, -1.0, 1.0).astype(np.float32)[:, None]
    d["l0a_cos"] = np.ascontiguousarray(np.tile(cos, (2, 1)))
    d["l0a_sin"] = np.ascontiguousarray(np.tile(sin * sgn, (2, 1)))
    return d


def declare_l0_attn(c: Ctx):
    pass_ = None
    for nm, shp in [("l0a_wq", (D, D)), ("l0a_wk", (D, D)), ("l0a_wv", (D, D)),
                    ("l0a_wo", (D, D)), ("l0a_bq_col", (128, 8)), ("l0a_bk_col", (128, 8)),
                    ("l0a_bv_row", (1, D)), ("l0a_bo_col", (128, 8)),
                    ("l0a_cos", (128, S)), ("l0a_sin", (128, S))]:
        c.dram_in(nm, shp, F32)
    c.dram_tmp("l0a_qT", (D, S), BF16)
    c.dram_tmp("l0a_kT", (D, S), BF16)


def _proj_rope(c, wname, bcol_name, dst_dram, tag):
    p = c.p
    wb = c.sb["WBUF"]
    fb = c.sb["FBUF"]
    hb = c.sb["HBUF"]
    cosT = c.sb["CONST"]
    K = (tag,)
    for dch in range(NCHUNK):
        p.dma("pool", wb[:, dch * 1024:(dch + 1) * 1024],
              c.dram[wname][dch * 128:(dch + 1) * 128, :],
              writes=[("WBUF", dch)], stream="wl", tag=f"{tag}_w")
    bcol = c.sb["BCOL"]
    p.dma("sync", bcol[:, 0:8], c.dram[bcol_name], writes=[("BCOL",)], stream="bl")
    for och in range(NCHUNK):
        for tb in range(NTB):
            t0 = tb * TB
            ps = c.pb[tb % 2]
            kps = ("PB", tb % 2)
            for dch in range(NCHUNK):
                p.pe_mm(ps[:], wb[:, dch * 1024 + och * 128: dch * 1024 + och * 128 + 128],
                        xchunk(c, "XLN", dch, t0, TB),
                        start=(dch == 0), stop=(dch == NCHUNK - 1),
                        reads=[("WBUF", dch), ("XLN", dch)], writes=[kps])
            qf = fb[:, 0:TB]
            qp = fb[:, TB:2 * TB]
            t1 = fb[:, 2 * TB:3 * TB]
            p.dve("tensor_scalar_add", out=qf, in0=ps[:], scalar1=bcol[:, och:och + 1],
                  reads=[kps, ("BCOL",)], writes=[(K, "qf")])
            for blk in range(4):
                srcr = (blk ^ 1) * 32
                p.dma("sync", qp[blk * 32:blk * 32 + 32, :], qf[srcr:srcr + 32, :],
                      reads=[(K, "qf")], writes=[(K, "qp", blk)], stream="shuf")
            p.dve("tensor_mul", out=t1, in0=qf, in1=cosT[:, t0:t0 + TB],
                  reads=[(K, "qf"), ("CONSTC",)], writes=[(K, "t1")])
            qb_ = hb[:, 0:TB]
            p.dve("tensor_mul", out=qp, in0=qp, in1=cosT[:, S + t0:S + t0 + TB],
                  reads=[(K, "qp", 0), (K, "qp", 1), (K, "qp", 2), (K, "qp", 3), ("CONSTC",)],
                  writes=[(K, "qp2")])
            p.dve("tensor_add", out=qb_, in0=t1, in1=qp,
                  reads=[(K, "t1"), (K, "qp2")], writes=[(K, "qb")])
            p.dma("sync", dst_dram[och * 128:(och + 1) * 128, t0:t0 + TB], qb_,
                  reads=[(K, "qb")], writes=[(K, "od", och, tb)], stream="qout")
    return [(K, "od", och, tb) for och in range(NCHUNK) for tb in range(NTB)]


def emit_l0_attn(c: Ctx):
    p = c.p
    wb = c.sb["WBUF"]
    hb = c.sb["HBUF"]
    fb = c.sb["FBUF"]
    srow = c.sb["SROW"]
    NH, HD = 16, 64
    VW = 65 * NH
    emit_ln(c, tag="l0a_ln")
    p.dma("pool", c.sb["CONST"][:, 0:S], c.dram["l0a_cos"], writes=[("CONSTC",)], stream="cl")
    p.dma("pool", c.sb["CONST"][:, S:2 * S], c.dram["l0a_sin"], writes=[("CONSTC",)], stream="cl")
    qdone = _proj_rope(c, "l0a_wq", "l0a_bq_col", c.dram["l0a_qT"], "l0q")
    kdone = _proj_rope(c, "l0a_wk", "l0a_bk_col", c.dram["l0a_kT"], "l0k")
    for dch in range(NCHUNK):
        p.dma("pool", wb[:, dch * 1024:(dch + 1) * 1024],
              c.dram["l0a_wv"][dch * 128:(dch + 1) * 128, :],
              writes=[("WBUF", dch)], stream="wl", tag="l0v_w")
    bvrow = c.sb["BROW"]
    p.dma("pool", bvrow[0:1, 0:D], c.dram["l0a_bv_row"], writes=[("BROW",)], stream="bl")
    vstore = c.sb["VSTORE"]
    p.dve("memset", vstore[:], 0.0, writes=[("VST", kt2) for kt2 in range(16)])
    p.dve("memset", vstore[:, 64::65], 1.0,
          reads=[("VST", kt2) for kt2 in range(16)],
          writes=[("VONE",)])
    for kt in range(S // 128):
        tt0 = kt * 128
        for half in range(2):
            ps = c.pb[2 + half]
            kps = ("PB", 2 + half)
            for dch in range(NCHUNK):
                p.pe_mm(ps[:], xchunk(c, "XLN", dch, tt0, 128),
                        wb[:, dch * 1024 + half * 512: dch * 1024 + half * 512 + 512],
                        start=(dch == 0), stop=False,
                        reads=[("WBUF", dch), ("XLN", dch)], writes=[kps])
            p.pe_mm(ps[:], c.sb["ONESBR"][0:1, 0:128],
                    bvrow[0:1, half * 512:half * 512 + 512],
                    start=False, stop=True,
                    reads=[("BROW",), ("ONES",)], writes=[kps])
            dstv = vstore[:, kt * VW + half * 8 * 65: kt * VW + half * 8 * 65 + 8 * 65]
            dst3 = dstv.rearrange("p (h w) -> p h w", h=8)[:, :, 0:64]
            src3 = ps[:].rearrange("p (h w) -> p h w", h=8)
            p.dve("tensor_copy", out=dst3, in_=src3,
                  reads=[kps, ("VONE",)], writes=[("VST", kt)], tag="vevac")
    p.fence()
    emit_flash16(c, "l0fl", c.dram["l0a_qT"], c.dram["l0a_kT"], qdone, kdone)
    emit_outproj_residual(c, "l0o", c.dram["l0a_wo"], c.dram["l0a_bo_col"])
    p.fence()



def host_prep_l0_ffn(inp):
    d = {}
    d["l0f_scol"] = np.ascontiguousarray(inp["L0_ffn_ln_scale"].reshape(8, 128).T)
    d["l0f_bcol"] = np.ascontiguousarray(inp["L0_ffn_ln_bias"].reshape(8, 128).T)
    d["l0f_gwT"] = np.ascontiguousarray(inp["L0_ffn_gate_w"].T)
    d["l0f_Wg"] = inp["L0_ffn_Wg"]
    d["l0f_Wu"] = inp["L0_ffn_Wu"]
    d["l0f_Wd"] = inp["L0_ffn_Wd"]
    d["l0f_sWg"] = inp["L0_ffn_sWg"]
    d["l0f_sWu"] = inp["L0_ffn_sWu"]
    d["l0f_sWd"] = inp["L0_ffn_sWd"]
    ident = np.eye(128, dtype=np.float32)
    d["c_ident"] = ident
    return d


def declare_l0_ffn(c: Ctx):
    c.dram_in("l0f_scol", (128, 8), F32)
    c.dram_in("l0f_bcol", (128, 8), F32)
    c.dram_in("l0f_gwT", (D, 8), F32)
    c.dram_in("l0f_Wg", (8, D, 2048), F32)
    c.dram_in("l0f_Wu", (8, D, 2048), F32)
    c.dram_in("l0f_Wd", (8, 2048, D), F32)
    c.dram_in("l0f_sWg", (D, 2048), F32)
    c.dram_in("l0f_sWu", (D, 2048), F32)
    c.dram_in("l0f_sWd", (2048, D), F32)
    c.dram_tmp("l0f_h", (2048, S), BF16)


def emit_moe_gate_top2(c: Ctx, nexp, gwT_name, scol_name, bcol_name, tag,
                       renorm_eps=None, use_sigmoid=False, inner_ln=True,
                       probs_first2=False, bias_row=None):
    p = c.p
    p.fence()
    hb = c.sb["HBUF"]
    fb = c.sb["FBUF"]
    wb = c.sb["WBUF"]
    srow = c.sb["SROW"]
    K = (tag,)
    gwt = c.sb["CONST"]
    p.dma("pool", gwt[:, 0:NCHUNK * nexp].rearrange("p (c e) -> p c e", c=NCHUNK),
          c.dram[gwT_name].rearrange("(c p) e -> p c e", p=128),
          writes=[(K, "gwt")], stream="wl")
    idn = c.sb["IDN"]
    if bias_row is not None:
        p.dma("pool", c.sb["BROW"][0:1, 0:nexp], c.dram[bias_row],
              writes=[(K, "brow")], stream="bl")
    if inner_ln:
        scol = c.sb["BCOL"]
        p.dma("sync", scol[:, 0:8], c.dram[scol_name], writes=[(K, "scol")], stream="bl")
        p.dma("sync", scol[:, 8:16], c.dram[bcol_name], writes=[(K, "bcol")], stream="bl")
    onesb = c.sb["ONESB"]
    pT = c.pbT
    grb = c.sb["GRB"]
    for tb in range(NTB):
        t0 = tb * TB
        if inner_ln:
            for ch in range(NCHUNK):
                xv = xchunk(c, "XLN", ch, t0, TB)
                sqv = hb[:, 0:TB]
                p.act(sqv, xv, AF.Square, reads=[("XLN", ch)], writes=[(K, "sq")])
                p.pe_mm(c.pb[0][0:1, 0:TB], onesb[:, 0:1], xv,
                        start=(ch == 0), stop=(ch == NCHUNK - 1),
                        reads=[("XLN", ch)], writes=[("PB", 0)])
                p.pe_mm(c.pb[1][0:1, 0:TB], onesb[:, 0:1], sqv,
                        start=(ch == 0), stop=(ch == NCHUNK - 1),
                        reads=[(K, "sq")], writes=[("PB", 1)])
            mean = srow[0:1, 0:TB]
            nm = srow[0:1, TB:2 * TB]
            t_ = srow[0:1, 2 * TB:3 * TB]
            var = srow[0:1, 3 * TB:4 * TB]
            rstd = srow[0:1, 4 * TB:5 * TB]
            p.dve("tensor_scalar_mul", out=mean, in0=c.pb[0][0:1, 0:TB], scalar1=1.0 / D,
                  reads=[("PB", 0)], writes=[(K, "mean")])
            p.dve("tensor_scalar_mul", out=nm, in0=mean, scalar1=-1.0,
                  reads=[(K, "mean")], writes=[(K, "nm")])
            p.dve("tensor_mul", out=t_, in0=nm, in1=mean,
                  reads=[(K, "mean"), (K, "nm")], writes=[(K, "t")])
            p.dve("scalar_tensor_tensor", out=var, in0=c.pb[1][0:1, 0:TB], scalar=1.0 / D,
                  in1=t_, op0=ALU.mult, op1=ALU.add,
                  reads=[("PB", 1), (K, "t")], writes=[(K, "var")])
            c.constf(1e-5)
            p.act(var, var, AF.Ln, bias=1e-5, reads=[(K, "var"), ("CVAL",)], writes=[(K, "lnv")])
            p.act(rstd, var, AF.Exp, scale=-0.5, reads=[(K, "lnv")], writes=[(K, "rstd")])
            p.pe_mm(c.pb[2][:, 0:TB], c.sb["ONESR"][:], rstd, start=True, stop=True,
                    reads=[(K, "rstd"), ("ONES",)], writes=[("PB", 2)])
            p.pe_mm(c.pb[3][:, 0:TB], c.sb["ONESR"][:], nm, start=True, stop=True,
                    reads=[(K, "nm"), ("ONES",)], writes=[("PB", 3)])
            for ch in range(NCHUNK):
                xv = xchunk(c, "XLN", ch, t0, TB)
                tv = fb[:, 0:TB]
                p.dve("tensor_add", out=tv, in0=xv, in1=c.pb[3][:, 0:TB],
                      reads=[("XLN", ch), ("PB", 3)], writes=[(K, "tv")])
                p.dve("tensor_mul", out=tv, in0=tv, in1=c.pb[2][:, 0:TB],
                      reads=[(K, "tv"), ("PB", 2)], writes=[(K, "tv2")])
                p.dve("tensor_single_scalar", out=tv, in_=tv, scalar=scol[:, ch:ch + 1],
                      op=ALU.mult, reads=[(K, "tv2"), (K, "scol")], writes=[(K, "tv3")])
                p.dve("tensor_scalar", out=hb[:, ch * TB:ch * TB + TB], in0=tv,
                      scalar1=scol[:, 8 + ch:9 + ch], scalar2=None, op0=ALU.add,
                      reads=[(K, "tv3"), (K, "bcol")], writes=[(K, "xn", ch)])
            xn_src = lambda ch: hb[:, ch * TB:ch * TB + TB]
            xn_key = lambda ch: (K, "xn", ch)
        else:
            xn_src = lambda ch: xchunk(c, "XLN", ch, t0, TB)
            xn_key = lambda ch: ("XLN", ch)
        for tt in range(TB // 128):
            ps_l = c.pb[4][:, 0:nexp]
            for ch in range(NCHUNK):
                p.pe_mm(ps_l, xn_src(ch)[:, tt * 128:(tt + 1) * 128],
                        gwt[:, ch * nexp:(ch + 1) * nexp],
                        start=(ch == 0),
                        stop=(ch == NCHUNK - 1 and bias_row is None),
                        reads=[xn_key(ch), (K, "gwt")], writes=[("PB", 4)])
            if bias_row is not None:
                p.pe_mm(ps_l, c.sb["ONESBR"][0:1, 0:128], c.sb["BROW"][0:1, 0:nexp],
                        start=False, stop=True,
                        reads=[(K, "brow"), ("ONES",)], writes=[("PB", 4)])
            B0 = 512
            L = fb[:, B0 + 0:B0 + nexp]
            E = fb[:, B0 + 16:B0 + 16 + nexp]
            em = fb[:, B0 + 32:B0 + 32 + nexp]
            e2 = fb[:, B0 + 48:B0 + 48 + nexp]
            m1 = fb[:, B0 + 96:B0 + 97]
            m2 = fb[:, B0 + 97:B0 + 98]
            ssum = fb[:, B0 + 98:B0 + 99]
            sinv = fb[:, B0 + 99:B0 + 100]
            p1 = fb[:, B0 + 100:B0 + 101]
            p2 = fb[:, B0 + 101:B0 + 102]
            q = fb[:, B0 + 102:B0 + 103]
            w1 = fb[:, B0 + 103:B0 + 104]
            w2 = fb[:, B0 + 104:B0 + 105]
            mk1 = fb[:, B0 + 112:B0 + 112 + nexp]
            mk2 = fb[:, B0 + 128:B0 + 128 + nexp]
            p.dve("tensor_copy", out=L, in_=ps_l, reads=[("PB", 4)], writes=[(K, "L")])
            if use_sigmoid:
                p.act(E, L, AF.Sigmoid, reads=[(K, "L")], writes=[(K, "E")])
            else:
                p.act(E, L, AF.Exp, reads=[(K, "L")], writes=[(K, "E")])
            p.dve("reduce_max", out=m1, in_=E, axis=AX.X, reads=[(K, "E")], writes=[(K, "m1")])
            p.dve("tensor_tensor", out=mk1, in0=E, in1=m1.broadcast_to([128, nexp]),
                  op=ALU.is_ge, reads=[(K, "E")], sreads=[(K, "m1")], writes=[(K, "mk1")])
            p.dve("tensor_mul", out=em, in0=E, in1=mk1,
                  reads=[(K, "E"), (K, "mk1")], writes=[(K, "em")])
            p.dve("tensor_sub", out=e2, in0=E, in1=em,
                  reads=[(K, "E"), (K, "em")], writes=[(K, "e2")])
            p.dve("reduce_max", out=m2, in_=e2, axis=AX.X, reads=[(K, "e2")], writes=[(K, "m2")])
            p.dve("tensor_tensor", out=mk2, in0=e2, in1=m2.broadcast_to([128, nexp]),
                  op=ALU.is_ge, reads=[(K, "e2")], sreads=[(K, "m2")], writes=[(K, "mk2")])
            if probs_first2:
                p.dve("reduce_sum", out=ssum, in_=E, axis=AX.X, reads=[(K, "E")], writes=[(K, "ss")])
                p.dve("reciprocal", out=sinv, in_=ssum, sreads=[(K, "ss")], writes=[(K, "si")])
                p.dve("tensor_mul", out=w1, in0=E[:, 0:1], in1=sinv,
                      reads=[(K, "E")], sreads=[(K, "si")], writes=[(K, "w1")])
                p.dve("tensor_mul", out=w2, in0=E[:, 1:2], in1=sinv,
                      reads=[(K, "E")], sreads=[(K, "si")], writes=[(K, "w2")])
            elif renorm_eps is not None:
                if use_sigmoid:
                    p.dve("tensor_add", out=q, in0=m1, in1=m2,
                          sreads=[(K, "m1"), (K, "m2")], writes=[(K, "q0")])
                    if renorm_eps:
                        p.dve("tensor_scalar_add", out=q, in0=q, scalar1=renorm_eps,
                              sreads=[(K, "q0")], writes=[(K, "q1")])
                    p.dve("reciprocal", out=q, in_=q, sreads=[(K, "q1") if renorm_eps else (K, "q0")], writes=[(K, "q")])
                    p.dve("tensor_mul", out=w1, in0=m1, in1=q,
                          sreads=[(K, "m1"), (K, "q")], writes=[(K, "w1")])
                    p.dve("tensor_mul", out=w2, in0=m2, in1=q,
                          sreads=[(K, "m2"), (K, "q")], writes=[(K, "w2")])
                else:
                    p.dve("reduce_sum", out=ssum, in_=E, axis=AX.X, reads=[(K, "E")], writes=[(K, "ss")])
                    p.dve("reciprocal", out=sinv, in_=ssum, sreads=[(K, "ss")], writes=[(K, "si")])
                    p.dve("tensor_mul", out=p1, in0=m1, in1=sinv,
                          sreads=[(K, "m1"), (K, "si")], writes=[(K, "p1")])
                    p.dve("tensor_mul", out=p2, in0=m2, in1=sinv,
                          sreads=[(K, "m2"), (K, "si")], writes=[(K, "p2")])
                    p.dve("tensor_add", out=q, in0=p1, in1=p2,
                          sreads=[(K, "p1"), (K, "p2")], writes=[(K, "q0")])
                    p.dve("tensor_scalar_add", out=q, in0=q, scalar1=renorm_eps,
                          sreads=[(K, "q0")], writes=[(K, "q1")])
                    p.dve("reciprocal", out=q, in_=q, sreads=[(K, "q1")], writes=[(K, "q")])
                    p.dve("tensor_mul", out=w1, in0=p1, in1=q,
                          sreads=[(K, "p1"), (K, "q")], writes=[(K, "w1")])
                    p.dve("tensor_mul", out=w2, in0=p2, in1=q,
                          sreads=[(K, "p2"), (K, "q")], writes=[(K, "w2")])
            else:
                p.dve("reduce_sum", out=ssum, in_=E, axis=AX.X, reads=[(K, "E")], writes=[(K, "ss")])
                p.dve("reciprocal", out=sinv, in_=ssum, sreads=[(K, "ss")], writes=[(K, "si")])
                p.dve("tensor_mul", out=w1, in0=m1, in1=sinv,
                      sreads=[(K, "m1"), (K, "si")], writes=[(K, "w1")])
                p.dve("tensor_mul", out=w2, in0=m2, in1=sinv,
                      sreads=[(K, "m2"), (K, "si")], writes=[(K, "w2")])
            p.dve("tensor_tensor", out=mk1, in0=mk1, in1=w1.broadcast_to([128, nexp]),
                  op=ALU.mult, reads=[(K, "mk1")], sreads=[(K, "w1")], writes=[(K, "wt1")])
            p.dve("tensor_tensor", out=mk2, in0=mk2, in1=w2.broadcast_to([128, nexp]),
                  op=ALU.mult, reads=[(K, "mk2")], sreads=[(K, "w2")], writes=[(K, "wt2")])
            wtb = c.sb["WTOK"]
            p.dve("tensor_add", out=wtb[:, 0:nexp], in0=mk1, in1=mk2,
                  reads=[(K, "wt1"), (K, "wt2")], writes=[(K, "wtok")])
            p.pe_transpose(c.pb[7][0:nexp, 0:128], wtb[:, 0:nexp], idn[:],
                           reads=[(K, "wtok"), ("IDN",)], writes=[("PB", 7)])
            p.dve("tensor_copy", out=grb[0:nexp, t0 + tt * 128:t0 + (tt + 1) * 128],
                  in_=c.pb[7][0:nexp, 0:128],
                  reads=[("PB", 7)], writes=[(K, "grb", tb, tt)], tag="grbevac")
    p.op("pool", lambda e: e.nop(),
         reads=[(K, "grb", tb, tt) for tb in range(NTB) for tt in range(TB // 128)],
         writes=[("GRBDONE",)])
    p.fence()


def emit_l0_ffn(c: Ctx):
    p = c.p
    wb = c.sb["WBUF"]
    hb = c.sb["HBUF"]
    emit_ln(c, tag="l0f_ln")
    emit_moe_gate_top2(c, 8, "l0f_gwT", "l0f_scol", "l0f_bcol", "l0f_gate",
                       renorm_eps=1e-5, inner_ln=True)
    emit_swiglu_experts(c, "l0f_moe", n_routed=8, NI=16, INTER=2048,
                        wg_r="l0f_Wg", wu_r="l0f_Wu", wd_r="l0f_Wd",
                        wg_s="l0f_sWg", wu_s="l0f_sWu", wd_s="l0f_sWd",
                        h_dram=c.dram["l0f_h"])
    p.fence()



def declare_l1_attn(c: Ctx):
    c.dram_in("l1a_xg", (8, S, D), BF16)


def emit_l1_attn(c: Ctx, st_list=None):
    p = c.p
    p.fence()
    wb = c.sb["WBUF"]
    hb = c.sb["HBUF"]
    fb = c.sb["FBUF"]
    xg = c.dram["l1a_xg"]
    idn = c.sb["IDN"]
    K = ("l1a",)
    for st in (st_list if st_list is not None else range(S // 128)):
        s0 = st * 128
        for b in range(8):
            p.dma("sync", wb[:, b * 1024:(b + 1) * 1024], xg[b, s0:s0 + 128, :],
                  writes=[(K, "xg", b)], stream="xg")
        G = fb[:, 1024:1032]
        E = fb[:, 1040:1048]
        ssum = fb[:, 1048:1049]
        winv = fb[:, 1049:1050]
        w = fb[:, 1056:1064]
        for b in range(8):
            prod = hb[:, (b % 2) * 1024:(b % 2) * 1024 + 1024]
            p.dve("tensor_mul", out=prod, in0=wb[:, 0:1024],
                  in1=wb[:, b * 1024:(b + 1) * 1024],
                  reads=[(K, "xg", 0), (K, "xg", b)], writes=[(K, "prod", b % 2)])
            p.act(hb[:, 2048 + (b % 2) * 1024: 2048 + (b % 2) * 1024 + 1024], prod,
                  AF.Identity, scale=1.0 / 32.0, accum_out=G[:, b:b + 1],
                  reads=[(K, "prod", b % 2)], writes=[(K, "G", b), (K, "pd", b % 2)])
        p.act(E, G, AF.Exp, reads=[(K, "G", b) for b in range(8)], writes=[(K, "E")])
        p.dve("reduce_sum", out=ssum, in_=E, axis=AX.X, reads=[(K, "E")], writes=[(K, "ss")])
        p.dve("reciprocal", out=winv, in_=ssum, sreads=[(K, "ss")], writes=[(K, "wi")])
        p.dve("tensor_scalar", out=w, in0=E, scalar1=winv, scalar2=None, op0=ALU.mult,
              reads=[(K, "E")], sreads=[(K, "wi")], writes=[(K, "w")])
        acc = fb[:, 0:1024]
        p.dve("tensor_scalar", out=acc, in0=wb[:, 0:1024], scalar1=w[:, 0:1],
              scalar2=None, op0=ALU.mult,
              reads=[(K, "xg", 0)], sreads=[(K, "w")], writes=[(K, "acc")])
        for b in range(1, 8):
            p.dve("scalar_tensor_tensor", out=acc, in0=wb[:, b * 1024:(b + 1) * 1024],
                  scalar=w[:, b:b + 1], in1=acc, op0=ALU.mult, op1=ALU.add,
                  reads=[(K, "xg", b), (K, "acc")], sreads=[(K, "w")], writes=[(K, "acc")])
        for ch in range(NCHUNK):
            p.pe_transpose(c.pb[ch % 2][:, 0:128],
                           acc[:, ch * 128:(ch + 1) * 128],
                           idn[:],
                           reads=[(K, "acc"), ("IDN",)], writes=[("PB", ch % 2)])
            p.dve("tensor_add", out=xchunk(c, "XRES", ch, s0, 128),
                  in0=xchunk(c, "XRES", ch, s0, 128), in1=c.pb[ch % 2][:, 0:128],
                  reads=[("XRES", ch), ("PB", ch % 2)], writes=[("XRES", ch)],
                  tag="l1a_res")
    p.fence()


def emit_flash16(c: Ctx, tag, qT_dram, kT_dram, qdone, kdone, scale=0.125):
    p = c.p
    hb = c.sb["HBUF"]
    fb = c.sb["FBUF"]
    srow = c.sb["SROW"]
    tri = c.sb["TRI"]
    vstore = c.sb["VSTORE"]
    ktile = c.sb["KTILE"]
    qtile = c.sb["QTILE"]
    NH = 16
    VW = 65 * NH
    K = (tag,)
    for h in range(NH):
        hch, hrow = h // 2, (h % 2) * 64
        p.dma("sync", ktile[0:64, :], kT_dram[h * 64:(h + 1) * 64, :],
              reads=kdone, writes=[(K, "KT")], stream="kld", tag=f"kld{h}")
        for qb in range(NTB):
            t0 = qb * TB
            p.dma("sync", qtile[0:64, 0:TB], qT_dram[h * 64:(h + 1) * 64, t0:t0 + TB],
                  reads=qdone, writes=[(K, "QT")], stream="kld", tag=f"qld{h}")
            pso = c.pb[6]
            kpso = ("PB", 6)
            nkt = (qb + 1) * 4
            for kt in range(nkt):
                ps = c.pb[3 + (kt % 2)]
                kps = ("PB", 3 + (kt % 2))
                p.pe_mm(ps[:], ktile[0:64, kt * 128:(kt + 1) * 128], qtile[0:64, 0:TB],
                        start=True, stop=True,
                        reads=[(K, "KT"), (K, "QT")], writes=[kps], tag=f"qk{h}")
                pt = hb[:, (kt % 2) * TB:(kt % 2) * TB + TB]
                kpt = ((K, "P", kt % 2),)
                p.act(pt, ps[:], AF.Exp, scale=scale,
                      reads=[kps], writes=[kpt], tag=f"exp{h}")
                qoff = kt * 128 - t0
                if qoff >= 0:
                    if qoff > 0:
                        p.dve("memset", hb[:, (kt % 2) * TB:(kt % 2) * TB + qoff], 0.0,
                              reads=[kpt], writes=[kpt], tag="mz")
                    p.dve("tensor_mul", out=hb[:, (kt % 2) * TB + qoff:(kt % 2) * TB + qoff + 128],
                          in0=hb[:, (kt % 2) * TB + qoff:(kt % 2) * TB + qoff + 128],
                          in1=tri[:], reads=[kpt, ("TRI",)], writes=[kpt], tag="mt")
                p.pe_mm(pso[0:65, :], vstore[:, kt * VW + h * 65: kt * VW + h * 65 + 65], pt,
                        start=(kt == 0), stop=(kt == nkt - 1),
                        reads=[("VST", kt), kpt], writes=[kpso], tag=f"pv{h}")
            lrow = srow[0:1, 5 * TB:5 * TB + TB]
            p.act(lrow, pso[64:65, 0:TB], AF.Ln, reads=[kpso], writes=[(K, "lr")])
            p.act(lrow, lrow, AF.Exp, scale=-1.0, reads=[(K, "lr")], writes=[(K, "lr2")])
            psn = c.pb[5]
            p.pe_mm(psn[0:64, 0:TB], c.sb["ONESR"][0:1, 0:64], lrow,
                    start=True, stop=True, reads=[(K, "lr2"), ("ONES",)], writes=[("PB", 5)])
            nb = fb[:, 0:TB]
            p.dve("tensor_copy", out=nb[0:64, :], in_=psn[0:64, 0:TB],
                  reads=[("PB", 5)], writes=[(K, "nb")])
            p.dve("tensor_mul", out=c.sb["XLN"][hrow:hrow + 64, hch * S + t0:hch * S + t0 + TB],
                  in0=pso[0:64, 0:TB], in1=nb[0:64, :],
                  reads=[kpso, (K, "nb")], writes=[("XLN", hch)], tag="aoevac")


def emit_outproj_residual(c: Ctx, tag, w_dram, bcol_dram=None):
    p = c.p
    wb = c.sb["WBUF"]
    for dch in range(NCHUNK):
        p.dma("pool", wb[:, dch * 1024:(dch + 1) * 1024],
              w_dram[dch * 128:(dch + 1) * 128, :],
              writes=[("WBUF", dch)], stream="wl", tag=f"{tag}_w")
    if bcol_dram is not None:
        bocol = c.sb["BCOL"]
        p.dma("sync", bocol[:, 8:16], bcol_dram, writes=[("BCOL2",)], stream="bl")
    for tb in range(NTB):
        t0 = tb * TB
        for dch in range(NCHUNK):
            ps = c.pb[dch % 2]
            kps = ("PB", dch % 2)
            for ach in range(NCHUNK):
                p.pe_mm(ps[:], wb[:, ach * 1024 + dch * 128: ach * 1024 + dch * 128 + 128],
                        xchunk(c, "XLN", ach, t0, TB),
                        start=(ach == 0), stop=(ach == NCHUNK - 1),
                        reads=[("WBUF", ach), ("XLN", ach)], writes=[kps])
            if bcol_dram is not None:
                p.dve("scalar_tensor_tensor", out=xchunk(c, "XRES", dch, t0, TB),
                      in0=ps[:], scalar=c.sb["BCOL"][:, 8 + dch:9 + dch],
                      in1=xchunk(c, "XRES", dch, t0, TB), op0=ALU.add, op1=ALU.add,
                      reads=[kps, ("BCOL2",), ("XRES", dch)], writes=[("XRES", dch)],
                      tag=f"{tag}_res")
            else:
                p.dve("tensor_add", out=xchunk(c, "XRES", dch, t0, TB),
                      in0=xchunk(c, "XRES", dch, t0, TB), in1=ps[:],
                      reads=[kps, ("XRES", dch)], writes=[("XRES", dch)],
                      tag=f"{tag}_res")



def host_prep_l1_ffn(inp):
    d = {}
    d["l1f_wln1_col"] = np.ascontiguousarray(inp["L1_ffn_w_ln1"].reshape(8, 128).T)
    d["l1f_wln2_col"] = np.ascontiguousarray(inp["L1_ffn_w_ln2"].reshape(8, 128).T)
    for nm in ("W_kvd", "W_qd", "W_ku", "W_qu", "W_vu", "W_ropeq", "W_ropek", "W_o",
               "Wg_s", "Wu_s", "Wd_s", "Wg_r", "Wu_r", "Wd_r"):
        d[f"l1f_{nm}"] = inp[f"L1_ffn_{nm}"]
    d["l1f_routerT"] = np.ascontiguousarray(inp["L1_ffn_W_router"].T)
    d["l1f_routerT"] = np.ascontiguousarray(inp["L1_ffn_W_router"])
    d["l1f_rbias"] = np.ascontiguousarray(inp["L1_ffn_routing_bias"].reshape(1, 7))
    hh = 32
    inv = 1.0 / (10000.0 ** (np.arange(0, hh, 2, dtype=np.float64) / hh))
    t = np.arange(S, dtype=np.float64)
    fr = np.outer(t, inv)
    emb = np.concatenate([fr, fr], 1)
    cos = np.cos(emb).astype(np.float32).T
    sin = np.sin(emb).astype(np.float32).T
    sgn = np.where(np.arange(32) < 16, -1.0, 1.0).astype(np.float32)[:, None]
    d["l1f_cos"] = np.ascontiguousarray(np.tile(cos, (4, 1)))
    d["l1f_sin"] = np.ascontiguousarray(np.tile(sin * sgn, (4, 1)))
    return d


def declare_l1_ffn(c: Ctx):
    H, L, NHH = D, 256, 512
    c.dram_in("l1f_wln1_col", (128, 8), F32)
    c.dram_in("l1f_wln2_col", (128, 8), F32)
    c.dram_in("l1f_W_kvd", (H, L), F32)
    c.dram_in("l1f_W_qd", (H, L), F32)
    c.dram_in("l1f_W_ku", (L, NHH), F32)
    c.dram_in("l1f_W_qu", (L, NHH), F32)
    c.dram_in("l1f_W_vu", (L, H), F32)
    c.dram_in("l1f_W_ropeq", (L, NHH), F32)
    c.dram_in("l1f_W_ropek", (H, NHH), F32)
    c.dram_in("l1f_W_o", (H, H), F32)
    c.dram_in("l1f_Wg_s", (H, H), F32)
    c.dram_in("l1f_Wu_s", (H, H), F32)
    c.dram_in("l1f_Wd_s", (H, H), F32)
    c.dram_in("l1f_Wg_r", (7, H, H), F32)
    c.dram_in("l1f_Wu_r", (7, H, H), F32)
    c.dram_in("l1f_Wd_r", (7, H, H), F32)
    c.dram_in("l1f_routerT", (H, 7), F32)
    c.dram_in("l1f_rbias", (1, 7), F32)
    c.dram_in("l1f_cos", (128, S), F32)
    c.dram_in("l1f_sin", (128, S), F32)
    c.dram_tmp("l1f_xln", (D, S), BF16)
    c.dram_tmp("l1f_x1", (D, S), BF16)
    c.dram_tmp("l1f_qT", (D, S), BF16)
    c.dram_tmp("l1f_kT", (D, S), BF16)
    c.dram_tmp("l1f_h", (1024, S), BF16)
    c.dram_tmp("l1f_vT", (S, 1024), BF16)


def emit_rms_inplace(c: Ctx, tag, wcol_name, eps=1e-5, src_dram=None, src_name="XLN"):
    p = c.p
    hb = c.sb["HBUF"]
    srow = c.sb["SROW"]
    onesb = c.sb["ONESB"]
    wcol = c.sb["BCOL"]
    K = (tag,)
    p.dma("sync", wcol[:, 0:8], c.dram[wcol_name], writes=[(K, "wcol")], stream="bl")
    for tb in range(NTB):
        t0 = tb * TB
        for ch in range(NCHUNK):
            if src_dram is not None:
                xv = hb[:, 2048 + ch % 2 * TB: 2048 + ch % 2 * TB + TB]
                p.dma("sync", xv, src_dram[ch * 128:(ch + 1) * 128, t0:t0 + TB],
                      writes=[(K, "xv", ch % 2)], stream="x1s")
                rk = [(K, "xv", ch % 2)]
            else:
                xv = xchunk(c, src_name, ch, t0, TB)
                rk = [(src_name, ch)]
            sqv = hb[:, 0:TB]
            p.act(sqv, xv, AF.Square, reads=rk, writes=[(K, "sq")])
            p.pe_mm(c.pb[0][0:1, 0:TB], onesb[:, 0:1], sqv,
                    start=(ch == 0), stop=(ch == NCHUNK - 1),
                    reads=[(K, "sq")], writes=[("PB", 0)])
        nrm = srow[0:1, 0:TB]
        p.dve("tensor_scalar_mul", out=nrm, in0=c.pb[0][0:1, 0:TB], scalar1=1.0 / D,
              reads=[("PB", 0)], writes=[(K, "ms")])
        p.act(nrm, nrm, AF.Sqrt, reads=[(K, "ms")], writes=[(K, "rt")])
        p.dve("tensor_scalar_add", out=nrm, in0=nrm, scalar1=eps,
              reads=[(K, "rt")], writes=[(K, "rte")])
        p.act(nrm, nrm, AF.Ln, reads=[(K, "rte")], writes=[(K, "ln")])
        p.act(nrm, nrm, AF.Exp, scale=-1.0, reads=[(K, "ln")], writes=[(K, "ri")])
        p.pe_mm(c.pb[1][:, 0:TB], c.sb["ONESR"][:], nrm, start=True, stop=True,
                reads=[(K, "ri"), ("ONES",)], writes=[("PB", 1)])
        for ch in range(NCHUNK):
            if src_dram is not None:
                xv = hb[:, 2048 + ch % 2 * TB: 2048 + ch % 2 * TB + TB]
                p.dma("sync", xv, src_dram[ch * 128:(ch + 1) * 128, t0:t0 + TB],
                      writes=[(K, "xv2", ch % 2)], stream="x1s")
                rk = [(K, "xv2", ch % 2)]
            else:
                xv = xchunk(c, src_name, ch, t0, TB)
                rk = [(src_name, ch)]
            tv = c.sb["FBUF"][:, 0:TB]
            p.dve("tensor_mul", out=tv, in0=xv, in1=c.pb[1][:, 0:TB],
                  reads=rk + [("PB", 1)], writes=[(K, "tv")])
            p.dve("tensor_scalar", out=xchunk(c, "XLN", ch, t0, TB), in0=tv,
                  scalar1=wcol[:, ch:ch + 1], scalar2=None, op0=ALU.mult,
                  reads=[(K, "tv"), (K, "wcol")], writes=[("XLN", ch)])


def emit_l1_ffn(c: Ctx, upto=None):
    p = c.p
    wb = c.sb["WBUF"]
    hb = c.sb["HBUF"]
    fb = c.sb["FBUF"]
    K = ("l1f",)
    emit_ln(c, tag="l1f_ln")
    for tb in range(NTB):
        t0 = tb * TB
        for ch in range(NCHUNK):
            p.dve("tensor_add", out=xchunk(c, "XRES", ch, t0, TB),
                  in0=xchunk(c, "XRES", ch, t0, TB), in1=xchunk(c, "XLN", ch, t0, TB),
                  reads=[("XRES", ch), ("XLN", ch)], writes=[("XRES", ch)], tag="l1f_addln")
    for ch in range(NCHUNK):
        p.dma("sync", c.dram["l1f_xln"][ch * 128:(ch + 1) * 128, :], xchunk(c, "XLN", ch),
              reads=[("XLN", ch)], writes=[(K, "xlnd", ch)], stream="qout")
    emit_rms_inplace(c, "l1f_rms1", "l1f_wln1_col")
    p.fence()
    if upto == "rms1":
        return
    vst = c.sb["VSTORE"]
    for nm, dst0 in (("l1f_W_kvd", 0), ("l1f_W_qd", 4096)):
        for dch in range(NCHUNK):
            p.dma("pool", wb[:, dch * 256:dch * 256 + 256],
                  c.dram[nm][dch * 128:(dch + 1) * 128, :],
                  writes=[(K, "wlat", dch)], stream="wl")
        for och in range(2):
            for tb in range(NTB):
                t0 = tb * TB
                ps = c.pb[tb % 2]
                kps = ("PB", tb % 2)
                for dch in range(NCHUNK):
                    p.pe_mm(ps[:], wb[:, dch * 256 + och * 128: dch * 256 + och * 128 + 128],
                            xchunk(c, "XLN", dch, t0, TB),
                            start=(dch == 0), stop=(dch == NCHUNK - 1),
                            reads=[(K, "wlat", dch), ("XLN", dch)], writes=[kps])
                p.dve("tensor_copy", out=vst[:, dst0 + och * S + t0: dst0 + och * S + t0 + TB],
                      in_=ps[:], reads=[kps], writes=[(K, "lat", dst0, och)], tag="latevac")
    latk = [(K, "lat", d0, o) for d0 in (0, 4096) for o in range(2)]
    cosT = c.sb["CONST"]
    p.dma("pool", cosT[:, 0:S], c.dram["l1f_cos"], writes=[("CONSTC",)], stream="cl")
    p.dma("pool", cosT[:, S:2 * S], c.dram["l1f_sin"], writes=[("CONSTC",)], stream="cl")
    for nm, lat0, dst in (("l1f_W_ku", 0, "l1f_kT"), ("l1f_W_qu", 4096, "l1f_qT")):
        for lch in range(2):
            p.dma("pool", wb[:, 2048 + lch * 512:2048 + lch * 512 + 512],
                  c.dram[nm][lch * 128:(lch + 1) * 128, :],
                  writes=[(K, "wup", lch)], stream="wl")
        for och in range(4):
            for tb in range(NTB):
                t0 = tb * TB
                ps = c.pb[2]
                kps = ("PB", 2)
                for lch in range(2):
                    p.pe_mm(ps[:], wb[:, 2048 + lch * 512 + och * 128: 2048 + lch * 512 + och * 128 + 128],
                            vst[:, lat0 + lch * S + t0: lat0 + lch * S + t0 + TB],
                            start=(lch == 0), stop=(lch == 1),
                            reads=[(K, "wup", lch)] + latk, writes=[kps])
                bfv = hb[:, 0:TB]
                p.dve("tensor_copy", out=bfv, in_=ps[:], reads=[kps], writes=[(K, "upe")])
                for hh4 in range(4):
                    p.dma("sync",
                          c.dram[dst][och * 256 + hh4 * 64: och * 256 + hh4 * 64 + 32, t0:t0 + TB],
                          bfv[hh4 * 32:(hh4 + 1) * 32, :],
                          reads=[(K, "upe")], writes=[(K, "kqd", dst, och, tb)], stream="qout")
    for nm, src_mode, dst in (("l1f_W_ropek", "xln", "l1f_kT"), ("l1f_W_ropeq", "qlat", "l1f_qT")):
        nsrc = NCHUNK if src_mode == "xln" else 2
        for sch in range(nsrc):
            p.dma("pool", wb[:, 4096 + sch * 512:4096 + sch * 512 + 512],
                  c.dram[nm][sch * 128:(sch + 1) * 128, :],
                  writes=[(K, "wrp", sch)], stream="wl")
        for och in range(4):
            for tb in range(NTB):
                t0 = tb * TB
                ps = c.pb[3]
                kps = ("PB", 3)
                for sch in range(nsrc):
                    rhsv = (xchunk(c, "XLN", sch, t0, TB) if src_mode == "xln"
                            else vst[:, 4096 + sch * S + t0: 4096 + sch * S + t0 + TB])
                    rrd = [("XLN", sch)] if src_mode == "xln" else latk
                    p.pe_mm(ps[:], wb[:, 4096 + sch * 512 + och * 128: 4096 + sch * 512 + och * 128 + 128],
                            rhsv,
                            start=(sch == 0), stop=(sch == nsrc - 1),
                            reads=[(K, "wrp", sch)] + rrd, writes=[kps])
                qf = fb[:, 0:TB]
                qp = fb[:, TB:2 * TB]
                p.dve("tensor_copy", out=qf, in_=ps[:], reads=[kps], writes=[(K, "qf")])
                for blk in range(8):
                    srcr = (blk ^ 1) * 16
                    p.dma("sync", qp[blk * 16:blk * 16 + 16, :], qf[srcr:srcr + 16, :],
                          reads=[(K, "qf")], writes=[(K, "qp", blk)], stream="shuf")
                t1 = fb[:, 2 * TB:3 * TB]
                p.dve("tensor_mul", out=t1, in0=qf, in1=cosT[:, t0:t0 + TB],
                      reads=[(K, "qf"), ("CONSTC",)], writes=[(K, "t1")])
                p.dve("tensor_mul", out=qp, in0=qp, in1=cosT[:, S + t0:S + t0 + TB],
                      reads=[(K, "qp", b) for b in range(8)] + [("CONSTC",)],
                      writes=[(K, "qp2")])
                bfv = hb[:, 0:TB]
                p.dve("tensor_add", out=bfv, in0=t1, in1=qp,
                      reads=[(K, "t1"), (K, "qp2")], writes=[(K, "rp")])
                for hh4 in range(4):
                    p.dma("sync",
                          c.dram[dst][och * 256 + hh4 * 64 + 32: och * 256 + hh4 * 64 + 64, t0:t0 + TB],
                          bfv[hh4 * 32:(hh4 + 1) * 32, :],
                          reads=[(K, "rp")], writes=[(K, "rpd", dst, och, tb)], stream="qout")
    p.fence()
    for lch in range(2):
        p.dma("pool", wb[:, lch * 1024:lch * 1024 + 1024],
              c.dram["l1f_W_vu"][lch * 128:(lch + 1) * 128, :],
              writes=[(K, "wv", lch)], stream="wl")
    vstore = c.sb["VSTORE"]
    VW = 65 * 16
    for kt in range(S // 128):
        tt0 = kt * 128
        for half in range(2):
            ps = c.pb[2 + half]
            kps = ("PB", 2 + half)
            for lch in range(2):
                p.pe_mm(ps[:], vst[:, lch * S + tt0: lch * S + tt0 + 128],
                        wb[:, lch * 1024 + half * 512: lch * 1024 + half * 512 + 512],
                        start=(lch == 0), stop=(lch == 1),
                        reads=[(K, "wv", lch)] + latk, writes=[kps])
            bfv = hb[:, half * TB:half * TB + TB]
            p.dve("tensor_copy", out=bfv, in_=ps[:], reads=[kps], writes=[(K, "ve", half)])
            p.dma("sync", c.dram["l1f_vT"][tt0:tt0 + 128, half * 512:half * 512 + 512], bfv,
                  reads=[(K, "ve", half)], writes=[(K, "vd", kt, half)], stream="qout")
    p.fence()
    if upto == "proj":
        return
    p.dve("memset", vstore[:], 0.0, writes=[("VST", kt2) for kt2 in range(16)], tag="vclr")
    p.dve("memset", vstore[:, 64::65], 1.0,
          reads=[("VST", kt2) for kt2 in range(16)], writes=[("VONE",)])
    for kt in range(S // 128):
        p.dma("sync",
              vstore[:, kt * VW:(kt + 1) * VW].rearrange("p (h w) -> p h w", h=16)[:, :, 0:64],
              c.dram["l1f_vT"][kt * 128:(kt + 1) * 128, :].rearrange("p (h w) -> p h w", h=16),
              reads=[(K, "vd", kt, 0), (K, "vd", kt, 1), ("VONE",)],
              writes=[("VST", kt)], stream="kld")
    p.fence()
    if upto == "refill":
        return
    qdone = [(K, "kqd", "l1f_qT", och, tb) for och in range(4) for tb in range(NTB)] + \
            [(K, "rpd", "l1f_qT", och, tb) for och in range(4) for tb in range(NTB)]
    kdone = [(K, "kqd", "l1f_kT", och, tb) for och in range(4) for tb in range(NTB)] + \
            [(K, "rpd", "l1f_kT", och, tb) for och in range(4) for tb in range(NTB)]
    emit_flash16(c, "l1fl", c.dram["l1f_qT"], c.dram["l1f_kT"], qdone, kdone)
    p.fence()
    if upto == "flash":
        return
    for dch in range(NCHUNK):
        p.dma("pool", wb[:, dch * 1024:(dch + 1) * 1024],
              c.dram["l1f_W_o"][dch * 128:(dch + 1) * 128, :],
              writes=[("WBUF", dch)], stream="wl", tag="l1o_w")
    for tb in range(NTB):
        t0 = tb * TB
        for dch in range(NCHUNK):
            ps = c.pb[dch % 2]
            kps = ("PB", dch % 2)
            for ach in range(NCHUNK):
                p.pe_mm(ps[:], wb[:, ach * 1024 + dch * 128: ach * 1024 + dch * 128 + 128],
                        xchunk(c, "XLN", ach, t0, TB),
                        start=(ach == 0), stop=(ach == NCHUNK - 1),
                        reads=[("WBUF", ach), ("XLN", ach)], writes=[kps])
            xlt = hb[:, 0:TB]
            p.dma("sync", xlt, c.dram["l1f_xln"][dch * 128:(dch + 1) * 128, t0:t0 + TB],
                  reads=[(K, "xlnd", dch)], writes=[(K, "xlt")], stream="x1s")
            p.dve("tensor_add", out=xchunk(c, "XRES", dch, t0, TB),
                  in0=xchunk(c, "XRES", dch, t0, TB), in1=ps[:],
                  reads=[kps, ("XRES", dch)], writes=[("XRES", dch)], tag="l1mla_res")
            x1t = hb[:, TB:2 * TB]
            p.dve("tensor_add", out=x1t, in0=xlt, in1=ps[:],
                  reads=[kps, (K, "xlt")], writes=[(K, "x1t")])
            p.dma("sync", c.dram["l1f_x1"][dch * 128:(dch + 1) * 128, t0:t0 + TB], x1t,
                  reads=[(K, "x1t")], writes=[(K, "x1d", dch, tb)], stream="qout")
    p.fence()
    if upto == "mla":
        return
    emit_rms_inplace(c, "l1f_rms2", "l1f_wln2_col", src_dram=c.dram["l1f_x1"])
    p.fence()
    emit_moe_gate_top2(c, 7, "l1f_routerT", None, None, "l1f_gate",
                       renorm_eps=0.0, use_sigmoid=True, inner_ln=False,
                       bias_row="l1f_rbias")
    emit_swiglu_experts(c, "l1f_moe", n_routed=7, NI=8, INTER=1024,
                        wg_r="l1f_Wg_r", wu_r="l1f_Wu_r", wd_r="l1f_Wd_r",
                        wg_s="l1f_Wg_s", wu_s="l1f_Wu_s", wd_s="l1f_Wd_s",
                        h_dram=c.dram["l1f_h"])
    p.fence()


def emit_swiglu_experts(c: Ctx, tag, n_routed, NI, INTER,
                        wg_r, wu_r, wd_r, wg_s, wu_s, wd_s, h_dram):
    p = c.p
    wb = c.sb["WBUF"]
    hb = c.sb["HBUF"]
    grb = c.sb["GRB"]
    wrow = c.sb["WROW"]
    for e in range(n_routed + 1):
        shared = (e == n_routed)
        wg_d = c.dram[wg_s] if shared else c.dram[wg_r][e]
        wu_d = c.dram[wu_s] if shared else c.dram[wu_r][e]
        wd_d = c.dram[wd_s] if shared else c.dram[wd_r][e]
        if not shared:
            p.dma("sync", wrow[0:1, :], grb[e:e + 1, :],
                  reads=[("GRBDONE",)], writes=[("WROW",)], stream="wrow")
            for tb in range(NTB):
                p.pe_mm(c.pb[2][:, 0:TB], c.sb["ONESBR"][0:1, 0:128],
                        wrow[0:1, tb * TB:(tb + 1) * TB],
                        start=True, stop=True,
                        reads=[("WROW",), ("ONES",)], writes=[("PB", 2)])
                p.dve("tensor_copy", out=hb[:, 1024 + tb * TB:1024 + (tb + 1) * TB],
                      in_=c.pb[2][:, 0:TB],
                      reads=[("PB", 2)], writes=[("WBC", tb)], tag="wbc")
        wgv = wb[:, 0:2048].rearrange("p (s c n) -> p s c n", s=2, c=NCHUNK)
        wuv = wb[:, 2048:4096].rearrange("p (s c n) -> p s c n", s=2, c=NCHUNK)
        wdv = wb[:, 4096:6144].rearrange("p (s n) -> p s n", s=2)
        for i in range(NI):
            sl = i % 2
            p.dma("pool", wgv[:, sl],
                  wg_d.rearrange("(c p) n -> p c n", p=128)[:, :, i * 128:(i + 1) * 128],
                  writes=[("WG", sl)], stream="wl", tag=f"{tag}wg{e}_{i}")
            p.dma("pool", wuv[:, sl],
                  wu_d.rearrange("(c p) n -> p c n", p=128)[:, :, i * 128:(i + 1) * 128],
                  writes=[("WU", sl)], stream="wl", tag=f"{tag}wu{e}_{i}")
            for tb in range(NTB):
                t0 = tb * TB
                for ch in range(NCHUNK):
                    p.pe_mm(c.pb[0][:], wgv[:, sl, ch], xchunk(c, "XLN", ch, t0, TB),
                            start=(ch == 0), stop=(ch == NCHUNK - 1),
                            reads=[("WG", sl), ("XLN", ch)], writes=[("PB", 0)])
                    p.pe_mm(c.pb[1][:], wuv[:, sl, ch], xchunk(c, "XLN", ch, t0, TB),
                            start=(ch == 0), stop=(ch == NCHUNK - 1),
                            reads=[("WU", sl), ("XLN", ch)], writes=[("PB", 1)])
                gs = hb[:, 3072:3584]
                uw = hb[:, 3584:4096]
                p.act(gs, c.pb[0][:], AF.Silu, reads=[("PB", 0)], writes=[("GS",)])
                if shared:
                    p.dve("tensor_copy", out=uw, in_=c.pb[1][:],
                          reads=[("PB", 1)], writes=[("UW",)])
                else:
                    p.dve("tensor_mul", out=uw, in0=c.pb[1][:],
                          in1=hb[:, 1024 + tb * TB:1024 + (tb + 1) * TB],
                          reads=[("PB", 1), ("WBC", tb)], writes=[("UW",)])
                htile = hb[:, sl * TB:(sl + 1) * TB]
                p.dve("tensor_mul", out=htile, in0=gs, in1=uw,
                      reads=[("GS",), ("UW",)], writes=[("HT", sl)])
                p.dma("sync", h_dram[i * 128:(i + 1) * 128, t0:t0 + TB], htile,
                      reads=[("HT", sl)], writes=[("HD", i, tb)], stream="hout")
        for tb in range(NTB):
            t0 = tb * TB
            for i in range(NI):
                sl = i % 2
                p.dma("pool", wdv[:, sl], wd_d[i * 128:(i + 1) * 128, :],
                      writes=[("WD", sl)], stream="wl", tag=f"{tag}wd{e}_{i}")
                p.dma("sync", hb[:, sl * TB:(sl + 1) * TB], h_dram[i * 128:(i + 1) * 128, t0:t0 + TB],
                      reads=[("HD", i, tb)], writes=[("HT", sl)], stream="hin")
                for dch in range(NCHUNK):
                    p.pe_mm(c.pb[dch][:], wdv[:, sl, dch * 128:(dch + 1) * 128],
                            hb[:, sl * TB:(sl + 1) * TB],
                            start=(i == 0), stop=(i == NI - 1),
                            reads=[("WD", sl), ("HT", sl)], writes=[("PB", dch)])
            for dch in range(NCHUNK):
                p.dve("tensor_add", out=xchunk(c, "XRES", dch, t0, TB),
                      in0=xchunk(c, "XRES", dch, t0, TB), in1=c.pb[dch][:],
                      reads=[("XRES", dch), ("PB", dch)], writes=[("XRES", dch)],
                      tag=f"{tag}_res")



def host_prep_l2_attn(inp):
    d = {}
    d["l2a_prw_col"] = np.ascontiguousarray(inp["L2_attn_pre_norm_w"].reshape(8, 128).T)
    d["l2a_ffnw_col"] = np.ascontiguousarray(inp["L2_attn_ffn_norm_w"].reshape(8, 128).T)
    cw = inp["L2_attn_conv_w"].reshape(1024, 4)
    d["l2a_convw_col"] = np.ascontiguousarray(cw.reshape(8, 128, 4).transpose(1, 0, 2).reshape(128, 32))
    d["l2a_convb_col"] = np.ascontiguousarray(inp["L2_attn_conv_b"].reshape(8, 128).T)
    d["l2a_db_col"] = np.ascontiguousarray(inp["L2_attn_decay_bias"].reshape(8, 128).T)
    d["l2a_waT"] = np.ascontiguousarray(inp["L2_attn_w_a"].T)
    d["l2a_wiT"] = np.ascontiguousarray(inp["L2_attn_w_i"].T)
    d["l2a_wvT"] = np.ascontiguousarray(inp["L2_attn_w_v"].T)
    d["l2a_wopT"] = np.ascontiguousarray(inp["L2_attn_out_proj_w"].T)
    wgu = inp["L2_attn_w_gate_up"]
    d["l2a_wg"] = np.ascontiguousarray(wgu[0:4096].T)
    d["l2a_wu"] = np.ascontiguousarray(wgu[4096:8192].T)
    d["l2a_wd"] = np.ascontiguousarray(inp["L2_attn_w_down"].T)
    return d


def declare_l2_attn(c: Ctx):
    for nm, shp in [("l2a_prw_col", (128, 8)), ("l2a_ffnw_col", (128, 8)),
                    ("l2a_convw_col", (128, 32)), ("l2a_convb_col", (128, 8)),
                    ("l2a_db_col", (128, 8)),
                    ("l2a_waT", (D, D)), ("l2a_wiT", (D, D)), ("l2a_wvT", (D, D)),
                    ("l2a_wopT", (D, D)), ("l2a_wg", (D, 4096)), ("l2a_wu", (D, 4096)),
                    ("l2a_wd", (4096, D))]:
        c.dram_in(nm, shp, F32)
    c.dram_tmp("l2a_a", (D, S), F32)
    c.dram_tmp("l2a_iv", (D, S), F32)
    c.dram_tmp("l2a_h", (4096, S), BF16)
    c.dram_tmp("l2a_xln", (D, S), BF16)
    c.dram_tmp("l2a_x1", (D, S), BF16)


def emit_l2_attn(c: Ctx, upto=None):
    p = c.p
    wb = c.sb["WBUF"]
    hb = c.sb["HBUF"]
    fb = c.sb["FBUF"]
    K = ("l2a",)
    emit_ln(c, tag="l2a_ln")
    for tb in range(NTB):
        t0 = tb * TB
        for ch in range(NCHUNK):
            p.dve("tensor_add", out=xchunk(c, "XRES", ch, t0, TB),
                  in0=xchunk(c, "XRES", ch, t0, TB), in1=xchunk(c, "XLN", ch, t0, TB),
                  reads=[("XRES", ch), ("XLN", ch)], writes=[("XRES", ch)], tag="l2a_addln")
    for ch in range(NCHUNK):
        p.dma("sync", c.dram["l2a_xln"][ch * 128:(ch + 1) * 128, :], xchunk(c, "XLN", ch),
              reads=[("XLN", ch)], writes=[(K, "xlnd", ch)], stream="qout")
    emit_rms_inplace(c, "l2a_rms1", "l2a_prw_col", eps=1e-6)
    p.fence()
    if upto == "rms1":
        return
    ccol = c.sb["BCOL"]
    p.dma("sync", ccol[:, 0:8], c.dram["l2a_convb_col"], writes=[(K, "cb")], stream="bl")
    cwcol = c.sb["CONVW"]
    p.dma("sync", cwcol[:], c.dram["l2a_convw_col"], writes=[(K, "cw")], stream="bl")
    for ch in range(NCHUNK):
        pad = hb[:, 0:2051]
        p.dve("memset", pad[:, 0:3], 0.0, writes=[(K, "pad0")])
        p.dve("tensor_copy", out=pad[:, 3:2051], in_=xchunk(c, "XLN", ch),
              reads=[("XLN", ch), (K, "pad0")], writes=[(K, "pad")])
        acc2 = c.sb["VSTORE"][:].bitcast(F32)[:, 0:2048]
        p.dve("tensor_scalar", out=acc2, in0=pad[:, 0:2048], scalar1=cwcol[:, ch * 4:ch * 4 + 1],
              scalar2=None, op0=ALU.mult,
              reads=[(K, "pad"), (K, "cw")], writes=[(K, "acc")])
        for j in range(1, 4):
            p.dve("scalar_tensor_tensor", out=acc2, in0=pad[:, j:j + 2048],
                  scalar=cwcol[:, ch * 4 + j:ch * 4 + j + 1], in1=acc2,
                  op0=ALU.mult, op1=ALU.add,
                  reads=[(K, "pad"), (K, "cw"), (K, "acc")], writes=[(K, "acc")])
        p.dve("tensor_scalar", out=xchunk(c, "XLN", ch), in0=acc2,
              scalar1=ccol[:, ch:ch + 1], scalar2=None, op0=ALU.add,
              reads=[(K, "acc"), (K, "cb")], writes=[("XLN", ch)])
    p.fence()
    if upto == "conv":
        return
    dbcol = c.sb["BCOL"]
    p.dma("sync", dbcol[:, 8:16], c.dram["l2a_db_col"], writes=[(K, "db")], stream="bl")
    for nm, role in (("l2a_waT", "a"), ("l2a_wiT", "i"), ("l2a_wvT", "v")):
        for dch in range(NCHUNK):
            p.dma("pool", wb[:, dch * 1024:(dch + 1) * 1024],
                  c.dram[nm][dch * 128:(dch + 1) * 128, :],
                  writes=[("WBUF", dch)], stream="wl", tag=f"l2a_{role}w")
        for och in range(NCHUNK):
            for tb in range(NTB):
                t0 = tb * TB
                bank = 2 + ((och * NTB + tb) % 2) if role != "v" else 4 + ((och * NTB + tb) % 2)
                ps = c.pb[bank]
                kps = ("PB", bank)
                for dch in range(NCHUNK):
                    p.pe_mm(ps[:], wb[:, dch * 1024 + och * 128: dch * 1024 + och * 128 + 128],
                            xchunk(c, "XLN", dch, t0, TB),
                            start=(dch == 0), stop=(dch == NCHUNK - 1),
                            reads=[("WBUF", dch), ("XLN", dch)], writes=[kps])
                if role == "a":
                    av = fb[:, 0:TB]
                    p.act(av, ps[:], AF.Sigmoid, bias=dbcol[:, 8 + och:9 + och],
                          reads=[kps, (K, "db")], writes=[(K, "av")])
                    p.dma("sync", c.dram["l2a_a"][och * 128:(och + 1) * 128, t0:t0 + TB], av,
                          reads=[(K, "av")], writes=[(K, "ad", och, tb)], stream="hout")
                elif role == "i":
                    iv = fb[:, TB:2 * TB]
                    p.act(iv, ps[:], AF.Sigmoid, reads=[kps], writes=[(K, "ig", och, tb)])
                    p.dma("sync", c.dram["l2a_iv"][och * 128:(och + 1) * 128, t0:t0 + TB], iv,
                          reads=[(K, "ig", och, tb)], writes=[(K, "igd", och, tb)], stream="hout")
                else:
                    iv = fb[:, TB:2 * TB]
                    p.dma("sync", iv, c.dram["l2a_iv"][och * 128:(och + 1) * 128, t0:t0 + TB],
                          reads=[(K, "igd", och, tb)], writes=[(K, "ig2", och, tb)], stream="hin")
                    vv = fb[:, 2 * TB:3 * TB]
                    p.dve("tensor_mul", out=vv, in0=iv, in1=ps[:],
                          reads=[kps, (K, "ig2", och, tb)], writes=[(K, "vv")])
                    p.dma("sync", c.dram["l2a_iv"][och * 128:(och + 1) * 128, t0:t0 + TB], vv,
                          reads=[(K, "vv")], writes=[(K, "ivd", och, tb)], stream="hout")
    p.fence()
    if upto == "gates":
        return
    LNCLIP = float(np.log(1e-10))
    vsf = c.sb["VSTORE"][:].bitcast(F32)
    A = vsf[:, 0:2048]
    Bv = vsf[:, 2048:4096]
    C = vsf[:, 4096:6144]
    Dv = vsf[:, 6144:8192]
    for ch in range(NCHUNK):
        Ka = (K, "sc", "A")
        Kb = (K, "sc", "B")
        Kc = (K, "sc", "C")
        Kd = (K, "sc", "D")
        p.dma("sync", A, c.dram["l2a_a"][ch * 128:(ch + 1) * 128, :],
              reads=[(K, "ad", ch, tb) for tb in range(NTB)], writes=[Ka], stream="hin")
        p.dma("sync", Bv, c.dram["l2a_iv"][ch * 128:(ch + 1) * 128, :],
              reads=[(K, "ivd", ch, tb) for tb in range(NTB)], writes=[Kb], stream="hin")
        p.act(C, A, AF.Square, reads=[Ka], writes=[Kc])
        p.dve("tensor_scalar_mul", out=C, in0=C, scalar1=-1.0, reads=[Kc], writes=[Kc])
        p.dve("tensor_scalar_add", out=C, in0=C, scalar1=1.0, reads=[Kc], writes=[Kc])
        p.dve("tensor_scalar_max", out=C, in0=C, scalar1=1e-8, reads=[Kc], writes=[Kc])
        p.act(C, C, AF.Sqrt, reads=[Kc], writes=[Kc])
        p.dve("tensor_mul", out=Bv, in0=Bv, in1=C, reads=[Kb, Kc], writes=[Kb])
        p.act(C, A, AF.Ln, reads=[Ka], writes=[Kc])
        for blk in range(32):
            p.dve("tensor_tensor_scan", out=A[:, blk * 64:(blk + 1) * 64],
                  data0=C[:, blk * 64:(blk + 1) * 64], data1=C[:, blk * 64:(blk + 1) * 64],
                  initial=0.0, op0=ALU.add, op1=ALU.bypass,
                  reads=[Kc], writes=[Ka], tag="cum1")
        p.act(C, A, AF.Exp, reads=[Ka], writes=[Kc])
        p.dve("tensor_scalar_max", out=Dv, in0=A, scalar1=LNCLIP, reads=[Ka], writes=[Kd])
        p.act(Dv, Dv, AF.Exp, scale=-1.0, reads=[Kd], writes=[Kd])
        p.dve("tensor_mul", out=Bv, in0=Bv, in1=Dv, reads=[Kb, Kd], writes=[Kb])
        for blk in range(32):
            p.dve("tensor_tensor_scan", out=Dv[:, blk * 64:(blk + 1) * 64],
                  data0=Bv[:, blk * 64:(blk + 1) * 64], data1=Bv[:, blk * 64:(blk + 1) * 64],
                  initial=0.0, op0=ALU.add, op1=ALU.bypass,
                  reads=[Kb], writes=[Kd], tag="cum2")
        p.dve("tensor_mul", out=Bv, in0=C, in1=Dv, reads=[Kc, Kd], writes=[Kb])
        for blk in range(1, 32):
            p.dve("scalar_tensor_tensor", out=Bv[:, blk * 64:(blk + 1) * 64],
                  in0=C[:, blk * 64:(blk + 1) * 64],
                  scalar=Bv[:, blk * 64 - 1:blk * 64],
                  in1=Bv[:, blk * 64:(blk + 1) * 64], op0=ALU.mult, op1=ALU.add,
                  reads=[Kc], sreads=[Kb], writes=[Kb], tag="interchunk")
        p.dve("tensor_copy", out=xchunk(c, "XLN", ch), in_=Bv,
              reads=[Kb], writes=[("XLN", ch)], tag="recout")
    p.fence()
    if upto == "scan":
        return
    for dch in range(NCHUNK):
        p.dma("pool", wb[:, dch * 1024:(dch + 1) * 1024],
              c.dram["l2a_wopT"][dch * 128:(dch + 1) * 128, :],
              writes=[("WBUF", dch)], stream="wl", tag="l2o_w")
    for tb in range(NTB):
        t0 = tb * TB
        for dch in range(NCHUNK):
            ps = c.pb[dch % 2]
            kps = ("PB", dch % 2)
            for ach in range(NCHUNK):
                p.pe_mm(ps[:], wb[:, ach * 1024 + dch * 128: ach * 1024 + dch * 128 + 128],
                        xchunk(c, "XLN", ach, t0, TB),
                        start=(ach == 0), stop=(ach == NCHUNK - 1),
                        reads=[("WBUF", ach), ("XLN", ach)], writes=[kps])
            xlt = hb[:, 0:TB]
            p.dma("sync", xlt, c.dram["l2a_xln"][dch * 128:(dch + 1) * 128, t0:t0 + TB],
                  reads=[(K, "xlnd", dch)], writes=[(K, "xlt")], stream="hin")
            p.dve("tensor_add", out=xchunk(c, "XRES", dch, t0, TB),
                  in0=xchunk(c, "XRES", dch, t0, TB), in1=ps[:],
                  reads=[kps, ("XRES", dch)], writes=[("XRES", dch)], tag="l2o_res")
            x1t = hb[:, TB:2 * TB]
            p.dve("tensor_add", out=x1t, in0=xlt, in1=ps[:],
                  reads=[kps, (K, "xlt")], writes=[(K, "x1t")])
            p.dma("sync", c.dram["l2a_x1"][dch * 128:(dch + 1) * 128, t0:t0 + TB], x1t,
                  reads=[(K, "x1t")], writes=[(K, "x1d", dch, tb)], stream="qout")
    p.fence()
    if upto == "oproj":
        return
    emit_rms_inplace(c, "l2a_rms2", "l2a_ffnw_col", eps=1e-6, src_dram=c.dram["l2a_x1"])
    p.fence()
    if upto == "rms2":
        return
    emit_swiglu_experts(c, "l2a_ffn", n_routed=0, NI=32, INTER=4096,
                        wg_r=None, wu_r=None, wd_r=None,
                        wg_s="l2a_wg", wu_s="l2a_wu", wd_s="l2a_wd",
                        h_dram=c.dram["l2a_h"])
    p.fence()






def host_prep_l2_ffn(inp):
    d = {}
    d["l2f_gwT"] = np.ascontiguousarray(inp["L2_ffn_router_w"])
    d["l2f_rb"] = np.ascontiguousarray(inp["L2_ffn_router_b"].reshape(1, 8))
    d["l2f_w1"] = inp["L2_ffn_w1"]
    d["l2f_w2"] = inp["L2_ffn_w2"]
    d["l2f_lin2"] = inp["L2_ffn_lin2_w"]
    d["l2f_l2b_col"] = np.ascontiguousarray(inp["L2_ffn_lin2_b"].reshape(8, 128).T)
    return d


def declare_l2_ffn(c: Ctx):
    c.dram_in("l2f_gwT", (D, 8), F32)
    c.dram_in("l2f_rb", (1, 8), F32)
    c.dram_in("l2f_w1", (8, D, 256), F32)
    c.dram_in("l2f_w2", (8, 256, 4096), F32)
    c.dram_in("l2f_lin2", (4096, D), F32)
    c.dram_in("l2f_l2b_col", (128, 8), F32)
    c.dram_tmp("l2f_h", (4096, S), BF16)


def emit_l2_ffn(c: Ctx):
    p = c.p
    wb = c.sb["WBUF"]
    hb = c.sb["HBUF"]
    K = ("l2f",)
    emit_ln(c, tag="l2f_ln")
    emit_moe_gate_top2(c, 8, "l2f_gwT", None, None, "l2f_gate",
                       renorm_eps=None, use_sigmoid=False, inner_ln=False,
                       bias_row="l2f_rb")
    grb = c.sb["GRB"]
    wrow = c.sb["WROW"]
    tstore = c.sb["VSTORE"]
    for tb in range(NTB):
        t0 = tb * TB
        for e in range(8):
            p.dma("sync", wrow[0:1, 0:TB], grb[e:e + 1, t0:t0 + TB],
                  reads=[("GRBDONE",)], writes=[("WROW",)], stream="wrow")
            p.pe_mm(c.pb[6][:, 0:TB], c.sb["ONESBR"][0:1, 0:128], wrow[0:1, 0:TB],
                    start=True, stop=True, reads=[("WROW",), ("ONES",)], writes=[("PB", 6)])
            p.dve("tensor_copy", out=hb[:, e * TB:(e + 1) * TB], in_=c.pb[6][:, 0:TB],
                  reads=[("PB", 6)], writes=[(K, "wbc", e)], tag="wbc")
        for e in range(8):
            sl = e % 2
            p.dma("pool", wb[:, sl * 2048:(sl + 1) * 2048].rearrange("p (c n) -> p c n", c=NCHUNK),
                  c.dram["l2f_w1"][e].rearrange("(c p) n -> p c n", p=128),
                  writes=[(K, "w1", sl)], stream="wl", tag=f"w1_{e}")
            w1v = wb[:, sl * 2048:(sl + 1) * 2048].rearrange("p (c n) -> p c n", c=NCHUNK)
            for rch in range(2):
                ps = c.pb[rch]
                kps = ("PB", rch)
                for dch in range(NCHUNK):
                    p.pe_mm(ps[:], w1v[:, dch, rch * 128:(rch + 1) * 128],
                            xchunk(c, "XLN", dch, t0, TB),
                            start=(dch == 0), stop=(dch == NCHUNK - 1),
                            reads=[(K, "w1", sl), ("XLN", dch)], writes=[kps])
                p.dve("tensor_mul", out=tstore[:, (e * 2 + rch) * TB:(e * 2 + rch + 1) * TB],
                      in0=ps[:], in1=hb[:, e * TB:(e + 1) * TB],
                      reads=[kps, (K, "wbc", e)], writes=[(K, "te", e, rch)], tag="teevac")
        for fch in range(32):
            sl = fch % 2
            p.dma("pool", wb[:, 4096 + sl * 2048: 4096 + sl * 2048 + 2048].rearrange(
                      "p (e rc n) -> p e rc n", e=8, rc=2),
                  c.dram["l2f_w2"][:, :, fch * 128:(fch + 1) * 128].rearrange(
                      "e (rc p) n -> p e rc n", p=128),
                  writes=[(K, "w2", sl)], stream="wl", tag=f"w2_{fch}")
            ps = c.pb[2 + sl]
            kps = ("PB", 2 + sl)
            nmm = 0
            for e in range(8):
                for rch in range(2):
                    p.pe_mm(ps[:], wb[:, 4096 + sl * 2048 + (e * 2 + rch) * 128: 4096 + sl * 2048 + (e * 2 + rch) * 128 + 128],
                            tstore[:, (e * 2 + rch) * TB:(e * 2 + rch + 1) * TB],
                            start=(nmm == 0), stop=(nmm == 15),
                            reads=[(K, "w2", sl), (K, "te", e, rch)], writes=[kps])
                    nmm += 1
            gl = hb[:, 4096 - TB:4096]
            p.act(gl, ps[:], AF.Gelu, reads=[kps], writes=[(K, "gl", sl), (K, "wbc", 7)])
            p.dma("sync", c.dram["l2f_h"][fch * 128:(fch + 1) * 128, t0:t0 + TB], gl,
                  reads=[(K, "gl", sl)], writes=[(K, "hd", fch, tb)], stream="hout")
    p.fence()
    bcol = c.sb["BCOL"]
    p.dma("sync", bcol[:, 8:16], c.dram["l2f_l2b_col"], writes=[(K, "l2b")], stream="bl")
    for tb in range(NTB):
        t0 = tb * TB
        for i in range(32):
            sl = i % 2
            p.dma("pool", wb[:, 4096 + sl * 1024:4096 + sl * 1024 + 1024],
                  c.dram["l2f_lin2"][i * 128:(i + 1) * 128, :],
                  writes=[(K, "wd", sl)], stream="wl")
            p.dma("sync", hb[:, sl * TB:(sl + 1) * TB],
                  c.dram["l2f_h"][i * 128:(i + 1) * 128, t0:t0 + TB],
                  reads=[(K, "hd", i, tb)], writes=[(K, "ht", sl)], stream="hin")
            for dch in range(NCHUNK):
                p.pe_mm(c.pb[dch][:], wb[:, 4096 + sl * 1024 + dch * 128:4096 + sl * 1024 + dch * 128 + 128],
                        hb[:, sl * TB:(sl + 1) * TB],
                        start=(i == 0), stop=(i == 31),
                        reads=[(K, "wd", sl), (K, "ht", sl)], writes=[("PB", dch)])
        for dch in range(NCHUNK):
            p.dve("scalar_tensor_tensor", out=xchunk(c, "XRES", dch, t0, TB),
                  in0=c.pb[dch][:], scalar=bcol[:, 8 + dch:9 + dch],
                  in1=xchunk(c, "XRES", dch, t0, TB), op0=ALU.add, op1=ALU.add,
                  reads=[("PB", dch), (K, "l2b"), ("XRES", dch)], writes=[("XRES", dch)],
                  tag="l2f_res")
    p.fence()



def host_prep_l3_attn(inp):
    d = {}
    d["l3a_wqT"] = np.ascontiguousarray(inp["L3_attn_Wq"])
    d["l3a_wqT"] = np.ascontiguousarray(inp["L3_attn_Wq"].T)
    d["l3a_wkT"] = np.ascontiguousarray(inp["L3_attn_Wk"].T)
    d["l3a_wvT"] = np.ascontiguousarray(inp["L3_attn_Wv"].T)
    d["l3a_bq_col"] = np.ascontiguousarray(inp["L3_attn_bq"].reshape(8, 128).T)
    d["l3a_bk_col"] = np.ascontiguousarray(inp["L3_attn_bk"].reshape(8, 128).T)
    d["l3a_bv_row"] = np.ascontiguousarray(inp["L3_attn_bv"].reshape(1, D))
    d["c_nodiag"] = (1.0 - np.eye(128)).astype(np.float32)
    return d


def declare_l3_attn(c: Ctx):
    for nm, shp in [("l3a_wqT", (D, D)), ("l3a_wkT", (D, D)), ("l3a_wvT", (D, D)),
                    ("l3a_bq_col", (128, 8)), ("l3a_bk_col", (128, 8)),
                    ("l3a_bv_row", (1, D)), ("c_nodiag", (128, 128))]:
        c.dram_in(nm, shp, F32)
    c.dram_tmp("l3a_qT", (D, S), BF16)
    c.dram_tmp("l3a_kT", (D, S), BF16)


def _proj_plain(c, wname, bcol_name, dst_dram, tag):
    p = c.p
    wb = c.sb["WBUF"]
    hb = c.sb["HBUF"]
    K = (tag,)
    for dch in range(NCHUNK):
        p.dma("pool", wb[:, dch * 1024:(dch + 1) * 1024],
              c.dram[wname][dch * 128:(dch + 1) * 128, :],
              writes=[("WBUF", dch)], stream="wl", tag=f"{tag}_w")
    bcol = c.sb["BCOL"]
    p.dma("sync", bcol[:, 0:8], c.dram[bcol_name], writes=[("BCOL",)], stream="bl")
    for och in range(NCHUNK):
        for tb in range(NTB):
            t0 = tb * TB
            ps = c.pb[tb % 2]
            kps = ("PB", tb % 2)
            for dch in range(NCHUNK):
                p.pe_mm(ps[:], wb[:, dch * 1024 + och * 128: dch * 1024 + och * 128 + 128],
                        xchunk(c, "XLN", dch, t0, TB),
                        start=(dch == 0), stop=(dch == NCHUNK - 1),
                        reads=[("WBUF", dch), ("XLN", dch)], writes=[kps])
            qb_ = hb[:, 0:TB]
            p.dve("tensor_scalar", out=qb_, in0=ps[:], scalar1=bcol[:, och:och + 1],
                  scalar2=None, op0=ALU.add,
                  reads=[kps, ("BCOL",)], writes=[(K, "qb")])
            p.dma("sync", dst_dram[och * 128:(och + 1) * 128, t0:t0 + TB], qb_,
                  reads=[(K, "qb")], writes=[(K, "od", och, tb)], stream="qout")
    return [(K, "od", och, tb) for och in range(NCHUNK) for tb in range(NTB)]


def emit_l3_attn(c: Ctx):
    p = c.p
    wb = c.sb["WBUF"]
    hb = c.sb["HBUF"]
    fb = c.sb["FBUF"]
    srow = c.sb["SROW"]
    K = ("l3a",)
    emit_ln(c, tag="l3a_ln")
    nodiag = c.sb["NODIAG"]
    p.dma("pool", nodiag[:], c.dram["c_nodiag"], writes=[(K, "nd")], stream="cl")
    qdone = _proj_plain(c, "l3a_wqT", "l3a_bq_col", c.dram["l3a_qT"], "l3q")
    kdone = _proj_plain(c, "l3a_wkT", "l3a_bk_col", c.dram["l3a_kT"], "l3k")
    for dch in range(NCHUNK):
        p.dma("pool", wb[:, dch * 1024:(dch + 1) * 1024],
              c.dram["l3a_wvT"][dch * 128:(dch + 1) * 128, :],
              writes=[("WBUF", dch)], stream="wl", tag="l3v_w")
    bvrow = c.sb["BROW"]
    p.dma("pool", bvrow[0:1, 0:D], c.dram["l3a_bv_row"], writes=[("BROW",)], stream="bl")
    vstore = c.sb["VSTORE"]
    VW = 1025
    p.dve("memset", vstore[:], 0.0, writes=[("VST", kt2) for kt2 in range(16)], tag="vclr")
    p.dve("memset", vstore[:, 1024::1025], 1.0,
          reads=[("VST", kt2) for kt2 in range(16)], writes=[("VONE",)])
    for kt in range(S // 128):
        tt0 = kt * 128
        for half in range(2):
            ps = c.pb[2 + half]
            kps = ("PB", 2 + half)
            for dch in range(NCHUNK):
                p.pe_mm(ps[:], xchunk(c, "XLN", dch, tt0, 128),
                        wb[:, dch * 1024 + half * 512: dch * 1024 + half * 512 + 512],
                        start=(dch == 0), stop=False,
                        reads=[("WBUF", dch), ("XLN", dch)], writes=[kps])
            p.pe_mm(ps[:], c.sb["ONESBR"][0:1, 0:128],
                    bvrow[0:1, half * 512:half * 512 + 512],
                    start=False, stop=True,
                    reads=[("BROW",), ("ONES",)], writes=[kps])
            p.dve("tensor_copy", out=vstore[:, kt * VW + half * 512: kt * VW + half * 512 + 512],
                  in_=ps[:], reads=[kps, ("VONE",)], writes=[("VST", kt)], tag="vevac")
    p.fence()
    for qb in range(NTB):
        t0 = qb * TB
        qtl = c.sb["KTILE"]
        for chq in range(NCHUNK):
            p.dma("sync", hb[:, chq * TB:(chq + 1) * TB],
                  c.dram["l3a_qT"][chq * 128:(chq + 1) * 128, t0:t0 + TB],
                  reads=qdone, writes=[(K, "q", chq)], stream="kld")
        lps = c.pb[6]
        for kt in range(S // 128):
            ps = c.pb[4 + (kt % 2)]
            kps = ("PB", 4 + (kt % 2))
            ktile = c.sb["KTILE"]
            kof = (kt % 2) * 1024
            for chk in range(NCHUNK):
                p.dma("sync", ktile[:, kof + chk * 128:kof + (chk + 1) * 128],
                      c.dram["l3a_kT"][chk * 128:(chk + 1) * 128, kt * 128:(kt + 1) * 128],
                      reads=kdone, writes=[(K, "k", kt % 2, chk)], stream="kld")
            for chk in range(NCHUNK):
                p.pe_mm(ps[:], ktile[:, kof + chk * 128:kof + (chk + 1) * 128],
                        hb[:, chk * TB:(chk + 1) * TB],
                        start=(chk == 0), stop=(chk == NCHUNK - 1),
                        reads=[(K, "k", kt % 2, chk), (K, "q", chk)],
                        writes=[kps], tag="l3qk")
            pt = wb[:, kt * TB:(kt + 1) * TB]
            kpt = (K, "P", kt)
            p.act(pt, ps[:], AF.Exp, scale=1.0 / 1024.0,
                  reads=[kps], writes=[kpt], tag="l3exp")
            qoff = kt * 128 - t0
            if 0 <= qoff < TB:
                p.dve("tensor_mul", out=wb[:, kt * TB + qoff: kt * TB + qoff + 128],
                      in0=wb[:, kt * TB + qoff: kt * TB + qoff + 128], in1=nodiag[:],
                      reads=[kpt, (K, "nd")], writes=[kpt], tag="l3mask")
            p.pe_mm(lps[0:1, 0:TB], c.sb["ONESB"][:, 0:1], pt,
                    start=(kt == 0), stop=(kt == 15),
                    reads=[kpt], writes=[("PB", 6)], tag="l3l")
        lrow = srow[0:1, 5 * TB:6 * TB]
        p.act(lrow, lps[0:1, 0:TB], AF.Ln, reads=[("PB", 6)], writes=[(K, "lr")])
        p.act(lrow, lrow, AF.Exp, scale=-1.0, reads=[(K, "lr")], writes=[(K, "lr2")])
        psn = c.pb[5]
        p.pe_mm(psn[:, 0:TB], c.sb["ONESR"][:], lrow,
                start=True, stop=True, reads=[(K, "lr2"), ("ONES",)], writes=[("PB", 5)])
        nbv = fb[:, 0:TB]
        p.dve("tensor_copy", out=nbv, in_=psn[:, 0:TB],
              reads=[("PB", 5)], writes=[(K, "nb")])
        for half in range(2):
            for dh in range(4):
                dch = half * 4 + dh
                pso = c.pb[dh]
                kpso = ("PB", dh)
                for kt in range(S // 128):
                    p.pe_mm(pso[:], vstore[:, kt * VW + dch * 128: kt * VW + dch * 128 + 128],
                            wb[:, kt * TB:(kt + 1) * TB],
                            start=(kt == 0), stop=(kt == 15),
                            reads=[("VST", kt), (K, "P", kt)], writes=[kpso], tag="l3pv")
            for dh in range(4):
                dch = half * 4 + dh
                tmp = fb[:, TB:2 * TB]
                p.dve("tensor_mul", out=tmp, in0=c.pb[dh][:], in1=nbv,
                      reads=[("PB", dh), (K, "nb")], writes=[(K, "tmp")])
                p.dve("tensor_add", out=xchunk(c, "XRES", dch, t0, TB),
                      in0=xchunk(c, "XRES", dch, t0, TB), in1=tmp,
                      reads=[(K, "tmp"), ("XRES", dch)], writes=[("XRES", dch)],
                      tag="l3a_res")
    p.fence()



def host_prep_l3_ffn(inp):
    d = {}
    d["l3f_gwT"] = np.ascontiguousarray(inp["L3_ffn_gate_w"].T)
    d["l3f_gb"] = np.ascontiguousarray(inp["L3_ffn_gate_b"].reshape(1, 8))
    for nm in ("w1", "w2", "w3"):
        d[f"l3f_{nm}"] = inp[f"L3_ffn_{nm}"]
    for nm in ("b1", "b2", "b3"):
        d[f"l3f_{nm}"] = np.ascontiguousarray(
            inp[f"L3_ffn_{nm}"].reshape(8, 8, 128).transpose(2, 0, 1).reshape(128, 64))
    return d


def declare_l3_ffn(c: Ctx):
    c.dram_in("l3f_gwT", (D, 8), F32)
    c.dram_in("l3f_gb", (1, 8), F32)
    for nm in ("w1", "w2", "w3"):
        c.dram_in(f"l3f_{nm}", (8, D, D), F32)
    for nm in ("b1", "b2", "b3"):
        c.dram_in(f"l3f_{nm}", (128, 64), F32)
    c.dram_tmp("l3f_h1", (D, S), BF16)
    c.dram_tmp("l3f_h2", (D, S), BF16)


def emit_l3_ffn(c: Ctx):
    p = c.p
    wb = c.sb["WBUF"]
    hb = c.sb["HBUF"]
    K = ("l3f",)
    emit_ln(c, tag="l3f_ln")
    emit_moe_gate_top2(c, 8, "l3f_gwT", None, None, "l3f_gate",
                       renorm_eps=None, use_sigmoid=False, inner_ln=False,
                       probs_first2=True, bias_row="l3f_gb")
    grb = c.sb["GRB"]
    wrow = c.sb["WROW"]
    bcol = c.sb["BCOL"]
    for nm in ("b1", "b2", "b3"):
        pass
    bc1 = c.sb["B3COL"]
    for i, nm in enumerate(("l3f_b1", "l3f_b2", "l3f_b3")):
        p.dma("sync", bc1[:, i * 64:(i + 1) * 64], c.dram[nm],
              writes=[(K, "bc", i)], stream="bl")
    for e in range(8):
        p.dma("sync", wrow[0:1, :], grb[e:e + 1, :],
              reads=[("GRBDONE",)], writes=[("WROW",)], stream="wrow")
        for tb in range(NTB):
            p.pe_mm(c.pb[6][:, 0:TB], c.sb["ONESBR"][0:1, 0:128],
                    wrow[0:1, tb * TB:(tb + 1) * TB],
                    start=True, stop=True,
                    reads=[("WROW",), ("ONES",)], writes=[("PB", 6)])
            p.dve("tensor_copy", out=c.sb["VSTORE"][:, tb * TB:(tb + 1) * TB],
                  in_=c.pb[6][:, 0:TB],
                  reads=[("PB", 6)], writes=[("WBC", tb)], tag="wbc")
        for stage in range(3):
            wnm = c.dram[f"l3f_w{stage + 1}"][e]
            src_name = "XLN" if stage == 0 else None
            src_dram = None if stage == 0 else c.dram[f"l3f_h{stage}"]
            dst_dram = c.dram[f"l3f_h{stage + 1}"] if stage < 2 else None
            for dch in range(NCHUNK):
                p.dma("pool", wb[:, dch * 1024:(dch + 1) * 1024],
                      wnm[dch * 128:(dch + 1) * 128, :],
                      writes=[("WBUF", dch)], stream="wl", tag=f"l3f_w{stage}")
            for tb in range(NTB):
                t0 = tb * TB
                if stage > 0:
                    for dch in range(NCHUNK):
                        p.dma("sync", hb[:, dch * TB:(dch + 1) * TB],
                              src_dram[dch * 128:(dch + 1) * 128, t0:t0 + TB],
                              reads=[(K, f"h{stage}", dch, tb)],
                              writes=[(K, "ht", dch)], stream="hin")
                for och in range(NCHUNK):
                    ps = c.pb[och % 2]
                    kps = ("PB", och % 2)
                    for dch in range(NCHUNK):
                        if stage == 0:
                            rhv = xchunk(c, "XLN", dch, t0, TB)
                            rrd = [("XLN", dch)]
                        else:
                            rhv = hb[:, dch * TB:(dch + 1) * TB]
                            rrd = [(K, "ht", dch)]
                        p.pe_mm(ps[:], wb[:, dch * 1024 + och * 128: dch * 1024 + och * 128 + 128],
                                rhv,
                                start=(dch == 0), stop=(dch == NCHUNK - 1),
                                reads=[("WBUF", dch)] + rrd, writes=[kps])
                    if stage < 2:
                        ov = c.sb["FBUF"][:, TB:2 * TB]
                        p.act(ov, ps[:], AF.Relu, bias=bc1[:, stage * 64 + e * 8 + och:stage * 64 + e * 8 + och + 1],
                              reads=[kps, (K, "bc", stage)], writes=[(K, "ov")])
                        p.dma("pool", dst_dram[och * 128:(och + 1) * 128, t0:t0 + TB], ov,
                              reads=[(K, "ov")],
                              writes=[(K, f"h{stage + 1}", och, tb)], stream="hout")
                    else:
                        tmp = c.sb["FBUF"][:, 0:TB]
                        p.dve("scalar_tensor_tensor", out=tmp, in0=ps[:],
                              scalar=bc1[:, 128 + e * 8 + och:129 + e * 8 + och],
                              in1=c.sb["VSTORE"][:, tb * TB:(tb + 1) * TB],
                              op0=ALU.add, op1=ALU.mult,
                              reads=[kps, (K, "bc", 2), ("WBC", tb)], writes=[(K, "tmp")])
                        p.dve("tensor_add", out=xchunk(c, "XRES", och, t0, TB),
                              in0=xchunk(c, "XRES", och, t0, TB), in1=tmp,
                              reads=[(K, "tmp"), ("XRES", och)], writes=[("XRES", och)],
                              tag="l3f_res")
    p.fence()



_LAYERS = [
    ("l0a", declare_l0_attn, emit_l0_attn, host_prep_l0_attn),
    ("l0f", declare_l0_ffn, emit_l0_ffn, host_prep_l0_ffn),
    ("l1a", declare_l1_attn, emit_l1_attn, None),
    ("l1f", declare_l1_ffn, emit_l1_ffn, host_prep_l1_ffn),
    ("l2a", declare_l2_attn, emit_l2_attn, host_prep_l2_attn),
    ("l2f", declare_l2_ffn, emit_l2_ffn, host_prep_l2_ffn),
    ("l3a", declare_l3_attn, emit_l3_attn, host_prep_l3_attn),
    ("l3f", declare_l3_ffn, emit_l3_ffn, host_prep_l3_ffn),
]


def _build_layer_launch(name, decl, emit, with_xln_out=False):
    nc = bass.Bass("TRN2", target_bir_lowering=False, debug=False)
    es = ExitStack()
    p = Prog(nc)
    c = Ctx(nc, es, p)
    x_in = c.dram_in("x_in", [D, S], F32)
    x_out = c.dram_out("x_out", [D, S], F32)
    xln_out = c.dram_out("xln_out", [D, S], BF16) if with_xln_out else None
    declare_common(c)
    decl(c)
    alloc_pools(c)
    emit_init(c)
    emit_load_x(c, x_in)
    emit(c)
    if with_xln_out:
        emit_ln(c, tag="post_ln")
        for ch in range(NCHUNK):
            p.dma("sync", xln_out[ch * 128:(ch + 1) * 128, :], xchunk(c, "XLN", ch),
                  reads=[("XLN", ch)], writes=[("XLNO", ch)], stream="xout")
    emit_store_x(c, x_out)
    if with_xln_out:
        p.op("pool", lambda e: e.nop(), reads=[("XLNO", ch) for ch in range(NCHUNK)])
    p.finalize_and_emit(es)
    es.close()
    return nc


def _build_groupA():
    nc = bass.Bass("TRN2", target_bir_lowering=False, debug=False)
    es = ExitStack()
    p = Prog(nc)
    c = Ctx(nc, es, p)
    x_in = c.dram_in("x_in", [D, S], F32)
    x_out = c.dram_out("x_out", [D, S], F32)
    xln_out = c.dram_out("xln_out", [D, S], BF16)
    declare_common(c)
    declare_l0_attn(c)
    declare_l0_ffn(c)
    alloc_pools(c)
    emit_init(c)
    emit_load_x(c, x_in)
    emit_l0_attn(c)
    emit_l0_ffn(c)
    emit_ln(c, tag="post_ln")
    for ch in range(NCHUNK):
        p.dma("sync", xln_out[ch * 128:(ch + 1) * 128, :], xchunk(c, "XLN", ch),
              reads=[("XLN", ch)], writes=[("XLNO", ch)], stream="xout")
    emit_store_x(c, x_out)
    p.op("pool", lambda e: e.nop(), reads=[("XLNO", ch) for ch in range(NCHUNK)])
    p.finalize_and_emit(es)
    es.close()
    return nc


def _build_groupB():
    nc = bass.Bass("TRN2", target_bir_lowering=False, debug=False)
    es = ExitStack()
    p = Prog(nc)
    c = Ctx(nc, es, p)
    x_in = c.dram_in("x_in", [D, S], F32)
    x_out = c.dram_out("x_out", [D, S], F32)
    declare_common(c)
    declare_l1_attn(c)
    declare_l1_ffn(c)
    declare_l2_attn(c)
    declare_l2_ffn(c)
    declare_l3_attn(c)
    declare_l3_ffn(c)
    alloc_pools(c)
    emit_init(c)
    emit_load_x(c, x_in)
    emit_l1_attn(c)
    emit_l1_ffn(c)
    emit_l2_attn(c)
    emit_l2_ffn(c)
    emit_l3_attn(c)
    emit_l3_ffn(c)
    emit_store_x(c, x_out)
    p.finalize_and_emit(es)
    es.close()
    return nc


def kernel(**inputs):
    import ml_dtypes
    x = np.asarray(inputs["L0_attn_x"])
    xs = [np.ascontiguousarray(x[b].T) for b in range(B)]
    common = host_common()
    prepsA = {}
    prepsA.update(host_prep_l0_attn(inputs))
    prepsA.update(host_prep_l0_ffn(inputs))
    ncA = _build_groupA()
    in_maps = []
    for b in range(B):
        m = {"x_in": xs[b]}
        m.update(common)
        m.update(prepsA)
        in_maps.append(m)
    resA = run_bass_kernel_spmd(ncA, in_maps, core_ids=list(range(B)))
    xs = [np.asarray(resA.results[b]["x_out"]) for b in range(B)]
    xg = np.stack([np.asarray(resA.results[b]["xln_out"]) for b in range(B)])
    xln_tok = np.ascontiguousarray(xg.transpose(0, 2, 1))
    if xln_tok.dtype == np.uint16:
        xln_tok = xln_tok.view(ml_dtypes.bfloat16)

    prepsB = {}
    prepsB.update(host_prep_l1_ffn(inputs))
    prepsB.update(host_prep_l2_attn(inputs))
    prepsB.update(host_prep_l2_ffn(inputs))
    prepsB.update(host_prep_l3_attn(inputs))
    prepsB.update(host_prep_l3_ffn(inputs))
    ncB = _build_groupB()
    in_maps = []
    for b in range(B):
        m = {"x_in": xs[b]}
        m.update(common)
        m.update(prepsB)
        order = [(b + i) % B for i in range(B)]
        m["l1a_xg"] = np.ascontiguousarray(xln_tok[order])
        in_maps.append(m)
    resB = run_bass_kernel_spmd(ncB, in_maps, core_ids=list(range(B)))
    xs = [np.asarray(resB.results[b]["x_out"]) for b in range(B)]
    out = np.stack([xs[b].T for b in range(B)]).astype(np.float32)
    return out
